# version 04
# baseline (speedup 1.0000x reference)
import os
import contextlib
import numpy as np
import ml_dtypes
import concourse.bass as bass
import concourse.mybir as mybir
from concourse.bass_utils import run_bass_kernel_spmd

NB = 2

F32 = mybir.dt.float32
BF16 = mybir.dt.bfloat16
AF = mybir.ActivationFunctionType
ALU = mybir.AluOpType
AX = mybir.AxisListType

P = 128
D = 1024
KC = D // P
SEQ = 4096
TCH = 512
NCHUNK = SEQ // TCH
LN_EPS = 1e-5


class Sem:

    def __init__(self, nc, es, name):
        self.h = es.enter_context(nc.semaphore(name))
        self.n = 0

    def inc(self, instr, k=1):
        instr.then_inc(self.h, k)
        self.n += k
        return self.n

    def wait(self, eng, target=None):
        t = self.n if target is None else target
        if t > 0:
            eng.wait_ge(self.h, t)


class Ctx:

    def __init__(self, nc, es):
        self.nc = nc
        self.es = es
        self.pe = nc.tensor
        self.dve = nc.vector
        self.act = nc.scalar
        self.sp = nc.sync
        self.pool = nc.gpsimd
        self.s_pe = Sem(nc, es, "s_pe")
        self.s_v = Sem(nc, es, "s_v")
        self.s_a = Sem(nc, es, "s_a")
        self.s_p = Sem(nc, es, "s_p")
        self.s_wload = Sem(nc, es, "s_wload")
        self.s_io = Sem(nc, es, "s_io")
        self.s_store = Sem(nc, es, "s_store")

    def sbuf(self, name, shape, dt):
        return self.es.enter_context(self.nc.sbuf_tensor(name, shape, dt))

    def psum(self, name, shape, dt=F32):
        return self.es.enter_context(self.nc.psum_tensor(name, shape, dt))

    def sem(self, name):
        return Sem(self.nc, self.es, name)


CONST_VALS = [1e-5, 1e-6, 0.001, 0.998, -1.0, 0.5, 2.0, 1e-9]


def make_consts():
    pool = np.tile(np.array(CONST_VALS, np.float32)[None, :], (P, 1))
    return {
        "c_eye": np.eye(P, dtype=ml_dtypes.bfloat16),
        "c_eyef": np.eye(P, dtype=np.float32),
        "c_ones_row": np.ones((1, P), dtype=ml_dtypes.bfloat16),
        "c_ones16n": np.full((16, 1), -1.0, dtype=np.float32),
        "c_pool": pool,
    }


def declare_consts(nc):
    return {
        "eye": nc.dram_tensor("c_eye", [P, P], BF16, kind="ExternalInput").ap(),
        "eyef": nc.dram_tensor("c_eyef", [P, P], F32, kind="ExternalInput").ap(),
        "ones_row": nc.dram_tensor("c_ones_row", [1, P], BF16, kind="ExternalInput").ap(),
        "ones16n": nc.dram_tensor("c_ones16n", [16, 1], F32, kind="ExternalInput").ap(),
        "pool": nc.dram_tensor("c_pool", [P, len(CONST_VALS)], F32, kind="ExternalInput").ap(),
    }


def load_consts(ctx, aps):
    t = {}
    t["eye"] = ctx.sbuf("t_eye", [P, P], BF16)
    t["eyef"] = ctx.sbuf("t_eyef", [P, P], F32)
    t["ones_row"] = ctx.sbuf("t_ones_row", [1, P], BF16)
    t["ones16n"] = ctx.sbuf("t_ones16n", [16, 1], F32)
    t["onesK"] = ctx.sbuf("t_onesK", [P, 1], BF16)
    t["onesKf"] = ctx.sbuf("t_onesKf", [P, 1], F32)
    ctx.s_io.inc(ctx.sp.dma_start(out=t["eye"][:, :], in_=aps["eye"][:, :]), 16)
    ctx.s_io.inc(ctx.sp.dma_start(out=t["eyef"][:, :], in_=aps["eyef"][:, :]), 16)
    ctx.s_io.inc(ctx.sp.dma_start(out=t["ones_row"][:, :], in_=aps["ones_row"][:, :]), 16)
    ctx.s_io.inc(ctx.sp.dma_start(out=t["ones16n"][:, :], in_=aps["ones16n"][:, :]), 16)
    t["pool"] = ctx.sbuf("t_pool", [P, len(CONST_VALS)], F32)
    ctx.s_io.inc(ctx.sp.dma_start(out=t["pool"][:, :], in_=aps["pool"][:, :]), 16)
    for i, v in enumerate(CONST_VALS):
        ctx.nc.const_aps.register(F32, v, t["pool"][:, i:i + 1])
    ctx.s_io.wait(ctx.act)
    ctx.s_io.wait(ctx.dve)
    ctx.s_io.wait(ctx.pe)
    ctx.s_io.wait(ctx.pool)
    ctx.s_v.inc(ctx.dve.memset(t["onesK"][:, :], 1.0))
    ctx.s_v.inc(ctx.dve.memset(t["onesKf"][:, :], 1.0))
    ctx.s_v.wait(ctx.pe)
    return t


def load_cast_bf16(ctx, dst_slice, src_ap, sem=None):
    s = sem or ctx.s_wload
    s.inc(ctx.pool.dma_start(out=dst_slice, in_=src_ap), 16)
    return s


class TransposeHelper:

    def __init__(self, ctx, consts, pst=None, width=P):
        self.ctx = ctx
        self.eye = consts["eye"]
        self.pst = pst if pst is not None else ctx.psum("pstT", [P, P], BF16)
        self.guard = 0

    def transpose(self, src_slice, dst_slice, rows=P, cols=P):
        ctx = self.ctx
        if self.guard:
            ctx.s_a.wait(ctx.pe, self.guard)
        tp = ctx.pe.transpose(self.pst[0:cols, 0:rows], src_slice,
                              self.eye[0:rows, 0:rows])
        ctx.s_pe.inc(tp)
        ctx.s_pe.wait(ctx.act)
        ev = ctx.act.activation(out=dst_slice, in_=self.pst[0:cols, 0:rows], func=AF.Copy)
        self.guard = ctx.s_a.inc(ev)


def load_weight_T_big(ctx, consts, th, w_ap, name, n_in=KC, n_out=KC, stg=None):
    if stg is None:
        stg = ctx.sbuf(f"stg_{name}", [P, n_in * P], BF16)
    wT = ctx.sbuf(f"wT_{name}", [P, n_in * n_out * P], BF16)
    s = ctx.sem(f"swl_{name}")
    for oc in range(n_out):
        ctx.s_pe.wait(ctx.pool)
        load_cast_bf16(ctx, stg[:, :], w_ap[oc * P:(oc + 1) * P, :], sem=s)
        s.wait(ctx.pe)
        for kc in range(n_in):
            th.transpose(stg[:, kc * P:(kc + 1) * P],
                         wT[:, (kc * n_out + oc) * P:(kc * n_out + oc + 1) * P])
    return wT


def emit_ln_chunk(ctx, consts, resid_ap, c, bufs, eps=LN_EPS, rms=False,
                  load_guard=None, src_sbuf=None, src_ofs=0):
    par = c % 2
    X, XB, SQ, XN = bufs.get("X"), bufs.get("XB"), bufs["SQ"], bufs["XN"]
    nrm = bufs["nrm"]
    nslx = bufs.get("nslx", 2)
    parx = par % nslx
    xofs = (par % bufs.get("nslxn", 2)) * KC * TCH
    xofsx = parx * KC * TCH
    if src_sbuf is None:
        s_x = bufs["s_x"][parx]
        if load_guard is not None:
            for sem, cnt in (load_guard if isinstance(load_guard, list) else [load_guard]):
                sem.wait(ctx.sp, cnt)
        for j in range(KC):
            dma = ctx.sp.dma_start(out=X[:, xofsx + j * TCH: xofsx + (j + 1) * TCH],
                                   in_=resid_ap[j * P:(j + 1) * P, c * TCH:(c + 1) * TCH])
            s_x.inc(dma, 16)
        s_x.wait(ctx.act)
    if bufs["xb_guard"][parx]:
        ctx.s_v.wait(ctx.act, bufs["xb_guard"][parx][0])
        ctx.s_pe.wait(ctx.act, bufs["xb_guard"][parx][1])
    if bufs.get("pre_square_guard"):
        ctx.s_pe.wait(ctx.act, bufs["pre_square_guard"][0])
        if len(bufs["pre_square_guard"]) > 1:
            ctx.s_v.wait(ctx.act, bufs["pre_square_guard"][1])
        bufs["pre_square_guard"] = None
    if src_sbuf is None:
        for j in range(KC):
            sl = slice(xofsx + j * TCH, xofsx + (j + 1) * TCH)
            ctx.s_a.inc(ctx.act.activation(out=XB[:, sl], in_=X[:, sl], func=AF.Copy))
            ctx.s_a.inc(ctx.act.activation(out=SQ[:, sl], in_=X[:, sl], func=AF.Square))
        src, sofs = XB, xofsx
    else:
        for j in range(KC):
            sl = slice(xofsx + j * TCH, xofsx + (j + 1) * TCH)
            ssl = slice(src_ofs + j * TCH, src_ofs + (j + 1) * TCH)
            ctx.s_a.inc(ctx.act.activation(out=SQ[:, sl], in_=src_sbuf[:, ssl],
                                           func=AF.Square))
        src, sofs = src_sbuf, src_ofs
    ctx.s_a.wait(ctx.pe)
    psS = nrm["psS"]
    if nrm["stats_guard"][0]:
        ctx.s_v.wait(ctx.pe, nrm["stats_guard"][0])
    for j in range(KC):
        sl = slice(xofsx + j * TCH, xofsx + (j + 1) * TCH)
        ssl = slice(sofs + j * TCH, sofs + (j + 1) * TCH)
        if not rms:
            ctx.s_pe.inc(ctx.pe.matmul(psS[0:1, 0:TCH], consts["onesK"][:, :],
                                       src[:, ssl], start=(j == 0), stop=(j == KC - 1)))
        ctx.s_pe.inc(ctx.pe.matmul(psS[0:1, TCH:2 * TCH], consts["onesK"][:, :],
                                   SQ[:, sl], start=(j == 0), stop=(j == KC - 1)))
    pe_stats_done = ctx.s_pe.n
    st = nrm["st"]
    mean = st[0:1, 0:TCH]
    var = st[0:1, TCH:2 * TCH]
    rstd = st[0:1, 2 * TCH:3 * TCH]
    men2 = st[0:1, 3 * TCH:4 * TCH]
    mrs = men2
    rowb = nrm["rowb"]
    rstd_b = rowb[0:1, 0:TCH]
    mrs_b = rowb[0:1, TCH:2 * TCH]
    ctx.s_pe.wait(ctx.dve)
    if nrm["st_guard"][0]:
        ctx.s_a.wait(ctx.dve, nrm["st_guard"][0])
    if not rms:
        ctx.s_v.inc(ctx.dve.tensor_scalar(out=mean, in0=psS[0:1, 0:TCH], scalar1=1.0 / D,
                                          scalar2=None, op0=ALU.mult))
        ctx.s_v.wait(ctx.dve)
        ctx.s_v.inc(ctx.dve.tensor_tensor(out=men2, in0=mean, in1=mean, op=ALU.mult))
        ctx.s_v.wait(ctx.dve)
        ctx.s_v.inc(ctx.dve.scalar_tensor_tensor(out=var, in0=psS[0:1, TCH:2 * TCH],
                                                 scalar=1.0 / D, in1=men2,
                                                 op0=ALU.mult, op1=ALU.subtract))
    else:
        ctx.s_v.inc(ctx.dve.tensor_scalar(out=var, in0=psS[0:1, TCH:2 * TCH],
                                          scalar1=1.0 / D, scalar2=None, op0=ALU.mult))
    nrm["stats_guard"][0] = ctx.s_v.n
    ctx.s_v.wait(ctx.act)
    ctx.s_a.inc(ctx.act.activation(out=var, in_=var, func=AF.Ln, bias=eps))
    ctx.s_a.wait(ctx.act)
    ctx.s_a.inc(ctx.act.activation(out=rstd, in_=var, func=AF.Exp, scale=-0.5))
    if not rms:
        ctx.s_a.wait(ctx.dve)
        ctx.s_v.inc(ctx.dve.tensor_tensor(out=mrs, in0=mean, in1=rstd, op=ALU.mult))
        ctx.s_v.wait(ctx.act)
    ctx.s_a.wait(ctx.act)
    ctx.s_a.inc(ctx.act.activation(out=rstd_b, in_=rstd, func=AF.Copy))
    if not rms:
        ctx.s_a.inc(ctx.act.activation(out=mrs_b, in_=mrs, func=AF.Copy))
    nrm["st_guard"][0] = ctx.s_a.n
    psB = nrm["psB"]
    rbs = nrm["rbs"]
    if nrm["rbs_guard"][0]:
        ctx.s_v.wait(ctx.act, nrm["rbs_guard"][0])
    rstd_bb = rbs[:, 0:TCH]
    mrs_bb = rbs[:, TCH:2 * TCH]
    if nrm["bcast_guard"][0]:
        ctx.s_a.wait(ctx.pe, nrm["bcast_guard"][0])
    ctx.s_a.wait(ctx.pe)
    ctx.s_pe.inc(ctx.pe.matmul(psB[:, 0:TCH], consts["ones_row"][:, :], rstd_b,
                               start=True, stop=True))
    ctx.s_pe.wait(ctx.act)
    ctx.s_a.inc(ctx.act.activation(out=rstd_bb, in_=psB[:, 0:TCH], func=AF.Copy))
    if not rms:
        ctx.s_a.wait(ctx.pe)
        ctx.s_pe.inc(ctx.pe.matmul(psB[:, 0:TCH], consts["ones_row"][:, :], mrs_b,
                                   start=True, stop=True))
        ctx.s_pe.wait(ctx.act)
        ctx.s_a.inc(ctx.act.activation(out=mrs_bb, in_=psB[:, 0:TCH], func=AF.Copy))
    nrm["bcast_guard"][0] = ctx.s_a.n
    ctx.s_a.wait(ctx.dve)
    if bufs["xn_guard"][par]:
        ctx.s_pe.wait(ctx.dve, bufs["xn_guard"][par])
    gain = bufs.get("gain_cols")
    for j in range(KC):
        sl = slice(xofs + j * TCH, xofs + (j + 1) * TCH)
        ssl = slice(sofs + j * TCH, sofs + (j + 1) * TCH)
        if rms and gain is not None:
            ctx.s_v.inc(ctx.dve.scalar_tensor_tensor(out=XN[:, sl], in0=src[:, ssl],
                                                     scalar=gain[:, j:j + 1],
                                                     in1=rstd_bb, op0=ALU.mult,
                                                     op1=ALU.mult))
        else:
            ctx.s_v.inc(ctx.dve.tensor_tensor(out=XN[:, sl], in0=src[:, ssl],
                                              in1=rstd_bb, op=ALU.mult))
    if not rms:
        ctx.s_v.wait(ctx.dve)
        for j in range(KC):
            sl = slice(xofs + j * TCH, xofs + (j + 1) * TCH)
            ctx.s_v.inc(ctx.dve.tensor_tensor(out=XN[:, sl], in0=XN[:, sl], in1=mrs_bb,
                                              op=ALU.subtract))
    bufs["xb_guard"][parx] = (ctx.s_v.n, pe_stats_done)
    nrm["rbs_guard"][0] = ctx.s_v.n
    return


def alloc_nrm(ctx, tag):
    return {
        "psS": ctx.psum(f"lnS_{tag}", [1, 2 * TCH]),
        "psB": ctx.psum(f"lnB_{tag}", [P, TCH]),
        "st": ctx.sbuf(f"lnst_{tag}", [1, 4 * TCH], F32),
        "rowb": ctx.sbuf(f"lnrowb_{tag}", [1, 2 * TCH], BF16),
        "rbs": ctx.sbuf(f"lnrbs_{tag}", [P, 2 * TCH], BF16),
        "stats_guard": [0],
        "bcast_guard": [0],
        "st_guard": [0],
        "rbs_guard": [0],
    }


def alloc_ln_bufs(ctx, tag, nrm, tiles=True, nslx=2, nslxn=2):
    b = {
        "nrm": nrm,
        "nslx": nslx,
        "s_x": [ctx.sem(f"lnsx0_{tag}"), ctx.sem(f"lnsx1_{tag}")],
        "xb_guard": [None, None],
        "xn_guard": [0, 0],
    }
    b["nslxn"] = nslxn
    if tiles:
        b["X"] = ctx.sbuf(f"lnX_{tag}", [P, nslx * KC * TCH], F32)
        b["XB"] = ctx.sbuf(f"lnXB_{tag}", [P, nslx * KC * TCH], BF16)
        b["XN"] = ctx.sbuf(f"lnXN_{tag}", [P, nslxn * KC * TCH], BF16)
    b["SQ"] = ctx.sbuf(f"lnSQ_{tag}", [P, nslx * KC * TCH], BF16)
    return b
def emit_l0aa(ctx, consts, th, aps, n_chunks, pre):
    nc = ctx.nc
    SETUP = int(os.environ.get('L0AA_SETUP', 4))
    if SETUP >= 2:
        w1T = load_weight_T_big(ctx, consts, th, aps["W1"], "w1")
    stg23 = ctx.sbuf("stg23", [64, D], BF16)
    if SETUP < 3:
        return
    s23 = ctx.sem("s23")
    m = ctx.dve.memset(stg23[:, :], 0.0)
    ctx.s_v.inc(m)
    ctx.s_v.wait(ctx.pool)
    load_cast_bf16(ctx, stg23[0:16, :], aps["W2"][:, :], sem=s23)
    load_cast_bf16(ctx, stg23[32:48, :], aps["W3"][:, :], sem=s23)
    s23.wait(ctx.pe)
    w23T = ctx.sbuf("w23T", [P, KC * 64], BF16)
    for kc in range(KC):
        th.transpose(stg23[0:64, kc * P:(kc + 1) * P], w23T[:, kc * 64:(kc + 1) * 64], rows=64)
    b1c = ctx.sbuf("b1c", [P, KC], F32)
    nb1c = ctx.sbuf("nb1c", [P, KC], F32)
    if SETUP < 4:
        return
    sb = ctx.sem("sb_l0aa")
    for j in range(KC):
        sb.inc(ctx.pool.dma_start(
            out=b1c[:, j:j + 1],
            in_=aps["b1"][j * P:(j + 1) * P].rearrange("(p one) -> p one", one=1)), 16)
    b2c = ctx.sbuf("b2c", [16, 1], F32)
    b3c = ctx.sbuf("b3c", [16, 1], F32)
    sb.inc(ctx.pool.dma_start(
        out=b2c[:, :], in_=aps["b2"][:].rearrange("(p one) -> p one", one=1)), 16)
    sb.inc(ctx.pool.dma_start(
        out=b3c[:, :], in_=aps["b3"][:].rearrange("(p one) -> p one", one=1)), 16)
    sb.wait(ctx.dve)
    ctx.s_v.inc(ctx.dve.tensor_scalar(out=nb1c[:, :], in0=b1c[:, :], scalar1=-1.0,
                                      scalar2=None, op0=ALU.mult))
    setup_act = ctx.s_a.n
    setup_v = ctx.s_v.n
    setup_sb = sb.n

    ln = pre["ln"]
    psA, psB_, psC = pre["psA"], pre["psB"], pre["psC"]
    Sd = pre["Sd"]
    Dl = pre["Dl"]
    BCP = pre["BCP"]
    BCC = pre["BCC"]
    PRm = pre["PR"]
    bcrow = pre["bcrow"]
    bcb = pre["bcb"]
    Yt = pre["Yt"]
    sst = pre["sst"]
    bankg = pre["bankg"]
    cg = pre["cg"]
    store_v = pre["store_v"]
    first = pre.get("first", True)

    gemm_banks = [psA, psB_]
    for b in range(NB):
        resid_in = aps["rin"][b]
        resid_out = aps["rout"][b]
        STAGE = int(os.environ.get('L0AA_STAGE', 6))
        for c in range(n_chunks):
            par = c % 2
            xofs = par * KC * TCH
            if STAGE < 2:
                continue
            emit_ln_chunk(ctx, consts, resid_in, c, ln,
                          load_guard=(sst[par], sst[par].n))
            xn_ready = ctx.s_v.n
            if first and b == 0 and c == 0:
                ctx.s_a.wait(ctx.pe, setup_act)
                ctx.s_v.wait(ctx.pe, setup_v)
                sb.wait(ctx.act, setup_sb)
                ctx.s_v.wait(ctx.act, setup_v)

            def XNs(kc):
                return ln["XN"][:, xofs + kc * TCH: xofs + (kc + 1) * TCH]

            if STAGE < 3:
                continue
            ctx.s_v.wait(ctx.pe, xn_ready)
            if cg[0]:
                ctx.s_a.wait(ctx.pe, cg[0])
            for kc in range(KC):
                ctx.s_pe.inc(ctx.pe.matmul(psC[0:64, :], w23T[:, kc * 64:(kc + 1) * 64],
                                           XNs(kc), start=(kc == 0), stop=(kc == KC - 1)))
            ctx.s_pe.wait(ctx.act)
            ctx.s_a.inc(ctx.act.activation(out=BCP[:, par * TCH:(par + 1) * TCH],
                                           in_=psC[0:16, :], func=AF.Identity,
                                           bias=b2c[:, :], scale=1.0))
            ctx.s_a.inc(ctx.act.activation(out=BCC[:, par * TCH:(par + 1) * TCH],
                                           in_=psC[32:48, :], func=AF.Identity,
                                           bias=b3c[:, :], scale=1.0))
            bc_evac = ctx.s_a.n
            ctx.s_a.wait(ctx.dve)
            ctx.s_v.inc(ctx.dve.tensor_tensor(
                out=PRm[:, par * TCH:(par + 1) * TCH],
                in0=BCP[:, par * TCH:(par + 1) * TCH],
                in1=BCC[:, par * TCH:(par + 1) * TCH], op=ALU.mult))
            pr_done = ctx.s_v.n

            if STAGE < 4:
                continue
            def delta_gemm(oc):
                bank = gemm_banks[oc % 2]
                if bankg[oc % 2]:
                    ctx.s_a.wait(ctx.pe, bankg[oc % 2])
                for kc in range(KC):
                    ctx.s_pe.inc(ctx.pe.matmul(
                        bank[:, :], w1T[:, (kc * KC + oc) * P:(kc * KC + oc + 1) * P],
                        XNs(kc), start=(kc == 0), stop=(kc == KC - 1)))
                ctx.s_pe.wait(ctx.act)
                ctx.s_a.wait(ctx.act)
                ctx.s_a.inc(ctx.act.activation(out=Sd[:, :], in_=bank[:, :],
                                               func=AF.Sigmoid, bias=nb1c[:, oc:oc + 1],
                                               scale=-1.0))
                bankg[oc % 2] = ctx.s_a.n
                ctx.s_a.wait(ctx.act)
                ctx.s_a.inc(ctx.act.activation(
                    out=Dl[:, xofs + oc * TCH: xofs + (oc + 1) * TCH],
                    in_=Sd[:, :], func=AF.Ln))

            if STAGE >= 5:
                delta_gemm(0)
                delta_gemm(1)

            ctx.s_a.wait(ctx.pe, bc_evac)
            ctx.s_v.wait(ctx.pe, pr_done)
            ctx.s_pe.inc(ctx.pe.matmul(psC[0:1, :], consts["ones16n"][:, :],
                                       PRm[:, par * TCH:(par + 1) * TCH],
                                       start=True, stop=True))
            ctx.s_pe.wait(ctx.act)
            ctx.s_a.inc(ctx.act.activation(out=bcrow[0:1, par * TCH:(par + 1) * TCH],
                                           in_=psC[0:1, :], func=AF.Copy))
            bcrow_done = ctx.s_a.n

            if STAGE >= 5:
                delta_gemm(2)
                delta_gemm(3)

            ctx.s_a.wait(ctx.pe, bcrow_done)
            ctx.s_pe.inc(ctx.pe.matmul(psC[0:P, :], consts["ones_row"][:, :],
                                       bcrow[0:1, par * TCH:(par + 1) * TCH],
                                       start=True, stop=True))
            ctx.s_pe.wait(ctx.act)
            ctx.s_a.inc(ctx.act.activation(out=bcb[:, par * TCH:(par + 1) * TCH],
                                           in_=psC[0:P, :], func=AF.Copy))
            cg[0] = ctx.s_a.n
            bcb_done = ctx.s_a.n

            if STAGE >= 5:
                for oc in range(4, KC):
                    delta_gemm(oc)
            if STAGE < 6:
                continue

            ln["xn_guard"][par] = ctx.s_pe.n
            act_delta_done = ctx.s_a.n
            ctx.s_a.wait(ctx.dve, act_delta_done)
            for oc in range(KC):
                osl = slice(xofs + oc * TCH, xofs + (oc + 1) * TCH)
                ctx.s_v.wait(ctx.dve)
                ctx.s_v.inc(ctx.dve.tensor_tensor(
                    out=Yt[:, par * TCH:(par + 1) * TCH], in0=Dl[:, osl],
                    in1=bcb[:, par * TCH:(par + 1) * TCH], op=ALU.mult))
                ctx.s_v.wait(ctx.dve)
                ctx.s_v.inc(ctx.dve.tensor_tensor(
                    out=Yt[:, par * TCH:(par + 1) * TCH],
                    in0=Yt[:, par * TCH:(par + 1) * TCH],
                    in1=ln["XN"][:, osl], op=ALU.mult))
                ctx.s_v.wait(ctx.dve)
                ctx.s_v.inc(ctx.dve.tensor_tensor(
                    out=ln["X"][:, osl], in0=Yt[:, par * TCH:(par + 1) * TCH],
                    in1=ln["X"][:, osl], op=ALU.add))
                ctx.s_v.wait(ctx.sp)
                sst[par].inc(ctx.sp.dma_start(
                    out=resid_out[oc * P:(oc + 1) * P, c * TCH:(c + 1) * TCH],
                    in_=ln["X"][:, osl]), 16)
    pre["first"] = False


def build_l0aa(n_chunks=int(os.environ.get('L0AA_NC', 8))):
    nc = bass.Bass("TRN2", target_bir_lowering=False)
    seq = n_chunks * TCH
    rin = nc.dram_tensor("rin", [NB, D, seq], F32, kind="ExternalInput").ap()
    W1 = nc.dram_tensor("W1", [D, D], F32, kind="ExternalInput").ap()
    b1 = nc.dram_tensor("b1", [D], F32, kind="ExternalInput").ap()
    W2 = nc.dram_tensor("W2", [16, D], F32, kind="ExternalInput").ap()
    b2 = nc.dram_tensor("b2", [16], F32, kind="ExternalInput").ap()
    W3 = nc.dram_tensor("W3", [16, D], F32, kind="ExternalInput").ap()
    b3 = nc.dram_tensor("b3", [16], F32, kind="ExternalInput").ap()
    rout = nc.dram_tensor("rout", [NB, D, seq], F32, kind="ExternalOutput").ap()
    caps = declare_consts(nc)

    es = contextlib.ExitStack()
    with es:
        ctx = Ctx(nc, es)
        consts = load_consts(ctx, caps)
        th = TransposeHelper(ctx, consts)
        pre = {
            "ln": alloc_ln_bufs(ctx, "l0aa", alloc_nrm(ctx, "l0aa")),
            "psA": ctx.psum("psA", [P, TCH]),
            "psB": ctx.psum("psB", [P, TCH]),
            "psC": ctx.psum("psC", [P, TCH]),
            "Sd": ctx.sbuf("Sd", [P, TCH], F32),
            "Dl": ctx.sbuf("Dl", [P, 2 * KC * TCH], BF16),
            "BCP": ctx.sbuf("BCP", [16, 2 * TCH], F32),
            "BCC": ctx.sbuf("BCC", [16, 2 * TCH], F32),
            "PR": ctx.sbuf("PRm", [16, 2 * TCH], F32),
            "bcrow": ctx.sbuf("bcrow", [1, 2 * TCH], BF16),
            "bcb": ctx.sbuf("bcb", [P, 2 * TCH], BF16),
            "Yt": ctx.sbuf("Yt", [P, 2 * TCH], F32),
            "sst": [ctx.sem("sst0"), ctx.sem("sst1")],
            "bankg": {0: 0, 1: 0},
            "cg": [0],
            "store_v": [0, 0],
        }
        aps = {"rin": rin, "rout": rout, "W1": W1, "b1": b1, "W2": W2, "b2": b2,
               "W3": W3, "b3": b3}
        emit_l0aa(ctx, consts, th, aps, n_chunks, pre)
        for s in pre["sst"]:
            s.wait(ctx.sp)
    return nc


NE = 8
HID = 256
BIGNEG = 1.0e9
def moe_consts():
    c = np.zeros((9, NE * P), np.float32)
    for e in range(NE):
        c[e, e * P:(e + 1) * P] = BIGNEG
        c[8, e * P:(e + 1) * P] = -BIGNEG
    return {"c_moepen": c.astype(ml_dtypes.bfloat16)}


def declare_moe_consts(nc):
    return {"moepen": nc.dram_tensor("c_moepen", [9, NE * P], BF16,
                                     kind="ExternalInput").ap()}


def emit_l1ffn(ctx, consts, th, aps, n_chunks, pre):
    nc = ctx.nc
    s_ws = ctx.sem("sws_l1ffn")
    pen = ctx.sbuf("moepen", [9, NE * P], BF16)
    s_ws.inc(ctx.pool.dma_start(out=pen[:, :], in_=aps["moepen"][:, :]), 16)
    stg8 = ctx.sbuf("stg8", [8, D], F32)
    s_ws.inc(ctx.pool.dma_start(out=stg8[:, :], in_=aps["Wg"][:, :]), 16)
    s_ws.wait(ctx.pe)
    wgTf = ctx.sbuf("wgTf", [P, KC * 8], F32)
    pstF0 = pre["pstF"]
    tguard = 0
    for kc in range(KC):
        if tguard:
            ctx.s_a.wait(ctx.pe, tguard)
        ctx.s_pe.inc(ctx.pe.transpose(pstF0[0:P, 0:8],
                                      stg8[0:8, kc * P:(kc + 1) * P],
                                      consts["eyef"][0:8, 0:8]))
        ctx.s_pe.wait(ctx.act)
        ctx.s_a.inc(ctx.act.activation(out=wgTf[:, kc * 8:(kc + 1) * 8],
                                       in_=pstF0[0:P, 0:8], func=AF.Copy))
        tguard = ctx.s_a.n
    if tguard:
        ctx.s_a.wait(ctx.pe, tguard)
    for kc in range(KC):
        ctx.s_pe.inc(ctx.pe.matmul(pstF0[0:8, 0:1], wgTf[:, kc * 8:(kc + 1) * 8],
                                   consts["onesKf"][:, :], start=(kc == 0),
                                   stop=(kc == KC - 1)))
    ctx.s_pe.wait(ctx.act)
    ncol = ctx.sbuf("ncol", [8, 1], F32)
    ctx.s_a.inc(ctx.act.activation(out=ncol[:, :], in_=pstF0[0:8, 0:1], func=AF.Copy,
                                   scale=-1.0))
    pre["pstF_g"][0] = ctx.s_a.n
    stgS = ctx.sbuf("stgS", [8, D], BF16)
    stgB = ctx.sbuf("stgB", [8, D], BF16)
    sS = ctx.sem("sS_l1ffn")
    load_cast_bf16(ctx, stgS[:, :], aps["ln_scale"][:, :], sem=sS)
    load_cast_bf16(ctx, stgB[:, :], aps["ln_bias"][:, :], sem=sS)
    sS.wait(ctx.pe)
    scolb = ctx.sbuf("scolb", [P, KC * 8], BF16)
    bcolb = ctx.sbuf("bcolb", [P, KC * 8], BF16)
    for kc in range(KC):
        th.transpose(stgS[0:8, kc * P:(kc + 1) * P], scolb[:, kc * 8:(kc + 1) * 8], rows=8)
        th.transpose(stgB[0:8, kc * P:(kc + 1) * P], bcolb[:, kc * 8:(kc + 1) * 8], rows=8)
    scolf = ctx.sbuf("scolf", [P, KC * 8], F32)
    ctx.s_a.wait(ctx.act)
    ctx.s_a.inc(ctx.act.activation(out=scolf[:, :], in_=scolb[:, :], func=AF.Copy))
    w1b = ctx.sbuf("w1b", [P, NE * KC * HID], BF16)
    sW1 = ctx.sem("sW1")
    for e in range(NE):
        for kc in range(KC):
            sW1.inc(ctx.pool.dma_start(
                out=w1b[:, (e * KC + kc) * HID:(e * KC + kc + 1) * HID],
                in_=aps["W1"][e, kc * P:(kc + 1) * P, :]), 16)
    w2b = ctx.sbuf("w2b", [P, NE * 2 * D], BF16)
    sW2 = ctx.sem("sW2")
    for e in range(NE):
        for hc in range(2):
            sW2.inc(ctx.pool.dma_start(
                out=w2b[:, (e * 2 + hc) * D:(e * 2 + hc + 1) * D],
                in_=aps["W2"][e, hc * P:(hc + 1) * P, :]), 16)
    b2l = ctx.sbuf("b2l", [8, KC * P], BF16)
    for oc in range(KC):
        sW2.inc(ctx.pool.dma_start(out=b2l[:, oc * P:(oc + 1) * P],
                                   in_=aps["b2"][:, oc * P:(oc + 1) * P]), 16)
    b1c = ctx.sbuf("b1c_ffn", [P, NE * 2], F32)
    sb1 = ctx.sem("sb1")
    for e in range(NE):
        for h in range(2):
            sb1.inc(ctx.pool.dma_start(
                out=b1c[:, e * 2 + h:e * 2 + h + 1],
                in_=aps["b1"][e, h * P:(h + 1) * P].rearrange("(p one) -> p one", one=1)), 16)
    ps1 = pre["pstF"]
    b1p = ctx.sbuf("b1p", [P, NE * 2], F32)
    sW1.wait(ctx.pe)
    sS.wait(ctx.pe)
    ctx.s_a.wait(ctx.pe)
    prev_ev = pre["pstF_g"][0]
    for e in range(NE):
        for h in range(2):
            if prev_ev:
                ctx.s_a.wait(ctx.pe, prev_ev)
            for kc in range(KC):
                ctx.s_pe.inc(ctx.pe.matmul(
                    ps1[0:P, 0:1],
                    w1b[:, (e * KC + kc) * HID + h * P:(e * KC + kc) * HID + (h + 1) * P],
                    bcolb[:, kc * 8 + e:kc * 8 + e + 1],
                    start=(kc == 0), stop=(kc == KC - 1)))
            ctx.s_pe.wait(ctx.act)
            sb1.wait(ctx.act)
            ctx.s_a.inc(ctx.act.activation(out=b1p[:, e * 2 + h:e * 2 + h + 1],
                                           in_=ps1[0:P, 0:1], func=AF.Identity,
                                           bias=b1c[:, e * 2 + h:e * 2 + h + 1],
                                           scale=1.0))
            prev_ev = ctx.s_a.n
    ctx.s_pe.wait(ctx.dve)
    ctx.s_a.wait(ctx.dve)
    for e in range(NE):
        for kc in range(KC):
            sl = slice((e * KC + kc) * HID, (e * KC + kc + 1) * HID)
            ctx.s_v.inc(ctx.dve.tensor_scalar(
                out=w1b[:, sl], in0=w1b[:, sl],
                scalar1=scolf[:, kc * 8 + e:kc * 8 + e + 1], scalar2=None,
                op0=ALU.mult))
    setup_v = ctx.s_v.n
    setup_a = ctx.s_a.n

    ln = pre["ln"]
    ln2 = pre["ln2"]
    pstF = pre["pstF"]
    phA, phB, psO = pre["phA"], pre["phB"], pre["psO"]
    lgS = pre["lgS"]
    LG = pre["LG"]
    Mx = pre["Mx"]
    mask01 = pre["mask01"]
    maskT = pre["maskT"]
    H = pre["H"]
    sst = pre["sst"]
    pstF_g = pre["pstF_g"]
    hg = pre["hg"]
    og = pre["og"]
    mg = pre["mg"]
    hwg = pre["hwg"]

    for par in range(2):
        for t in range(4):
            ctx.s_v.inc(ctx.dve.memset(mask01[:, par * 36 + 9 * t + 8: par * 36 + 9 * t + 9], 1.0))

    first = True
    for b in range(NB):
        resid_in = aps["rin"][b]
        resid_out = aps["rout"][b]
        for c in range(n_chunks):
            par = c % 2
            xofs = par * KC * TCH
            tsl = slice(par * TCH, (par + 1) * TCH)
            emit_ln_chunk(ctx, consts, resid_in, c, ln,
                          load_guard=[(sst[0], sst[0].n), (sst[1], sst[1].n)])
            outer_stats = ctx.s_pe.n
            xn_ready = ctx.s_v.n
            if first:
                ctx.s_v.wait(ctx.pe, setup_v)
                ctx.s_a.wait(ctx.pe, setup_a)
                sW2.wait(ctx.pe)
                s_ws.wait(ctx.pe)
                first = False

            def XNs(kc):
                return ln["XN"][:, xofs + kc * TCH: xofs + (kc + 1) * TCH]

            def XHs(kc):
                return ln2["XN"][:, xofs + kc * TCH: xofs + (kc + 1) * TCH]

            ln["s_x"][0].wait(ctx.pe)
            if pstF_g[0]:
                ctx.s_a.wait(ctx.pe, pstF_g[0])
            ctx.s_v.wait(ctx.pe, xn_ready)
            for kc in range(KC):
                ctx.s_pe.inc(ctx.pe.matmul(pstF[0:8, :], wgTf[:, kc * 8:(kc + 1) * 8],
                                           ln["X"][:, kc * TCH:(kc + 1) * TCH],
                                           start=(kc == 0), stop=(kc == KC - 1)))
            ctx.s_pe.wait(ctx.act)
            rbf = pre["rbf"]
            ctx.s_a.inc(ctx.act.activation(out=rbf[:, 0:TCH],
                                           in_=ln["nrm"]["rbs"][0:8, 0:TCH],
                                           func=AF.Copy))
            ctx.s_a.inc(ctx.act.activation(out=rbf[:, TCH:2 * TCH],
                                           in_=ln["nrm"]["rbs"][0:8, TCH:2 * TCH],
                                           func=AF.Copy))
            ctx.s_a.wait(ctx.dve)
            ctx.s_v.inc(ctx.dve.tensor_tensor(out=lgS[:, tsl], in0=pstF[0:8, :],
                                              in1=rbf[:, 0:TCH], op=ALU.mult))
            ctx.s_v.wait(ctx.dve)
            ctx.s_v.inc(ctx.dve.scalar_tensor_tensor(out=lgS[:, tsl], in0=rbf[:, TCH:2 * TCH],
                                                     scalar=ncol[:, :], in1=lgS[:, tsl],
                                                     op0=ALU.mult, op1=ALU.add))
            lg_done = ctx.s_v.n
            ctx.s_v.wait(ctx.pe, lg_done)
            pstF_g[0] = 0
            ctx.s_a.wait(ctx.pe)
            for t in range(4):
                ctx.s_pe.inc(ctx.pe.transpose(pstF[0:P, t * 8:(t + 1) * 8],
                                              lgS[0:8, par * TCH + t * P: par * TCH + (t + 1) * P],
                                              consts["eyef"][0:8, 0:8]))
            ctx.s_pe.wait(ctx.act)
            ctx.s_a.inc(ctx.act.activation(out=LG[:, par * 32:(par + 1) * 32],
                                           in_=pstF[0:P, 0:32], func=AF.Copy))
            pstF_g[0] = ctx.s_a.n
            ctx.s_a.wait(ctx.dve)
            for t in range(4):
                ctx.s_v.inc(ctx.dve.tensor_reduce(
                    out=Mx[:, par * 4 + t: par * 4 + t + 1],
                    in_=LG[:, par * 32 + t * 8: par * 32 + (t + 1) * 8],
                    axis=AX.X, op=ALU.max))
            ctx.s_v.wait(ctx.dve)
            for t in range(4):
                ctx.s_v.inc(ctx.dve.tensor_scalar(
                    out=mask01[:, par * 36 + 9 * t: par * 36 + 9 * t + 8],
                    in0=LG[:, par * 32 + t * 8: par * 32 + (t + 1) * 8],
                    scalar1=Mx[:, par * 4 + t: par * 4 + t + 1], scalar2=None,
                    op0=ALU.is_ge))
            msk_done = ctx.s_v.n
            ctx.s_v.wait(ctx.pe, msk_done)
            if mg[par]:
                ctx.s_pe.wait(ctx.act, mg[par])
            for t in range(4):
                th.transpose(mask01[:, par * 36 + 9 * t: par * 36 + 9 * (t + 1)],
                             maskT[:, par * TCH + t * P: par * TCH + (t + 1) * P],
                             rows=P, cols=9)
            mT_done = ctx.s_a.n
            ln2["pre_square_guard"] = (outer_stats, xn_ready)
            emit_ln_chunk(ctx, consts, None, c, ln2, src_sbuf=ln["XN"], src_ofs=xofs)
            xhat_ready = ctx.s_v.n
            banks = [phA, phB]
            ctx.s_v.wait(ctx.pe, xhat_ready)
            ctx.s_a.wait(ctx.pe, mT_done)
            k = 0
            for e in range(NE):
                for hc in range(2):
                    bank = banks[k % 2]
                    if hg[k % 2]:
                        ctx.s_a.wait(ctx.pe, hg[k % 2])
                    for kc in range(KC):
                        ctx.s_pe.inc(ctx.pe.matmul(
                            bank[:, :],
                            w1b[:, (e * KC + kc) * HID + hc * P:(e * KC + kc) * HID + (hc + 1) * P],
                            XHs(kc), start=(kc == 0), stop=False))
                    ctx.s_pe.inc(ctx.pe.matmul(
                        bank[:, :], pen[:, e * P:(e + 1) * P],
                        maskT[:, tsl], start=False, stop=True))
                    ctx.s_pe.wait(ctx.act)
                    if hwg[0]:
                        ctx.s_pe.wait(ctx.act, hwg[0])
                    ctx.s_a.inc(ctx.act.activation(
                        out=H[:, (e * 2 + hc) * TCH:(e * 2 + hc + 1) * TCH],
                        in_=bank[:, :], func=AF.Gelu,
                        bias=b1p[:, e * 2 + hc:e * 2 + hc + 1], scale=1.0))
                    hg[k % 2] = ctx.s_a.n
                    k += 1
            h_done = ctx.s_a.n
            ln["xn_guard"][par] = ctx.s_pe.n
            ln2["xn_guard"][par] = ctx.s_pe.n
            ctx.s_a.wait(ctx.pe, h_done)
            for oc in range(KC):
                if og[0]:
                    ctx.s_v.wait(ctx.pe, og[0])
                for e in range(NE):
                    for hc in range(2):
                        ctx.s_pe.inc(ctx.pe.matmul(
                            psO[:, :],
                            w2b[:, (e * 2 + hc) * D + oc * P:(e * 2 + hc) * D + (oc + 1) * P],
                            H[:, (e * 2 + hc) * TCH:(e * 2 + hc + 1) * TCH],
                            start=(e == 0 and hc == 0), stop=False))
                ctx.s_pe.inc(ctx.pe.matmul(psO[:, :], b2l[:, oc * P:(oc + 1) * P],
                                           maskT[0:8, tsl], start=False, stop=True))
                ctx.s_pe.wait(ctx.dve)
                osl = slice(oc * TCH, (oc + 1) * TCH)
                ctx.s_v.inc(ctx.dve.tensor_tensor(out=ln["X"][:, osl], in0=psO[:, :],
                                                  in1=ln["X"][:, osl], op=ALU.add))
                og[0] = ctx.s_v.n
                ctx.s_v.wait(ctx.sp)
                sst[par].inc(ctx.sp.dma_start(
                    out=resid_out[oc * P:(oc + 1) * P, c * TCH:(c + 1) * TCH],
                    in_=ln["X"][:, osl]), 16)
            hwg[0] = ctx.s_pe.n
            mg[par] = ctx.s_a.n


def build_l1ffn(n_chunks=8):
    nc = bass.Bass("TRN2", target_bir_lowering=False)
    seq = n_chunks * TCH
    rin = nc.dram_tensor("rin", [NB, D, seq], F32, kind="ExternalInput").ap()
    Wg = nc.dram_tensor("Wg", [NE, D], F32, kind="ExternalInput").ap()
    ln_scale = nc.dram_tensor("ln_scale", [NE, D], F32, kind="ExternalInput").ap()
    ln_bias = nc.dram_tensor("ln_bias", [NE, D], F32, kind="ExternalInput").ap()
    W1 = nc.dram_tensor("W1", [NE, D, HID], F32, kind="ExternalInput").ap()
    b1 = nc.dram_tensor("b1", [NE, HID], F32, kind="ExternalInput").ap()
    W2 = nc.dram_tensor("W2", [NE, HID, D], F32, kind="ExternalInput").ap()
    b2 = nc.dram_tensor("b2", [NE, D], F32, kind="ExternalInput").ap()
    rout = nc.dram_tensor("rout", [NB, D, seq], F32, kind="ExternalOutput").ap()
    caps = declare_consts(nc)
    caps.update(declare_moe_consts(nc))

    es = contextlib.ExitStack()
    with es:
        ctx = Ctx(nc, es)
        consts = load_consts(ctx, caps)
        th = TransposeHelper(ctx, consts)
        nrm = alloc_nrm(ctx, "ffn")
        ln = alloc_ln_bufs(ctx, "ffn", nrm, nslx=1)
        ln2 = alloc_ln_bufs(ctx, "ffn2", nrm, tiles=False, nslx=1)
        ln2["SQ"] = ln["SQ"]
        ln2["XN"] = ctx.sbuf("lnXN_ffn2", [P, 2 * KC * TCH], BF16)
        ln2["X"] = None
        ln2["XB"] = None
        pre = {
            "ln": ln, "ln2": ln2,
            "pstF": ctx.psum("pstF", [P, TCH]),
            "phA": ctx.psum("phA", [P, TCH]),
            "phB": ctx.psum("phB", [P, TCH]),
            "psO": ctx.psum("psO", [P, TCH]),
            "lgS": ctx.sbuf("lgS", [8, 2 * TCH], F32),
            "rbf": ctx.sbuf("rbf", [8, 2 * TCH], F32),
            "LG": ctx.sbuf("LG", [P, 2 * 32], F32),
            "Mx": ctx.sbuf("Mx", [P, 2 * 4], F32),
            "mask01": ctx.sbuf("mask01", [P, 2 * 36], BF16),
            "maskT": ctx.sbuf("maskT", [9, 2 * TCH], BF16),
            "H": ctx.sbuf("H", [P, NE * 2 * TCH], BF16),
            "sst": [ctx.sem("fst0"), ctx.sem("fst1")],
            "pstF_g": [0], "hg": {0: 0, 1: 0}, "og": [0],
            "mg": [0, 0], "hwg": [0],
        }
        aps = {"rin": rin, "rout": rout, "Wg": Wg, "ln_scale": ln_scale,
               "ln_bias": ln_bias, "W1": W1, "b1": b1, "W2": W2, "b2": b2,
               "moepen": caps["moepen"]}
        emit_l1ffn(ctx, consts, th, aps, n_chunks, pre)
        for s in pre["sst"]:
            s.wait(ctx.sp)
    return nc


NH = 16
HD = 64
def l1aa_consts():
    eh = np.zeros((P, KC * NH), np.float32)
    ehb = np.zeros((NH, KC * P), np.float32)
    for oc in range(KC):
        for p in range(P):
            h = 2 * oc + (p // HD)
            eh[p, oc * NH + h] = 1.0
            ehb[h, oc * P + p] = 1.0
    return {"c_eh": eh.astype(ml_dtypes.bfloat16),
            "c_ehb": ehb.astype(ml_dtypes.bfloat16)}


def declare_l1aa_consts(nc):
    return {"eh": nc.dram_tensor("c_eh", [P, KC * NH], BF16, kind="ExternalInput").ap(),
            "ehb": nc.dram_tensor("c_ehb", [NH, KC * P], BF16, kind="ExternalInput").ap()}


def load_col_f32(ctx, dst, col, src_1d, sem):
    sem.inc(ctx.pool.dma_start(
        out=dst[:, col:col + 1],
        in_=src_1d.rearrange("(p one) -> p one", one=1)), 16)


def emit_l1aa(ctx, consts, th, aps, n_chunks, pre):
    nc = ctx.nc
    wT = {}
    stg_sh = ctx.sbuf("stg_sh", [P, D], BF16)
    for nm in ("g", "r", "k", "v", "o"):
        wT[nm] = load_weight_T_big(ctx, consts, th, aps["W" + nm], "w" + nm, stg=stg_sh)
    sb = ctx.sem("sb_l1aa")
    mcol = ctx.sbuf("mcol", [P, 4 * KC], F32)
    bcol = ctx.sbuf("bcol", [P, 5 * KC], F32)
    wcol = ctx.sbuf("wcol", [P, KC], F32)
    ucol = ctx.sbuf("ucol", [P, KC], F32)
    deccol = ctx.sbuf("deccol", [P, KC], F32)
    nmcol = ctx.sbuf("nmcol", [P, 4 * KC], F32)
    for i, nm in enumerate(("mg", "mr", "mk", "mv")):
        for j in range(KC):
            load_col_f32(ctx, mcol, i * KC + j, aps[nm][j * P:(j + 1) * P], sb)
    for i, nm in enumerate(("bg", "br", "bk", "bv", "bo")):
        for j in range(KC):
            load_col_f32(ctx, bcol, i * KC + j, aps[nm][j * P:(j + 1) * P], sb)
    for j in range(KC):
        load_col_f32(ctx, wcol, j, aps["w"].rearrange("h d -> (h d)")[j * P:(j + 1) * P], sb)
        load_col_f32(ctx, ucol, j, aps["u"].rearrange("h d -> (h d)")[j * P:(j + 1) * P], sb)
    ehs = ctx.sbuf("ehs", [P, KC * NH], BF16)
    ehbs = ctx.sbuf("ehbs", [NH, KC * P], BF16)
    sb.inc(ctx.pool.dma_start(out=ehs[:, :], in_=aps["eh"][:, :]), 16)
    sb.inc(ctx.pool.dma_start(out=ehbs[:, :], in_=aps["ehb"][:, :]), 16)
    sb.wait(ctx.act)
    sb.wait(ctx.dve)
    ctx.s_a.inc(ctx.act.activation(out=deccol[:, :], in_=wcol[:, :], func=AF.Exp))
    ctx.s_a.wait(ctx.act)
    ctx.s_a.inc(ctx.act.activation(out=deccol[:, :], in_=deccol[:, :], func=AF.Exp,
                                   scale=-1.0))
    ctx.s_v.inc(ctx.dve.tensor_scalar(out=nmcol[:, :], in0=mcol[:, :], scalar1=-1.0,
                                      scalar2=1.0, op0=ALU.mult, op1=ALU.add))
    setup_a, setup_v = ctx.s_a.n, ctx.s_v.n

    ln = pre["ln"]
    psA, psB_ = pre["psA"], pre["psB"]
    psH, psH2 = pre["psH"], pre["psH2"]
    MIX = pre["MIX"]
    XSH0 = pre["XSH0"]
    SG = pre["SG"]
    RW = pre["RW"]
    RWN = pre["RWN"]
    Kt, Vt, Rt = pre["Kt"], pre["Vt"], pre["Rt"]
    KVt, WUt = pre["KVt"], pre["WUt"]
    DEC = pre["DEC"]
    SQr = pre["SQr"]
    hst = pre["hst"]
    hrb = pre["hrb"]
    hbb = pre["hbb"]
    carry = pre["carry"]
    prevc = pre["prevc"]
    mt = pre["mt"]
    sst = pre["sst"]
    bankg = pre["bankg"]
    nrm = ln["nrm"]

    first = True
    for b in range(NB):
        resid_in = aps["rin"][b]
        resid_out = aps["rout"][b]
        ctx.s_v.wait(ctx.dve)
        ctx.s_v.inc(ctx.dve.memset(carry[:, :], 0.0))
        xlast = pre["xlast"]
        sxl = pre["sxl"][b % 2]
        for j in range(KC):
            sxl.inc(ctx.sp.dma_start(
                out=xlast[:, j:j + 1],
                in_=resid_in[j * P:(j + 1) * P, n_chunks * TCH - 1: n_chunks * TCH],
                allow_slow_non_contiguous=True), 16)
        sxl.wait(ctx.act)
        ctx.s_a.wait(ctx.act)
        xlb = pre["xlb"]
        xlsq = pre["xlsq"]
        ctx.s_a.inc(ctx.act.activation(out=xlb[:, :], in_=xlast[:, :], func=AF.Copy))
        ctx.s_a.inc(ctx.act.activation(out=xlsq[:, :], in_=xlast[:, :], func=AF.Square))
        ctx.s_a.wait(ctx.pe)
        psS = nrm["psS"]
        if nrm["stats_guard"][0]:
            ctx.s_v.wait(ctx.pe, nrm["stats_guard"][0])
        for j in range(KC):
            ctx.s_pe.inc(ctx.pe.matmul(psS[0:1, 0:1], consts["onesK"][:, :],
                                       xlb[:, j:j + 1], start=(j == 0), stop=(j == KC - 1)))
            ctx.s_pe.inc(ctx.pe.matmul(psS[0:1, TCH:TCH + 1], consts["onesK"][:, :],
                                       xlsq[:, j:j + 1], start=(j == 0), stop=(j == KC - 1)))
        ctx.s_pe.wait(ctx.dve)
        lstat = pre["lstat"]
        ctx.s_v.inc(ctx.dve.tensor_scalar(out=lstat[0:1, 0:1], in0=psS[0:1, 0:1],
                                          scalar1=1.0 / D, scalar2=None, op0=ALU.mult))
        ctx.s_v.wait(ctx.dve)
        ctx.s_v.inc(ctx.dve.tensor_tensor(out=lstat[0:1, 1:2], in0=lstat[0:1, 0:1],
                                          in1=lstat[0:1, 0:1], op=ALU.mult))
        ctx.s_v.wait(ctx.dve)
        ctx.s_v.inc(ctx.dve.scalar_tensor_tensor(out=lstat[0:1, 2:3], in0=psS[0:1, TCH:TCH + 1],
                                                 scalar=1.0 / D, in1=lstat[0:1, 1:2],
                                                 op0=ALU.mult, op1=ALU.subtract))
        nrm["stats_guard"][0] = ctx.s_v.n
        ctx.s_v.wait(ctx.act)
        ctx.s_a.wait(ctx.act)
        ctx.s_a.inc(ctx.act.activation(out=lstat[0:1, 3:4], in_=lstat[0:1, 2:3],
                                       func=AF.Ln, bias=1e-5))
        ctx.s_a.wait(ctx.act)
        ctx.s_a.inc(ctx.act.activation(out=lstat[0:1, 3:4], in_=lstat[0:1, 3:4],
                                       func=AF.Exp, scale=-0.5))
        if nrm["bcast_guard"][0]:
            ctx.s_a.wait(ctx.pe, nrm["bcast_guard"][0])
        ctx.s_a.wait(ctx.act)
        ctx.s_a.inc(ctx.act.activation(out=pre["lrow"][0:1, 0:1], in_=lstat[0:1, 3:4],
                                       func=AF.Copy))
        ctx.s_a.inc(ctx.act.activation(out=pre["lrow"][0:1, 1:2], in_=lstat[0:1, 0:1],
                                       func=AF.Copy))
        ctx.s_a.wait(ctx.pe)
        psB0 = nrm["psB"]
        ctx.s_pe.inc(ctx.pe.matmul(psB0[:, 0:2], consts["ones_row"][:, :],
                                   pre["lrow"][0:1, 0:2], start=True, stop=True))
        ctx.s_pe.wait(ctx.act)
        lcols = pre["lcols"]
        ctx.s_a.wait(ctx.act)
        ctx.s_a.inc(ctx.act.activation(out=lcols[:, :], in_=psB0[:, 0:2], func=AF.Copy))
        nrm["bcast_guard"][0] = ctx.s_a.n
        ctx.s_a.wait(ctx.dve)
        ctx.s_v.wait(ctx.dve)
        ctx.s_v.inc(ctx.dve.tensor_scalar(out=prevc[:, :], in0=xlb[:, :],
                                          scalar1=lcols[:, 1:2], scalar2=None,
                                          op0=ALU.subtract))
        ctx.s_v.wait(ctx.dve)
        ctx.s_v.inc(ctx.dve.tensor_scalar(out=prevc[:, :], in0=prevc[:, :],
                                          scalar1=lcols[:, 0:1], scalar2=None,
                                          op0=ALU.mult))

        for c in range(n_chunks):
            par = c % 2
            xnofs = 0
            emit_ln_chunk(ctx, consts, resid_in, c, ln,
                          load_guard=[(sst[0], sst[0].n), (sst[1], sst[1].n)])
            xn_ready = ctx.s_v.n
            if first:
                ctx.s_a.wait(ctx.pe, setup_a)
                ctx.s_v.wait(ctx.pe, setup_v)
                sb.wait(ctx.pe)
                first = False

            def XNsl(kc):
                return ln["XN"][:, xnofs + kc * TCH: xnofs + (kc + 1) * TCH]

            xsh_ready = ctx.s_v.n

            gemm_banks = [psA, psB_]
            evac = {}

            def mix_and_gemm(mi, nm, k_):
                ctx.s_pe.wait(ctx.dve)
                ctx.s_v.wait(ctx.dve, xn_ready)
                for kc in range(KC):
                    mc = mcol[:, mi * KC + kc: mi * KC + kc + 1]
                    nmc = nmcol[:, mi * KC + kc: mi * KC + kc + 1]
                    ctx.s_a.wait(ctx.act)
                    ctx.s_v.wait(ctx.act)
                    ctx.s_a.inc(ctx.act.activation(
                        out=mt[:, 1:TCH],
                        in_=ln["XN"][:, xnofs + kc * TCH: xnofs + (kc + 1) * TCH - 1],
                        func=AF.Identity, bias=0.0, scale=nmc))
                    ctx.s_a.wait(ctx.dve)
                    ctx.s_v.inc(ctx.dve.scalar_tensor_tensor(
                        out=MIX[:, kc * TCH + 1:(kc + 1) * TCH],
                        in0=ln["XN"][:, xnofs + kc * TCH + 1: xnofs + (kc + 1) * TCH],
                        scalar=mc, in1=mt[:, 1:TCH], op0=ALU.mult, op1=ALU.add))
                    ctx.s_v.wait(ctx.dve)
                    ctx.s_v.inc(ctx.dve.tensor_scalar(
                        out=MIX[:, kc * TCH:kc * TCH + 1], in0=prevc[:, kc:kc + 1],
                        scalar1=nmc, scalar2=None, op0=ALU.mult))
                    ctx.s_v.wait(ctx.dve)
                    ctx.s_v.inc(ctx.dve.scalar_tensor_tensor(
                        out=MIX[:, kc * TCH:kc * TCH + 1],
                        in0=ln["XN"][:, xnofs + kc * TCH: xnofs + kc * TCH + 1],
                        scalar=mc, in1=MIX[:, kc * TCH:kc * TCH + 1],
                        op0=ALU.mult, op1=ALU.add))
                mix_done = ctx.s_v.n
                ctx.s_v.wait(ctx.pe, mix_done)
                for oc in range(KC):
                    bank = gemm_banks[oc % 2]
                    if bankg[oc % 2]:
                        ctx.s_a.wait(ctx.pe, bankg[oc % 2][0])
                        ctx.s_v.wait(ctx.pe, bankg[oc % 2][1])
                    for kc in range(KC):
                        ctx.s_pe.inc(ctx.pe.matmul(
                            bank[:, :], wT[nm][:, (kc * KC + oc) * P:(kc * KC + oc + 1) * P],
                            MIX[:, kc * TCH:(kc + 1) * TCH],
                            start=(kc == 0), stop=(kc == KC - 1)))
                    k_(oc, bank)
                return mix_done

            def ev_g(oc, bank):
                ctx.s_pe.wait(ctx.act)
                ctx.s_a.inc(ctx.act.activation(out=SG[:, oc * TCH:(oc + 1) * TCH],
                                               in_=bank[:, :], func=AF.Sigmoid,
                                               bias=bcol[:, 0 * KC + oc:0 * KC + oc + 1],
                                               scale=1.0))
                bankg[oc % 2] = (ctx.s_a.n, 0)

            mix_and_gemm(0, "g", ev_g)

            KB, VB = pre["KB"], pre["VB"]

            def ev_k(oc, bank):
                ctx.s_pe.wait(ctx.act)
                ctx.s_a.inc(ctx.act.activation(out=KB[:, oc * TCH:(oc + 1) * TCH],
                                               in_=bank[:, :], func=AF.Identity,
                                               bias=bcol[:, 2 * KC + oc:2 * KC + oc + 1],
                                               scale=1.0))
                bankg[oc % 2] = (ctx.s_a.n, 0)

            def ev_v(oc, bank):
                ctx.s_pe.wait(ctx.act)
                ctx.s_a.inc(ctx.act.activation(out=VB[:, oc * TCH:(oc + 1) * TCH],
                                               in_=bank[:, :], func=AF.Identity,
                                               bias=bcol[:, 3 * KC + oc:3 * KC + oc + 1],
                                               scale=1.0))
                bankg[oc % 2] = (ctx.s_a.n, 0)

            def ev_r(oc, bank):
                ctx.s_pe.wait(ctx.act)
                ctx.s_a.inc(ctx.act.activation(out=Rt[:, oc * TCH:(oc + 1) * TCH],
                                               in_=bank[:, :], func=AF.Identity,
                                               bias=bcol[:, 1 * KC + oc:1 * KC + oc + 1],
                                               scale=1.0))
                bankg[oc % 2] = (ctx.s_a.n, 0)

            mix_and_gemm(2, "k", ev_k)
            mix_and_gemm(3, "v", ev_v)
            k_done = ctx.s_a.n
            mix_and_gemm(1, "r", ev_r)
            r_done = ctx.s_a.n

            ctx.s_a.wait(ctx.dve, k_done)
            for oc in range(KC):
                osl = slice(oc * TCH, (oc + 1) * TCH)
                ctx.s_v.wait(ctx.dve)
                ctx.s_v.inc(ctx.dve.tensor_tensor(out=KVt[:, :], in0=KB[:, osl],
                                                  in1=VB[:, osl], op=ALU.mult))
                ctx.s_a.wait(ctx.act)
                ctx.s_v.wait(ctx.act)
                ctx.s_a.inc(ctx.act.activation(out=DEC[:, :], in_=pre["onesT"][:, :],
                                               func=AF.Identity, bias=0.0,
                                               scale=deccol[:, oc:oc + 1]))
                ctx.s_a.wait(ctx.dve)
                ctx.s_v.wait(ctx.dve)
                ctx.s_v.inc(ctx.dve.tensor_tensor_scan(
                    out=WUt[:, :], data0=DEC[:, :], data1=KVt[:, :],
                    initial=carry[:, oc:oc + 1], op0=ALU.mult, op1=ALU.add))
                ctx.s_v.wait(ctx.dve)
                ctx.s_v.inc(ctx.dve.tensor_copy(carry[:, oc:oc + 1], WUt[:, TCH - 1:TCH]))
                ctx.s_v.wait(ctx.dve)
                ctx.s_v.inc(ctx.dve.scalar_tensor_tensor(out=pre["WUb"][:, :], in0=KVt[:, :],
                                                         scalar=ucol[:, oc:oc + 1],
                                                         in1=WUt[:, :], op0=ALU.mult,
                                                         op1=ALU.add))
                ctx.s_v.wait(ctx.dve)
                ctx.s_v.inc(ctx.dve.tensor_tensor(out=RW[:, osl], in0=Rt[:, osl],
                                                  in1=pre["WUb"][:, :], op=ALU.mult))
            rw_done = ctx.s_v.n
            ctx.s_v.wait(ctx.act, rw_done)
            if pre["hstat_g"][0]:
                ctx.s_v.wait(ctx.pe, pre["hstat_g"][0])
            for oc in range(KC):
                osl = slice(oc * TCH, (oc + 1) * TCH)
                ctx.s_a.wait(ctx.act)
                ctx.s_pe.wait(ctx.act)
                ctx.s_a.inc(ctx.act.activation(out=SQr[:, :], in_=RW[:, osl], func=AF.Square))
                ctx.s_a.wait(ctx.pe)
                ctx.s_pe.inc(ctx.pe.matmul(psH[0:NH, :], ehs[:, oc * NH:(oc + 1) * NH],
                                           RW[:, osl], start=(oc == 0), stop=(oc == KC - 1)))
                ctx.s_pe.inc(ctx.pe.matmul(psH2[0:NH, :], ehs[:, oc * NH:(oc + 1) * NH],
                                           SQr[:, :], start=(oc == 0), stop=(oc == KC - 1)))
            ctx.s_pe.wait(ctx.dve)
            hm = hst[:, 0:TCH]
            hm2 = hst[:, TCH:2 * TCH]
            hv = hst[:, 2 * TCH:3 * TCH]
            hr = hst[:, 3 * TCH:4 * TCH]
            hmr = hm2
            ctx.s_v.wait(ctx.dve)
            ctx.s_v.inc(ctx.dve.tensor_scalar(out=hm, in0=psH[0:NH, :], scalar1=1.0 / HD,
                                              scalar2=None, op0=ALU.mult))
            ctx.s_v.wait(ctx.dve)
            ctx.s_v.inc(ctx.dve.tensor_tensor(out=hm2, in0=hm, in1=hm, op=ALU.mult))
            ctx.s_v.wait(ctx.dve)
            ctx.s_v.inc(ctx.dve.scalar_tensor_tensor(out=hv, in0=psH2[0:NH, :],
                                                     scalar=1.0 / HD, in1=hm2,
                                                     op0=ALU.mult, op1=ALU.subtract))
            pre["hstat_g"][0] = ctx.s_v.n
            ctx.s_v.wait(ctx.act)
            ctx.s_a.wait(ctx.act)
            ctx.s_a.inc(ctx.act.activation(out=hv, in_=hv, func=AF.Ln, bias=1e-5))
            ctx.s_a.wait(ctx.act)
            ctx.s_a.inc(ctx.act.activation(out=hr, in_=hv, func=AF.Exp, scale=-0.5))
            ctx.s_a.wait(ctx.dve)
            ctx.s_v.inc(ctx.dve.tensor_tensor(out=hmr, in0=hm, in1=hr, op=ALU.mult))
            ctx.s_v.wait(ctx.act)
            ctx.s_a.wait(ctx.act)
            ctx.s_a.inc(ctx.act.activation(out=hrb[:, 0:TCH], in_=hr, func=AF.Copy))
            ctx.s_a.inc(ctx.act.activation(out=hrb[:, TCH:2 * TCH], in_=hmr, func=AF.Copy))
            hrb_done = ctx.s_a.n
            ctx.s_pe.wait(ctx.dve)
            for oc in range(KC):
                osl = slice(oc * TCH, (oc + 1) * TCH)
                if nrm["bcast_guard"][0]:
                    ctx.s_a.wait(ctx.pe, nrm["bcast_guard"][0])
                ctx.s_a.wait(ctx.pe, hrb_done)
                psB0 = nrm["psB"]
                ctx.s_pe.inc(ctx.pe.matmul(psB0[:, 0:TCH], ehbs[:, oc * P:(oc + 1) * P],
                                           hrb[:, 0:TCH], start=True, stop=True))
                ctx.s_pe.wait(ctx.act)
                ctx.s_v.wait(ctx.act)
                ctx.s_a.inc(ctx.act.activation(out=hbb[:, 0:TCH], in_=psB0[:, 0:TCH],
                                               func=AF.Copy))
                hb1 = ctx.s_a.n
                ctx.s_a.wait(ctx.pe, hb1)
                ctx.s_pe.inc(ctx.pe.matmul(psB0[:, 0:TCH], ehbs[:, oc * P:(oc + 1) * P],
                                           hrb[:, TCH:2 * TCH], start=True, stop=True))
                ctx.s_pe.wait(ctx.act)
                ctx.s_a.inc(ctx.act.activation(out=hbb[:, TCH:2 * TCH], in_=psB0[:, 0:TCH],
                                               func=AF.Copy))
                nrm["bcast_guard"][0] = ctx.s_a.n
                ctx.s_a.wait(ctx.dve)
                ctx.s_v.wait(ctx.dve)
                ctx.s_v.inc(ctx.dve.tensor_tensor(out=RWN[:, osl], in0=RW[:, osl],
                                                  in1=hbb[:, 0:TCH], op=ALU.mult))
                ctx.s_v.wait(ctx.dve)
                ctx.s_v.inc(ctx.dve.tensor_tensor(out=RWN[:, osl], in0=RWN[:, osl],
                                                  in1=hbb[:, TCH:2 * TCH], op=ALU.subtract))
                ctx.s_v.wait(ctx.dve)
                ctx.s_v.inc(ctx.dve.tensor_tensor(out=RWN[:, osl], in0=RWN[:, osl],
                                                  in1=SG[:, osl], op=ALU.mult))
            rwn_done = ctx.s_v.n

            ctx.s_v.wait(ctx.dve)
            for kc in range(KC):
                ctx.s_v.inc(ctx.dve.tensor_copy(
                    prevc[:, kc:kc + 1],
                    ln["XN"][:, xnofs + (kc + 1) * TCH - 1: xnofs + (kc + 1) * TCH]))

            ln["xn_guard"][par] = ctx.s_pe.n
            ctx.s_v.wait(ctx.pe, rwn_done)
            for oc in range(KC):
                bank = gemm_banks[oc % 2]
                if bankg[oc % 2]:
                    ctx.s_a.wait(ctx.pe, bankg[oc % 2][0])
                    if bankg[oc % 2][1]:
                        ctx.s_v.wait(ctx.pe, bankg[oc % 2][1])
                for kc in range(KC):
                    ctx.s_pe.inc(ctx.pe.matmul(
                        bank[:, :], wT["o"][:, (kc * KC + oc) * P:(kc * KC + oc + 1) * P],
                        RWN[:, kc * TCH:(kc + 1) * TCH],
                        start=(kc == 0), stop=(kc == KC - 1)))
                ctx.s_pe.wait(ctx.dve)
                osl = slice(oc * TCH, (oc + 1) * TCH)
                ctx.s_v.inc(ctx.dve.scalar_tensor_tensor(
                    out=ln["X"][:, osl], in0=bank[:, :],
                    scalar=bcol[:, 4 * KC + oc:4 * KC + oc + 1],
                    in1=ln["X"][:, osl], op0=ALU.add, op1=ALU.add))
                bankg[oc % 2] = (0, ctx.s_v.n)
                ctx.s_v.wait(ctx.sp)
                sst[par].inc(ctx.sp.dma_start(
                    out=resid_out[oc * P:(oc + 1) * P, c * TCH:(c + 1) * TCH],
                    in_=ln["X"][:, osl]), 16)


def build_l1aa(n_chunks=8):
    nc = bass.Bass("TRN2", target_bir_lowering=False)
    seq = n_chunks * TCH
    rin = nc.dram_tensor("rin", [NB, D, seq], F32, kind="ExternalInput").ap()
    names = ["Wg", "Wr", "Wk", "Wv", "Wo"]
    aps = {"rin": rin}
    for nm in names:
        aps[nm] = nc.dram_tensor(nm, [D, D], F32, kind="ExternalInput").ap()
    for nm in ("bg", "br", "bk", "bv", "bo", "mg", "mr", "mk", "mv"):
        aps[nm] = nc.dram_tensor(nm, [D], F32, kind="ExternalInput").ap()
    aps["w"] = nc.dram_tensor("w", [NH, HD], F32, kind="ExternalInput").ap()
    aps["u"] = nc.dram_tensor("u", [NH, HD], F32, kind="ExternalInput").ap()
    aps["rout"] = nc.dram_tensor("rout", [NB, D, seq], F32, kind="ExternalOutput").ap()
    caps = declare_consts(nc)
    caps.update(declare_l1aa_consts(nc))
    aps["eh"], aps["ehb"] = caps["eh"], caps["ehb"]
    aps["Wgm"] = aps["Wg"]

    es = contextlib.ExitStack()
    with es:
        ctx = Ctx(nc, es)
        consts = load_consts(ctx, caps)
        th = TransposeHelper(ctx, consts)
        nrm = alloc_nrm(ctx, "aa")
        ln = alloc_ln_bufs(ctx, "aa", nrm, nslx=1, nslxn=1)
        pre = {
            "ln": ln,
            "psA": ctx.psum("psA", [P, TCH]),
            "psB": ctx.psum("psB2", [P, TCH]),
            "psH": ctx.psum("psH", [NH, TCH]),
            "psH2": ctx.psum("psH2", [NH, TCH]),
            "MIX": ctx.sbuf("MIX", [P, KC * TCH], BF16),
            "XSH0": ctx.sbuf("XSH0", [P, TCH], BF16),
            "SG": ctx.sbuf("SG", [P, KC * TCH], BF16),
            "RW": None,
            "RWN": None,
            "KB": ctx.sbuf("KB", [P, KC * TCH], BF16),
            "VB": ctx.sbuf("VB", [P, KC * TCH], BF16),
            "Kt": None, "Vt": None,
            "Rt": ctx.sbuf("Rt", [P, KC * TCH], BF16),
            "KVt": ctx.sbuf("KVt", [P, TCH], F32),
            "WUt": ctx.sbuf("WUt", [P, TCH], F32),
            "WUb": ctx.sbuf("WUb", [P, TCH], BF16),
            "DEC": ctx.sbuf("DEC", [P, TCH], F32),
            "SQr": ctx.sbuf("SQr", [P, TCH], BF16),
            "hst": ctx.sbuf("hst", [NH, 4 * TCH], F32),
            "hrb": ctx.sbuf("hrb", [NH, 2 * TCH], BF16),
            "hbb": ctx.sbuf("hbb", [P, 2 * TCH], BF16),
            "carry": ctx.sbuf("carry", [P, KC], F32),
            "prevc": ctx.sbuf("prevc", [P, KC], BF16),
            "mt": ctx.sbuf("mt", [P, TCH], BF16),
            "onesT": ctx.sbuf("onesT", [P, TCH], F32),
            "xlast": ctx.sbuf("xlast", [P, KC], F32),
            "xlb": ctx.sbuf("xlb", [P, KC], BF16),
            "xlsq": ctx.sbuf("xlsq", [P, KC], BF16),
            "lstat": ctx.sbuf("lstat", [1, 8], F32),
            "lrow": ctx.sbuf("lrow", [1, 2], BF16),
            "lcols": ctx.sbuf("lcols", [P, 2], F32),
            "sst": [ctx.sem("ast0"), ctx.sem("ast1")],
            "sxl": [ctx.sem("sxl0"), ctx.sem("sxl1")],
            "bankg": {0: None, 1: None},
            "hstat_g": [0],
        }
        pre["RWN"] = pre["MIX"]
        pre["RW"] = ln["SQ"]
        ctx.s_v.inc(ctx.dve.memset(pre["onesT"][:, :], 1.0))
        ctx.s_v.wait(ctx.act)
        aps2 = dict(aps)
        aps2["Wg"], aps2["Wr"], aps2["Wk"], aps2["Wv"], aps2["Wo"] = (
            aps["Wg"], aps["Wr"], aps["Wk"], aps["Wv"], aps["Wo"])
        emit_l1aa(ctx, consts, th, aps2, n_chunks, pre)
        for s in pre["sst"]:
            s.wait(ctx.sp)
    return nc


AB_FF = 3072
AB_FH = AB_FF // 2
AB_HCH = AB_FH // P
EPSB = 1e-6


def phase_barrier(ctx, extra_sems=()):
    for eng in (ctx.pe, ctx.dve, ctx.act, ctx.sp, ctx.pool):
        ctx.s_pe.wait(eng)
        ctx.s_v.wait(eng)
        ctx.s_a.wait(eng)
        for s in extra_sems:
            s.wait(eng)


def load_w_direct(ctx, dst, w_ap, n_in, n_out, sem, col0=0):
    for kc in range(n_in):
        sem.inc(ctx.pool.dma_start(
            out=dst[:, col0 + kc * n_out * P: col0 + (kc + 1) * n_out * P],
            in_=w_ap[kc * P:(kc + 1) * P, :]), 16)


def emit_l1ab(ctx, consts, aps, n_chunks, outer_es):
    nc = ctx.nc
    with contextlib.ExitStack() as es1:
        c1 = Ctx.__new__(Ctx)
        c1.__dict__ = dict(ctx.__dict__)
        c1.es = es1
        nrm = alloc_nrm(c1, "ab1")
        ln = alloc_ln_bufs(c1, "ab1", nrm, nslx=1, nslxn=1)
        ln2 = alloc_ln_bufs(c1, "ab1i", nrm, tiles=False, nslx=1, nslxn=1)
        ln2["SQ"] = ln["SQ"]
        ln2["XN"] = c1.sbuf("lnXN_ab1i", [P, KC * TCH], BF16)
        ln2["X"] = None
        ln2["XB"] = None
        sW = ctx.sem("abW1")
        wg = c1.sbuf("ab_wg", [P, KC * D], BF16)
        wv = c1.sbuf("ab_wv", [P, KC * D], BF16)
        wd = c1.sbuf("ab_wd", [P, KC * D], BF16)
        load_w_direct(c1, wg, aps["Wg"], KC, KC, sW)
        load_w_direct(c1, wv, aps["Wv"], KC, KC, sW)
        load_w_direct(c1, wd, aps["Wd"], KC, KC, sW)
        sB = ctx.sem("abB1")
        gcol = c1.sbuf("ab_gcol", [P, KC], F32)
        bcol = c1.sbuf("ab_bcol", [P, 3 * KC], F32)
        for j in range(KC):
            sB.inc(c1.pool.dma_start(out=gcol[:, j:j + 1],
                                     in_=aps["w_rms_mix"][j * P:(j + 1) * P]
                                     .rearrange("(p one) -> p one", one=1)), 16)
        for i, nm in enumerate(("bg", "bv", "bd")):
            for j in range(KC):
                sB.inc(c1.pool.dma_start(out=bcol[:, i * KC + j:i * KC + j + 1],
                                         in_=aps[nm][j * P:(j + 1) * P]
                                         .rearrange("(p one) -> p one", one=1)), 16)
        ln2["gain_cols"] = gcol
        psA = c1.psum("ab_psA", [P, TCH])
        psB_ = c1.psum("ab_psB", [P, TCH])
        SGt = c1.sbuf("ab_SG", [P, TCH], BF16)
        TVt = c1.sbuf("ab_TV", [P, TCH], BF16)
        XS = c1.sbuf("ab_XS", [P, KC * TCH], F32)
        At = c1.sbuf("ab_At", [P, KC * TCH], F32)
        SCt = c1.sbuf("ab_SC", [P, TCH], F32)
        s_sc = ctx.sem("ab_ssc")
        carry = c1.sbuf("ab_carry", [P, KC], F32)
        sst = [ctx.sem("abst0"), ctx.sem("abst1")]
        bankg = {0: None, 1: None}
        gb = [psA, psB_]
        sW.wait(c1.pe)
        sB.wait(c1.act)
        sB.wait(c1.dve)
        XNF = c1.sbuf("ab_XNF", [P, TCH], F32)
        for b in range(NB):
            rin = aps["rin"][b]
            r1 = aps["r1"][b]
            racc = aps["racc"][b]
            c1.s_v.wait(c1.dve)
            c1.s_v.inc(c1.dve.memset(carry[:, :], 0.0))
            for cc in range(n_chunks):
                par = cc % 2
                emit_ln_chunk(c1, consts, rin, cc, ln,
                              load_guard=[(sst[0], sst[0].n), (sst[1], sst[1].n),
                                          (s_sc, s_sc.n)])
                outer_stats = c1.s_pe.n
                t_ready = c1.s_v.n
                ln2["pre_square_guard"] = (outer_stats, t_ready)
                emit_ln_chunk(c1, consts, None, cc, ln2, eps=EPSB, rms=True,
                              src_sbuf=ln["XN"], src_ofs=0)
                xn_ready = c1.s_v.n

                def XNs(kc):
                    return ln2["XN"][:, kc * TCH:(kc + 1) * TCH]

                c1.s_v.wait(c1.pe, xn_ready)
                for oc in range(KC):
                    if bankg[0]:
                        c1.s_a.wait(c1.pe, bankg[0])
                    for kc in range(KC):
                        c1.s_pe.inc(c1.pe.matmul(psA[:, :],
                                                 wg[:, (kc * KC + oc) * P:(kc * KC + oc + 1) * P],
                                                 XNs(kc), start=(kc == 0), stop=(kc == KC - 1)))
                    if bankg[1]:
                        c1.s_a.wait(c1.pe, bankg[1])
                    for kc in range(KC):
                        c1.s_pe.inc(c1.pe.matmul(psB_[:, :],
                                                 wv[:, (kc * KC + oc) * P:(kc * KC + oc + 1) * P],
                                                 XNs(kc), start=(kc == 0), stop=(kc == KC - 1)))
                    pe_gv = c1.s_pe.n
                    c1.s_pe.wait(c1.act, pe_gv - KC)
                    c1.s_a.wait(c1.act)
                    c1.s_v.wait(c1.act)
                    c1.s_a.inc(c1.act.activation(out=SGt[:, :], in_=psA[:, :],
                                                 func=AF.Sigmoid,
                                                 bias=bcol[:, oc:oc + 1], scale=1.0))
                    bankg[0] = c1.s_a.n
                    c1.s_pe.wait(c1.act, pe_gv)
                    c1.s_a.inc(c1.act.activation(out=TVt[:, :], in_=psB_[:, :],
                                                 func=AF.Tanh,
                                                 bias=bcol[:, KC + oc:KC + oc + 1], scale=1.0))
                    bankg[1] = c1.s_a.n
                    c1.s_a.wait(c1.dve)
                    c1.s_v.inc(c1.dve.tensor_tensor(out=XS[:, oc * TCH:(oc + 1) * TCH],
                                                    in0=SGt[:, :], in1=TVt[:, :],
                                                    op=ALU.mult))
                    c1.s_a.wait(c1.pe, bankg[0])
                    for kc in range(KC):
                        c1.s_pe.inc(c1.pe.matmul(psA[:, :],
                                                 wd[:, (kc * KC + oc) * P:(kc * KC + oc + 1) * P],
                                                 XNs(kc), start=(kc == 0), stop=(kc == KC - 1)))
                    c1.s_pe.wait(c1.act)
                    c1.s_a.wait(c1.act)
                    c1.s_v.wait(c1.act)
                    c1.s_a.inc(c1.act.activation(out=SGt[:, :], in_=psA[:, :],
                                                 func=AF.Sigmoid,
                                                 bias=bcol[:, 2 * KC + oc:2 * KC + oc + 1],
                                                 scale=1.0))
                    bankg[0] = c1.s_a.n
                    c1.s_a.wait(c1.act)
                    c1.s_a.inc(c1.act.activation(out=At[:, oc * TCH:(oc + 1) * TCH],
                                                 in_=SGt[:, :], func=AF.Identity,
                                                 bias=0.001, scale=0.998))
                a_done = c1.s_a.n
                ln["xn_guard"][0] = c1.s_pe.n
                ln2["xn_guard"][0] = c1.s_pe.n
                c1.s_a.wait(c1.dve, a_done)
                for oc in range(KC):
                    osl = slice(oc * TCH, (oc + 1) * TCH)
                    c1.s_v.wait(c1.dve)
                    s_sc.wait(c1.dve, s_sc.n)
                    c1.s_v.inc(c1.dve.tensor_tensor_scan(
                        out=SCt[:, :], data0=At[:, osl], data1=XS[:, osl],
                        initial=carry[:, oc:oc + 1], op0=ALU.mult, op1=ALU.add))
                    c1.s_v.wait(c1.dve)
                    c1.s_v.inc(c1.dve.tensor_copy(carry[:, oc:oc + 1],
                                                  SCt[:, TCH - 1:TCH]))
                    c1.s_a.wait(c1.act)
                    c1.s_v.wait(c1.act)
                    c1.s_a.inc(c1.act.activation(out=XNF[:, :], in_=ln["XN"][:, osl],
                                                 func=AF.Copy))
                    c1.s_a.wait(c1.dve)
                    c1.s_v.wait(c1.dve)
                    c1.s_v.inc(c1.dve.tensor_tensor(out=SCt[:, :], in0=SCt[:, :],
                                                    in1=XNF[:, :], op=ALU.add))
                    c1.s_v.wait(c1.sp)
                    s_sc.inc(c1.sp.dma_start(
                        out=r1[oc * P:(oc + 1) * P, cc * TCH:(cc + 1) * TCH],
                        in_=SCt[:, :]), 16)
                    c1.s_v.wait(c1.dve)
                    c1.s_v.inc(c1.dve.tensor_tensor(out=ln["X"][:, osl], in0=SCt[:, :],
                                                    in1=ln["X"][:, osl], op=ALU.add))
                    c1.s_v.wait(c1.sp)
                    sst[par].inc(c1.sp.dma_start(
                        out=racc[oc * P:(oc + 1) * P, cc * TCH:(cc + 1) * TCH],
                        in_=ln["X"][:, osl]), 16)
        for s in sst:
            s.wait(c1.sp)
        s_sc.wait(c1.sp)
        phase_barrier(c1, extra_sems=sst + [s_sc])

    for half in range(2):
        with contextlib.ExitStack() as es2:
            c2 = Ctx.__new__(Ctx)
            c2.__dict__ = dict(ctx.__dict__)
            c2.es = es2
            nrm = alloc_nrm(c2, f"ab2{half}")
            ln = alloc_ln_bufs(c2, f"ab2{half}", nrm, nslx=1, nslxn=1)
            sW = ctx.sem(f"abW2{half}")
            wgt = c2.sbuf(f"ab_wgt{half}", [P, KC * AB_HCH * P], BF16)
            wup = c2.sbuf(f"ab_wup{half}", [P, KC * AB_HCH * P], BF16)
            wot = c2.sbuf(f"ab_wot{half}", [P, AB_HCH * KC * P], BF16)
            for kc in range(KC):
                sW.inc(c2.pool.dma_start(
                    out=wgt[:, kc * AB_HCH * P:(kc + 1) * AB_HCH * P],
                    in_=aps["W_gate"][kc * P:(kc + 1) * P, half * AB_FH:(half + 1) * AB_FH]), 16)
                sW.inc(c2.pool.dma_start(
                    out=wup[:, kc * AB_HCH * P:(kc + 1) * AB_HCH * P],
                    in_=aps["W_up"][kc * P:(kc + 1) * P, half * AB_FH:(half + 1) * AB_FH]), 16)
            for hc in range(AB_HCH):
                sW.inc(c2.pool.dma_start(
                    out=wot[:, hc * KC * P:(hc + 1) * KC * P],
                    in_=aps["W_out"][half * AB_FH + hc * P: half * AB_FH + (hc + 1) * P, :]), 16)
            sB = ctx.sem(f"abB2{half}")
            gcol = c2.sbuf(f"ab_g2{half}", [P, KC], F32)
            for j in range(KC):
                sB.inc(c2.pool.dma_start(out=gcol[:, j:j + 1],
                                         in_=aps["w_rms_ffn"][j * P:(j + 1) * P]
                                         .rearrange("(p one) -> p one", one=1)), 16)
            ln["gain_cols"] = gcol
            psA = c2.psum(f"ab2_psA{half}", [P, TCH])
            psB_ = c2.psum(f"ab2_psB{half}", [P, TCH])
            psO = c2.psum(f"ab2_psO{half}", [P, TCH])
            SILt = c2.sbuf(f"ab_SIL{half}", [P, TCH], BF16)
            UPt = c2.sbuf(f"ab_UP{half}", [P, TCH], BF16)
            HH = c2.sbuf(f"ab_HH{half}", [P, AB_HCH * TCH], BF16)
            XM = c2.sbuf(f"ab_XM{half}", [P, KC * TCH], F32)
            sxm = [ctx.sem(f"abxm0{half}"), ctx.sem(f"abxm1{half}")]
            sst = [ctx.sem(f"abs2a{half}"), ctx.sem(f"abs2b{half}")]
            bankg = {0: None, 1: None}
            og = [0]
            ogv = [0]
            sW.wait(c2.pe)
            sB.wait(c2.dve)
            sB.wait(c2.act)
            src_r = aps["r1"]
            acc_r = aps["racc"] if half == 0 else aps["rmid"]
            dst_r = aps["rmid"] if half == 0 else aps["rout"]
            for b in range(NB):
                for cc in range(n_chunks):
                    par = cc % 2
                    emit_ln_chunk(c2, consts, src_r[b], cc, ln, eps=EPSB, rms=True,
                                  load_guard=[(sst[0], sst[0].n), (sst[1], sst[1].n)])
                    xn_ready = c2.s_v.n
                    XMsrc = XM
                    sxm[0].wait(c2.sp, sxm[0].n)
                    sst[0].wait(c2.sp)
                    sst[1].wait(c2.sp)
                    for j in range(KC):
                        sxm[0].inc(c2.sp.dma_start(
                            out=XM[:, j * TCH:(j + 1) * TCH],
                            in_=acc_r[b][j * P:(j + 1) * P, cc * TCH:(cc + 1) * TCH]), 16)

                    def YNs(kc):
                        return ln["XN"][:, kc * TCH:(kc + 1) * TCH]

                    c2.s_v.wait(c2.pe, xn_ready)
                    for hc in range(AB_HCH):
                        if bankg[0]:
                            c2.s_a.wait(c2.pe, bankg[0])
                        for kc in range(KC):
                            c2.s_pe.inc(c2.pe.matmul(
                                psA[:, :], wgt[:, (kc * AB_HCH + hc) * P:(kc * AB_HCH + hc + 1) * P],
                                YNs(kc), start=(kc == 0), stop=(kc == KC - 1)))
                        if bankg[1]:
                            c2.s_a.wait(c2.pe, bankg[1])
                        for kc in range(KC):
                            c2.s_pe.inc(c2.pe.matmul(
                                psB_[:, :], wup[:, (kc * AB_HCH + hc) * P:(kc * AB_HCH + hc + 1) * P],
                                YNs(kc), start=(kc == 0), stop=(kc == KC - 1)))
                        pe_gu = c2.s_pe.n
                        c2.s_pe.wait(c2.act, pe_gu - KC)
                        c2.s_a.wait(c2.act)
                        c2.s_v.wait(c2.act)
                        c2.s_a.inc(c2.act.activation(out=SILt[:, :], in_=psA[:, :],
                                                     func=AF.Silu))
                        bankg[0] = c2.s_a.n
                        c2.s_pe.wait(c2.act, pe_gu)
                        c2.s_a.inc(c2.act.activation(out=UPt[:, :], in_=psB_[:, :],
                                                     func=AF.Copy))
                        bankg[1] = c2.s_a.n
                        c2.s_a.wait(c2.dve)
                        if og[0]:
                            c2.s_pe.wait(c2.dve, og[0])
                        c2.s_v.inc(c2.dve.tensor_tensor(out=HH[:, hc * TCH:(hc + 1) * TCH],
                                                        in0=SILt[:, :], in1=UPt[:, :],
                                                        op=ALU.mult))
                    h_done = c2.s_v.n
                    ln["xn_guard"][0] = c2.s_pe.n
                    c2.s_v.wait(c2.pe, h_done)
                    sxm[0].wait(c2.dve)
                    for oc in range(KC):
                        if ogv[0]:
                            c2.s_v.wait(c2.pe, ogv[0])
                        for hc in range(AB_HCH):
                            c2.s_pe.inc(c2.pe.matmul(
                                psO[:, :], wot[:, (hc * KC + oc) * P:(hc * KC + oc + 1) * P],
                                HH[:, hc * TCH:(hc + 1) * TCH],
                                start=(hc == 0), stop=(hc == AB_HCH - 1)))
                        c2.s_pe.wait(c2.dve)
                        osl = slice(oc * TCH, (oc + 1) * TCH)
                        c2.s_v.inc(c2.dve.tensor_tensor(out=XMsrc[:, osl], in0=psO[:, :],
                                                        in1=XMsrc[:, osl], op=ALU.add))
                        ogv[0] = c2.s_v.n
                        c2.s_v.wait(c2.sp)
                        sst[par].inc(c2.sp.dma_start(
                            out=dst_r[b][oc * P:(oc + 1) * P, cc * TCH:(cc + 1) * TCH],
                            in_=XMsrc[:, osl]), 16)
                    og[0] = c2.s_pe.n
            for s in sst:
                s.wait(c2.sp)
            phase_barrier(c2, extra_sems=sst + sxm)


def build_l1ab(n_chunks=8):
    nc = bass.Bass("TRN2", target_bir_lowering=False)
    seq = n_chunks * TCH
    aps = {"rin": nc.dram_tensor("rin", [NB, D, seq], F32, kind="ExternalInput").ap()}
    for nm, shp in (("Wg", [D, D]), ("Wv", [D, D]), ("Wd", [D, D]),
                    ("W_gate", [D, AB_FF]), ("W_up", [D, AB_FF]), ("W_out", [AB_FF, D])):
        aps[nm] = nc.dram_tensor(nm, shp, F32, kind="ExternalInput").ap()
    for nm in ("bg", "bv", "bd", "w_rms_mix", "w_rms_ffn"):
        aps[nm] = nc.dram_tensor(nm, [D], F32, kind="ExternalInput").ap()
    aps["rout"] = nc.dram_tensor("rout", [NB, D, seq], F32, kind="ExternalOutput").ap()
    aps["r1"] = nc.dram_tensor("r1", [NB, D, seq], F32, kind="Internal").ap()
    aps["racc"] = nc.dram_tensor("racc", [NB, D, seq], F32, kind="Internal").ap()
    aps["rmid"] = nc.dram_tensor("rmid", [NB, D, seq], F32, kind="Internal").ap()
    caps = declare_consts(nc)
    es = contextlib.ExitStack()
    with es:
        ctx = Ctx(nc, es)
        consts = load_consts(ctx, caps)
        emit_l1ab(ctx, consts, aps, n_chunks, es)
    return nc


FN_DH = 4096
FN_HCHT = FN_DH // P
FN_HH = FN_DH // 2
FN_HCH = FN_HH // P


def ffn_consts():
    u0 = np.zeros((64, FN_DH), np.float32)
    for i in range(FN_DH):
        u0[i % 64, i] = 1.0
    v0 = np.zeros((64, FN_DH), np.float32)
    for r in range(64):
        for j in range(64):
            v0[j, 64 * r + j] = 1.0
    return {"c_u0": u0.astype(ml_dtypes.bfloat16), "c_v0": v0.astype(ml_dtypes.bfloat16)}


def declare_ffn_consts(nc):
    return {"u0": nc.dram_tensor("c_u0", [64, FN_DH], BF16, kind="ExternalInput").ap(),
            "v0": nc.dram_tensor("c_v0", [64, FN_DH], BF16, kind="ExternalInput").ap()}



def wait_evac_done(ctx, sc, eng):
    for which in ("evac_state", "evacB_state"):
        if which in sc and sc[which]["sems"]:
            for s in sc[which]["sems"]:
                s.wait(eng)


def evac_store(ctx, sc, smid, ps_slice, dst_ap, rows, cols, which="evac"):
    slots = sc.setdefault(which + "_state", {"k": 0, "sems": None})
    if slots["sems"] is None:
        slots["sems"] = [ctx.sem(f"sev_{which}_{i}") for i in range(4)]
    k = slots["k"] % 4
    slots["k"] += 1
    sem = slots["sems"][k]
    buf = sc[which]
    if which == "evac":
        view = buf[0:rows, k * P: k * P + cols]
    else:
        view = buf[0:rows, k * 64: k * 64 + cols]
    sem.wait(ctx.act)
    ctx.s_pe.wait(ctx.act)
    ctx.s_a.inc(ctx.act.activation(out=view, in_=ps_slice, func=AF.Copy))
    sc["pg"][0] = ctx.s_a.n
    ctx.s_a.wait(ctx.sp)
    sem.inc(ctx.sp.dma_start(out=dst_ap, in_=view), 16)
    smid.inc_count_only = None

def emit_fold(ctx, consts, aps, sc):
    nc = ctx.nc
    th = sc["th"]
    sW = ctx.sem("ffold")

    def build_UV(w_ap, transposed, Udst, Vdst):
        sc["s_uv"].wait(ctx.dve)
        ctx.s_pe.wait(ctx.dve)
        ctx.s_v.wait(ctx.dve)
        ctx.s_v.inc(ctx.dve.tensor_copy(Udst[:, :], sc["u0"][:, :]))
        ctx.s_v.inc(ctx.dve.tensor_copy(Vdst[:, :], sc["v0"][:, :]))
        stages_u = list(range(6))
        stages_v = list(range(6, 12))
        if transposed:
            stages_u, stages_v = stages_u[::-1], stages_v[::-1]
        for tens, stages, within in ((Udst, stages_u, True), (Vdst, stages_v, False)):
            for s in stages:
                rowb, Wb, psW = sc["row"], sc["wb"], sc["psW"]
                sW.wait(ctx.pool, sW.n)
                ctx.s_pe.wait(ctx.pool)
                sW.inc(ctx.pool.dma_start(
                    out=rowb[:, :],
                    in_=w_ap[s].rearrange("a b c -> (a b c)")
                    .rearrange("(one f) -> one f", one=1)), 16)
                sW.wait(ctx.pe)
                if sc["wbg"][0]:
                    ctx.s_v.wait(ctx.pe, sc["wbg"][0])
                for i in range(16):
                    if sc["psWg"][0]:
                        ctx.s_a.wait(ctx.pe, sc["psWg"][0])
                    ctx.s_pe.inc(ctx.pe.matmul(psW[0:64, :], consts["ones_row"][:, 0:64],
                                               rowb[0:1, i * 512:(i + 1) * 512],
                                               start=True, stop=True))
                    ctx.s_pe.wait(ctx.act)
                    ctx.s_a.inc(ctx.act.activation(out=Wb[:, i * 512:(i + 1) * 512],
                                                   in_=psW[0:64, :], func=AF.Copy))
                    sc["psWg"][0] = ctx.s_a.n
                ctx.s_a.wait(ctx.dve)
                stx = 2 ** s
                if within:
                    nb_, t_ = 32 // stx, stx
                    fdims = [[64, 64], [2 * stx, nb_], [1, t_]]
                    b_off = stx
                    wdims = [[4 * 32, 64], [4 * stx, nb_], [4, t_]]
                else:
                    stb = stx // 64
                    nb_, t_ = 32 // stb, stb
                    fdims = [[64, 64], [2 * stb, nb_], [1, t_]]
                    b_off = stb
                    wdims = [[4, 64], [4 * 64 * stb, nb_], [4 * 64, t_]]
                w00o, w01o, w10o, w11o = 0, 1, 2, 3
                if transposed:
                    w01o, w10o = 2, 1
                PA = [[FN_DH, 64]]
                PW = [[4 * 2048, 64]]
                Ua = bass.AP(tens, 0, PA + fdims)
                Ub = bass.AP(tens, b_off, PA + fdims)
                Ta = bass.AP(sc["tmp1"], 0, PA + fdims)
                Tb = bass.AP(sc["tmp1"], b_off, PA + fdims)
                T2a = bass.AP(sc["tmp2"], 0, PA + fdims)
                T2b = bass.AP(sc["tmp2"], b_off, PA + fdims)
                W00 = bass.AP(Wb, w00o, PW + wdims)
                W01 = bass.AP(Wb, w01o, PW + wdims)
                W10 = bass.AP(Wb, w10o, PW + wdims)
                W11 = bass.AP(Wb, w11o, PW + wdims)
                for (o, i0, i1) in ((Ta, W00, Ua), (Tb, W01, Ub),
                                    (T2a, W11, Ub), (T2b, W10, Ua)):
                    ctx.s_v.wait(ctx.dve)
                    ctx.s_v.inc(ctx.dve.tensor_tensor(out=o, in0=i0, in1=i1, op=ALU.mult))
                ctx.s_v.wait(ctx.dve)
                ctx.s_v.inc(ctx.dve.tensor_tensor(out=Ua, in0=Ta, in1=Tb, op=ALU.add))
                ctx.s_v.wait(ctx.dve)
                ctx.s_v.inc(ctx.dve.tensor_tensor(out=Ub, in0=T2b, in1=T2a, op=ALU.add))
                sc["wbg"][0] = ctx.s_v.n

    build_UV(aps["up_w"], False, sc["U"], sc["V"])
    stg = sc["stg"]
    sL = ctx.sem("ffld2")
    wexpT_mc = sc["wexpT_mc"]
    ctx.s_v.wait(ctx.pe)
    smid = ctx.sem("fmid")
    for mc in range(KC):
        ctx.s_pe.wait(ctx.pool)
        sL.inc(ctx.pool.dma_start(out=stg[:, :], in_=aps["W_exp"][mc * P:(mc + 1) * P, :]), 16)
        sL.wait(ctx.pe)
        for j in range(64):
            th.transpose(stg[:, j * 64:(j + 1) * 64],
                         wexpT_mc[:, j * P:(j + 1) * P], rows=P, cols=64)
        ctx.s_a.wait(ctx.pe)
        for j in range(64):
            if sc["pg"][0]:
                ctx.s_a.wait(ctx.pe, sc["pg"][0])
            ctx.s_pe.inc(ctx.pe.matmul(
                sc["psF"][0:64, 0:P],
                sc["U"][:, j * 64:(j + 1) * 64],
                wexpT_mc[:, j * P:(j + 1) * P],
                start=True, stop=True))
            evac_store(ctx, sc, smid, sc["psF"][0:64, 0:P],
                       aps["wmid"][j * 64:(j + 1) * 64, mc * P:(mc + 1) * P], 64, P)
    for j in range(64):
        if sc["pg"][0]:
            ctx.s_a.wait(ctx.pe, sc["pg"][0])
        ctx.s_pe.inc(ctx.pe.matmul(
            sc["psF"][0:64, 0:1],
            sc["U"][:, j * 64:(j + 1) * 64],
            sc["bcolT"][0:64, j:j + 1],
            start=True, stop=True))
        evac_store(ctx, sc, smid, sc["psF"][0:64, 0:1],
                   aps["bmid"][j * 64:(j + 1) * 64, 0:1], 64, 1)
    sB = ctx.sem("fB")
    for r in range(64):
        wait_evac_done(ctx, sc, ctx.pool)
        ctx.s_a.wait(ctx.pool)
        ctx.s_pe.wait(ctx.pool)
        sB.inc(ctx.pool.dma_start(
            out=sc["lhsW"][:, 0:D],
            in_=aps["wmid"].rearrange("(q g) m -> q g m", g=64)[:, r, :]), 16)
        sB.inc(ctx.pool.dma_start(
            out=sc["lhsB"][0:64, 0:1],
            in_=aps["bmid"].rearrange("(q g) one -> q g one", g=64)[:, r, :],
            allow_slow_non_contiguous=True), 16)
        sB.wait(ctx.pe)
        for mc in range(KC):
            if sc["pg"][0]:
                ctx.s_a.wait(ctx.pe, sc["pg"][0])
            ctx.s_pe.inc(ctx.pe.matmul(
                sc["psF"][0:P, 0:64],
                sc["lhsW"][0:64, mc * P:(mc + 1) * P],
                sc["V"][:, r * 64:(r + 1) * 64],
                start=True, stop=True))
            evac_store(ctx, sc, smid, sc["psF"][0:P, 0:64],
                       aps["w1eff"][mc * P:(mc + 1) * P, r * 64:(r + 1) * 64],
                       P, 64, which="evacB")
        if sc["pg"][0]:
            ctx.s_a.wait(ctx.pe, sc["pg"][0])
        ctx.s_pe.inc(ctx.pe.matmul(
            sc["psF"][0:1, 0:64], sc["lhsB"][0:64, 0:1],
            sc["V"][:, r * 64:(r + 1) * 64], start=True, stop=True))
        evac_store(ctx, sc, smid, sc["psF"][0:1, 0:64],
                   aps["b1eff"][r:r + 1, 0:64], 1, 64, which="evacB")

    build_UV(aps["down_w"], True, sc["U"], sc["V"])
    sD = ctx.sem("fD")
    ctx.s_v.wait(ctx.pe)
    for r in range(64):
        ctx.s_pe.wait(ctx.pool)
        wait_evac_done(ctx, sc, ctx.pool)
        sD.inc(ctx.pool.dma_start(
            out=sc["lhsW"][:, 0:D],
            in_=aps["W_con"].rearrange("(q g) m -> q g m", g=64)[:, r, :]), 16)
        sD.wait(ctx.pe)
        for mc in range(KC):
            if sc["pg"][0]:
                ctx.s_a.wait(ctx.pe, sc["pg"][0])
            ctx.s_pe.inc(ctx.pe.matmul(
                sc["psF"][0:64, 0:P],
                sc["V"][:, r * 64:(r + 1) * 64],
                sc["lhsW"][0:64, mc * P:(mc + 1) * P],
                start=True, stop=True))
            evac_store(ctx, sc, smid, sc["psF"][0:64, 0:P],
                       aps["wmid"][r * 64:(r + 1) * 64, mc * P:(mc + 1) * P], 64, P)
    for j in range(64):
        ctx.s_pe.wait(ctx.pool)
        wait_evac_done(ctx, sc, ctx.pool)
        sD.inc(ctx.pool.dma_start(
            out=sc["lhsW"][:, 0:D],
            in_=aps["wmid"].rearrange("(q g) m -> q g m", g=64)[:, j, :]), 16)
        sD.wait(ctx.pe)
        for mc in range(KC):
            if sc["pg"][0]:
                ctx.s_a.wait(ctx.pe, sc["pg"][0])
            ctx.s_pe.inc(ctx.pe.matmul(
                sc["psF"][0:64, 0:P],
                sc["U"][:, j * 64:(j + 1) * 64],
                sc["lhsW"][0:64, mc * P:(mc + 1) * P],
                start=True, stop=True))
            evac_store(ctx, sc, smid, sc["psF"][0:64, 0:P],
                       aps["w2eff"][j * 64:(j + 1) * 64, mc * P:(mc + 1) * P], 64, P)
    for which in ("evac_state", "evacB_state"):
        if which in sc:
            for s in sc[which]["sems"]:
                s.wait(ctx.sp)
    sB.wait(ctx.sp)
    sD.wait(ctx.sp)
    return


def emit_ffn_apply(ctx, consts, aps, n_chunks, rin_key, rout_key, tagp):
    nc = ctx.nc
    for half in range(2):
        with contextlib.ExitStack() as es2:
            c2 = Ctx.__new__(Ctx)
            c2.__dict__ = dict(ctx.__dict__)
            c2.es = es2
            tag = f"{tagp}{half}"
            nrm = alloc_nrm(c2, tag)
            ln = alloc_ln_bufs(c2, tag, nrm, nslx=1, nslxn=1)
            sW = ctx.sem(f"fW_{tag}")
            w1 = c2.sbuf(f"f_w1_{tag}", [P, KC * FN_HCH * P], BF16)
            w2 = c2.sbuf(f"f_w2_{tag}", [P, FN_HCH * KC * P], BF16)
            for kc in range(KC):
                sW.inc(c2.pool.dma_start(
                    out=w1[:, kc * FN_HCH * P:(kc + 1) * FN_HCH * P],
                    in_=aps["w1eff"][kc * P:(kc + 1) * P,
                                     half * FN_HH:(half + 1) * FN_HH]), 16)
            for hc in range(FN_HCH):
                r0 = (half * FN_HH + hc * P) // 64
                for g in range(2):
                    sW.inc(c2.pool.dma_start(
                        out=w2[:, hc * KC * P:(hc + 1) * KC * P][g * 64:(g + 1) * 64, :],
                        in_=aps["w2eff"].rearrange("(j g) m -> j g m", g=64)
                        [:, r0 + g, :]), 16)
            sBl = ctx.sem(f"fb_{tag}")
            b1c = c2.sbuf(f"f_b1c_{tag}", [P, FN_HCH], F32)
            ub = c2.sbuf(f"f_ub_{tag}", [P, FN_HCH], F32)
            b1r = c2.sbuf(f"f_b1r_{tag}", [P, FN_HCH], F32)
            for hc in range(FN_HCH):
                pr0 = half * FN_HH + hc * P
                sBl.inc(c2.pool.dma_start(
                    out=b1c[:, hc:hc + 1],
                    in_=aps["b1eff"].rearrange("a b -> (a b)")[pr0:pr0 + P]
                    .rearrange("(p one) -> p one", one=1)), 16)
                for g in range(2):
                    r_ = pr0 // 64 + g
                    sBl.inc(c2.pool.dma_start(
                        out=ub[g * 64:(g + 1) * 64, hc:hc + 1],
                        in_=aps["up_bias"].rearrange("(j r) -> j r", r=64)[:, r_]
                        .rearrange("(p one) -> p one", one=1),
                        allow_slow_non_contiguous=True), 16)
            sBl.wait(c2.dve)
            c2.s_v.wait(c2.dve)
            c2.s_v.inc(c2.dve.tensor_tensor(out=b1r[:, :], in0=b1c[:, :], in1=ub[:, :],
                                            op=ALU.add))
            b1_ready = c2.s_v.n
            b2c = c2.sbuf(f"f_b2c_{tag}", [P, KC], F32)
            if half == 1:
                sB2 = ctx.sem(f"fb2_{tag}")
                dcol = c2.sbuf(f"f_dc_{tag}", [P, KC], F32)
                for j in range(KC):
                    sB2.inc(c2.pool.dma_start(
                        out=b2c[:, j:j + 1],
                        in_=aps["b_con"][j * P:(j + 1) * P]
                        .rearrange("(p one) -> p one", one=1)), 16)
                    sB2.inc(c2.pool.dma_start(
                        out=dcol[:, j:j + 1],
                        in_=aps["down_bias"][j * P:(j + 1) * P]
                        .rearrange("(p one) -> p one", one=1)), 16)
                sB2.wait(c2.dve)
                c2.s_v.wait(c2.dve)
                c2.s_v.inc(c2.dve.tensor_tensor(out=b2c[:, :], in0=b2c[:, :],
                                                in1=dcol[:, :], op=ALU.add))
            b2_ready = c2.s_v.n
            psA = c2.psum(f"f_psA_{tag}", [P, TCH])
            psO = c2.psum(f"f_psO_{tag}", [P, TCH])
            HHb = c2.sbuf(f"f_HH_{tag}", [P, FN_HCH * TCH], BF16)
            XM = c2.sbuf(f"f_XM_{tag}", [P, KC * TCH], F32)
            sxm = [ctx.sem(f"fxm_{tag}")]
            sst = [ctx.sem(f"fsa_{tag}"), ctx.sem(f"fsb_{tag}")]
            bankg = [0]
            og = [0]
            ogv = [0]
            sW.wait(c2.pe)
            c2.s_v.wait(c2.pe, b2_ready)
            src_r = aps[rin_key]
            acc_r = aps[rin_key] if half == 0 else aps["rmid2" if tagp == "B" else "rmid"]
            dst_r = aps["rmid2" if tagp == "B" else "rmid"] if half == 0 else aps[rout_key]
            for b in range(NB):
                for cc in range(n_chunks):
                    par = cc % 2
                    emit_ln_chunk(c2, consts, src_r[b], cc, ln,
                                  load_guard=[(sst[0], sst[0].n), (sst[1], sst[1].n)])
                    xn_ready = c2.s_v.n
                    if half == 0:
                        XMsrc = ln["X"]
                    else:
                        XMsrc = XM
                        sxm[0].wait(c2.sp, sxm[0].n)
                        sst[0].wait(c2.sp)
                        sst[1].wait(c2.sp)
                        for j in range(KC):
                            sxm[0].inc(c2.sp.dma_start(
                                out=XM[:, j * TCH:(j + 1) * TCH],
                                in_=acc_r[b][j * P:(j + 1) * P, cc * TCH:(cc + 1) * TCH]), 16)

                    def YNs(kc):
                        return ln["XN"][:, kc * TCH:(kc + 1) * TCH]

                    c2.s_v.wait(c2.pe, xn_ready)
                    for hc in range(FN_HCH):
                        if bankg[0]:
                            c2.s_a.wait(c2.pe, bankg[0])
                        for kc in range(KC):
                            c2.s_pe.inc(c2.pe.matmul(
                                psA[:, :], w1[:, (kc * FN_HCH + hc) * P:(kc * FN_HCH + hc + 1) * P],
                                YNs(kc), start=(kc == 0), stop=(kc == KC - 1)))
                        c2.s_pe.wait(c2.act)
                        c2.s_a.wait(c2.act)
                        if og[0]:
                            c2.s_pe.wait(c2.act, og[0])
                        c2.s_v.wait(c2.act, b1_ready)
                        c2.s_a.inc(c2.act.activation(
                            out=HHb[:, hc * TCH:(hc + 1) * TCH], in_=psA[:, :],
                            func=AF.Gelu, bias=b1r[:, hc:hc + 1], scale=1.0))
                        bankg[0] = c2.s_a.n
                    h_done = c2.s_a.n
                    ln["xn_guard"][0] = c2.s_pe.n
                    c2.s_a.wait(c2.pe, h_done)
                    if half == 1:
                        sxm[0].wait(c2.dve)
                    for oc in range(KC):
                        if ogv[0]:
                            c2.s_v.wait(c2.pe, ogv[0])
                        for hc in range(FN_HCH):
                            c2.s_pe.inc(c2.pe.matmul(
                                psO[:, :], w2[:, (hc * KC + oc) * P:(hc * KC + oc + 1) * P],
                                HHb[:, hc * TCH:(hc + 1) * TCH],
                                start=(hc == 0), stop=(hc == FN_HCH - 1)))
                        c2.s_pe.wait(c2.dve)
                        osl = slice(oc * TCH, (oc + 1) * TCH)
                        if half == 1:
                            c2.s_v.inc(c2.dve.scalar_tensor_tensor(
                                out=XMsrc[:, osl], in0=psO[:, :],
                                scalar=b2c[:, oc:oc + 1], in1=XMsrc[:, osl],
                                op0=ALU.add, op1=ALU.add))
                        else:
                            c2.s_v.inc(c2.dve.tensor_tensor(
                                out=XMsrc[:, osl], in0=psO[:, :],
                                in1=XMsrc[:, osl], op=ALU.add))
                        ogv[0] = c2.s_v.n
                        c2.s_v.wait(c2.sp)
                        sst[par].inc(c2.sp.dma_start(
                            out=dst_r[b][oc * P:(oc + 1) * P, cc * TCH:(cc + 1) * TCH],
                            in_=XMsrc[:, osl]), 16)
                    og[0] = c2.s_pe.n
            for s in sst:
                s.wait(c2.sp)
            for eng in (c2.pe, c2.dve, c2.act, c2.sp, c2.pool):
                c2.s_pe.wait(eng)
                c2.s_v.wait(eng)
                c2.s_a.wait(eng)
                for s in sst + sxm:
                    s.wait(eng)


def emit_l0ffn_fold_phase(ctx, consts, aps):
    with contextlib.ExitStack() as esf:
        cf = Ctx.__new__(Ctx)
        cf.__dict__ = dict(ctx.__dict__)
        cf.es = esf
        sc = {
            "th": TransposeHelper(cf, consts),
            "u0": cf.sbuf("f_u0", [64, FN_DH], BF16),
            "v0": cf.sbuf("f_v0", [64, FN_DH], BF16),
            "U": cf.sbuf("f_U", [64, FN_DH], BF16),
            "V": cf.sbuf("f_V", [64, FN_DH], BF16),
            "tmp1": cf.sbuf("f_t1", [64, FN_DH], BF16),
            "tmp2": cf.sbuf("f_t2", [64, FN_DH], BF16),
            "row": cf.sbuf("f_row", [1, 8192], BF16),
            "wb": cf.sbuf("f_wb", [64, 8192], BF16),
            "psW": cf.psum("f_psW", [64, 512]),
            "psF": cf.psum("f_psF", [P, 512]),
            "stg": cf.sbuf("f_stg", [P, FN_DH], BF16),
            "wexpT_mc": cf.sbuf("f_wexpTmc", [64, 64 * P], BF16),
            "bcolT": cf.sbuf("f_bcolT", [64, 64], BF16),
            "evac": cf.sbuf("f_evac", [64, 4 * P], BF16),
            "evacB": cf.sbuf("f_evacB", [P, 4 * 64], BF16),
            "lhsW": cf.sbuf("f_lhsW", [64, D], BF16),
            "lhsB": cf.sbuf("f_lhsB", [64, 1], BF16),
            "wbg": [0],
            "pg": [0],
            "psWg": [0],
        }
        s_uv = ctx.sem("fuv")
        s_uv.inc(cf.sp.dma_start(out=sc["u0"][:, :], in_=aps["u0"][:, :]), 16)
        s_uv.inc(cf.sp.dma_start(out=sc["v0"][:, :], in_=aps["v0"][:, :]), 16)
        sc["s_uv"] = s_uv
        sbc = ctx.sem("fbc")
        for j in range(64):
            sbc.inc(cf.pool.dma_start(
                out=sc["bcolT"][:, j:j + 1],
                in_=aps["b_exp"][j * 64:(j + 1) * 64].rearrange("(p one) -> p one", one=1)), 16)
        sbc.wait(cf.pe)
        emit_fold(cf, consts, aps, sc)
        for eng in (cf.pe, cf.dve, cf.act, cf.sp, cf.pool):
            cf.s_pe.wait(eng)
            cf.s_v.wait(eng)
            cf.s_a.wait(eng)
            wait_evac_done(cf, sc, eng)


def build_l0ffn(n_chunks=8):
    nc = bass.Bass("TRN2", target_bir_lowering=False)
    seq = n_chunks * TCH
    aps = {"rin": nc.dram_tensor("rin", [NB, D, seq], F32, kind="ExternalInput").ap()}
    aps["W_exp"] = nc.dram_tensor("W_exp", [D, FN_DH], F32, kind="ExternalInput").ap()
    aps["b_exp"] = nc.dram_tensor("b_exp", [FN_DH], F32, kind="ExternalInput").ap()
    aps["up_w"] = nc.dram_tensor("up_w", [12, 2048, 2, 2], F32, kind="ExternalInput").ap()
    aps["up_bias"] = nc.dram_tensor("up_bias", [FN_DH], F32, kind="ExternalInput").ap()
    aps["down_w"] = nc.dram_tensor("down_w", [12, 2048, 2, 2], F32, kind="ExternalInput").ap()
    aps["W_con"] = nc.dram_tensor("W_con", [FN_DH, D], F32, kind="ExternalInput").ap()
    aps["b_con"] = nc.dram_tensor("b_con", [D], F32, kind="ExternalInput").ap()
    aps["down_bias"] = nc.dram_tensor("down_bias", [D], F32, kind="ExternalInput").ap()
    aps["rout"] = nc.dram_tensor("rout", [NB, D, seq], F32, kind="ExternalOutput").ap()
    aps["rmid"] = nc.dram_tensor("rmid", [NB, D, seq], F32, kind="Internal").ap()
    aps["wmid"] = nc.dram_tensor("wmid", [FN_DH, D], BF16, kind="Internal").ap()
    aps["bmid"] = nc.dram_tensor("bmid", [FN_DH, 1], BF16, kind="Internal").ap()
    aps["w1eff"] = nc.dram_tensor("w1eff", [D, FN_DH], BF16, kind="Internal").ap()
    aps["b1eff"] = nc.dram_tensor("b1eff", [64, 64], BF16, kind="Internal").ap()
    aps["w2eff"] = nc.dram_tensor("w2eff", [FN_DH, D], BF16, kind="Internal").ap()
    caps = declare_consts(nc)
    caps.update(declare_ffn_consts(nc))
    aps["u0"], aps["v0"] = caps["u0"], caps["v0"]
    es = contextlib.ExitStack()
    with es:
        ctx = Ctx(nc, es)
        consts = load_consts(ctx, caps)
        emit_l0ffn_fold_phase(ctx, consts, aps)
        emit_ffn_apply(ctx, consts, aps, n_chunks, "rin", "rout", "A")
    return nc


N4 = 4096
E3 = 3072


def fft_consts():
    t1 = np.arange(64)
    k1 = np.arange(64)
    ang = 2 * np.pi * np.outer(t1, k1) / 64.0
    cs1 = np.concatenate([np.cos(ang), -np.sin(ang)], axis=1)
    T2 = np.zeros((64, 128, 128), np.float64)
    IA = np.zeros((64, 128, 128), np.float64)
    t2 = np.arange(64)
    k2 = np.arange(64)
    for a1 in range(64):
        th = 2 * np.pi * (np.outer(t2, np.full(64, a1)) / 4096.0 +
                          np.outer(t2, k2) / 64.0)
        wre, wim = np.cos(th), -np.sin(th)
        T2[a1, 0:64, 0:64] = wre
        T2[a1, 0:64, 64:128] = wim
        T2[a1, 64:128, 0:64] = -wim
        T2[a1, 64:128, 64:128] = wre
        b = np.arange(64)
        th2 = 2 * np.pi * (np.outer(k2, np.full(64, a1)) * 0 +
                           np.outer(k2, b) / 64.0)
        th2 = th2 + 2 * np.pi * a1 * b[None, :] / 4096.0
        wre2, wim2 = np.cos(th2) / 4096.0, np.sin(th2) / 4096.0
        IA[a1, 0:64, 0:64] = wre2
        IA[a1, 0:64, 64:128] = wim2
        IA[a1, 64:128, 0:64] = -wim2
        IA[a1, 64:128, 64:128] = wre2
    a = np.arange(64)
    angb = 2 * np.pi * np.outer(k1, a) / 64.0
    pb = np.concatenate([np.cos(angb), -np.sin(angb)], axis=0)
    return {
        "c_cs1": cs1.astype(ml_dtypes.bfloat16),
        "c_t2m": T2.reshape(64 * 128, 128).astype(ml_dtypes.bfloat16),
        "c_iam": IA.reshape(64 * 128, 128).astype(ml_dtypes.bfloat16),
        "c_pbm": pb.astype(ml_dtypes.bfloat16),
    }


def declare_fft_consts(nc):
    return {
        "cs1": nc.dram_tensor("c_cs1", [64, 128], BF16, kind="ExternalInput").ap(),
        "t2m": nc.dram_tensor("c_t2m", [64 * 128, 128], BF16, kind="ExternalInput").ap(),
        "iam": nc.dram_tensor("c_iam", [64 * 128, 128], BF16, kind="ExternalInput").ap(),
        "pbm": nc.dram_tensor("c_pbm", [128, 64], BF16, kind="ExternalInput").ap(),
    }


def emit_l0ab(ctx, consts, aps, n_chunks):
    nc = ctx.nc
    assert n_chunks == 8
    with contextlib.ExitStack() as e1:
        c1 = Ctx.__new__(Ctx)
        c1.__dict__ = dict(ctx.__dict__)
        c1.es = e1
        nrm = alloc_nrm(c1, "ab0")
        ln = alloc_ln_bufs(c1, "ab0", nrm, nslx=1, nslxn=1)
        sW = ctx.sem("pj_w")
        pw = c1.sbuf("pj_pw", [P, KC * E3], BF16)
        for kc in range(KC):
            sW.inc(c1.pool.dma_start(out=pw[:, kc * E3:(kc + 1) * E3],
                                     in_=aps["proj_w"][kc * P:(kc + 1) * P, :]), 16)
        pbrow = c1.sbuf("pj_pb", [1, E3], BF16)
        sW.inc(c1.pool.dma_start(out=pbrow[:, :],
                                 in_=aps["proj_b"].rearrange("(one f) -> one f", one=1)), 16)
        psA = c1.psum("pj_psA", [P, TCH])
        psB_ = c1.psum("pj_psB", [P, TCH])
        EV = c1.sbuf("pj_ev", [P, 2 * TCH], BF16)
        sst = [ctx.sem("pj_s0"), ctx.sem("pj_s1")]
        sev = [ctx.sem("pj_e0"), ctx.sem("pj_e1")]
        bankg = {0: 0, 1: 0}
        sW.wait(c1.pe)
        for b in range(NB):
            for cc in range(8):
                emit_ln_chunk(c1, consts, aps["rin"][b], cc, ln,
                              load_guard=[(sev[0], sev[0].n), (sev[1], sev[1].n)])
                xn_ready = c1.s_v.n
                c1.s_v.wait(c1.pe, xn_ready)
                for tt in range(4):
                    for ec in range(6):
                        k = (tt * 6 + ec) % 2
                        bank = [psA, psB_][k]
                        if bankg[k]:
                            c1.s_a.wait(c1.pe, bankg[k])
                        for kc in range(KC):
                            c1.s_pe.inc(c1.pe.matmul(
                                bank[:, :],
                                ln["XN"][:, kc * TCH + tt * P: kc * TCH + (tt + 1) * P],
                                pw[:, kc * E3 + ec * TCH: kc * E3 + (ec + 1) * TCH],
                                start=(kc == 0), stop=False))
                        c1.s_pe.inc(c1.pe.matmul(
                            bank[:, :], consts["ones_row"][:, :],
                            pbrow[0:1, ec * TCH:(ec + 1) * TCH],
                            start=False, stop=True))
                        c1.s_pe.wait(c1.act)
                        sev[k].wait(c1.act, sev[k].n)
                        c1.s_a.inc(c1.act.activation(
                            out=EV[:, k * TCH:(k + 1) * TCH], in_=bank[:, :],
                            func=AF.Copy))
                        bankg[k] = c1.s_a.n
                        c1.s_a.wait(c1.sp)
                        sev[k].inc(c1.sp.dma_start(
                            out=aps["p_h"][b][cc * TCH + tt * P: cc * TCH + (tt + 1) * P,
                                              ec * TCH:(ec + 1) * TCH],
                            in_=EV[:, k * TCH:(k + 1) * TCH]), 16)
                ln["xn_guard"][0] = c1.s_pe.n
        for eng in (c1.pe, c1.dve, c1.act, c1.sp, c1.pool):
            c1.s_pe.wait(eng); c1.s_v.wait(eng); c1.s_a.wait(eng)
            for s in sev:
                s.wait(eng)

    with contextlib.ExitStack() as e2:
        c2 = Ctx.__new__(Ctx)
        c2.__dict__ = dict(ctx.__dict__)
        c2.es = e2
        sC = ctx.sem("fft_c")
        cs1 = c2.sbuf("fft_cs1", [64, 128], BF16)
        t2m = c2.sbuf("fft_t2m", [P, 64 * 128], BF16)
        iam = c2.sbuf("fft_iam", [P, 64 * 128], BF16)
        pbm = c2.sbuf("fft_pbm", [P, 64], BF16)
        sC.inc(c2.sp.dma_start(out=cs1[:, :], in_=aps["cs1"][:, :]), 16)
        for a_ in range(64):
            sC.inc(c2.sp.dma_start(out=t2m[:, a_ * 128:(a_ + 1) * 128],
                                   in_=aps["t2m"][a_ * 128:(a_ + 1) * 128, :]), 16)
            sC.inc(c2.sp.dma_start(out=iam[:, a_ * 128:(a_ + 1) * 128],
                                   in_=aps["iam"][a_ * 128:(a_ + 1) * 128, :]), 16)
        sC.inc(c2.sp.dma_start(out=pbm[:, :], in_=aps["pbm"][:, :]), 16)
        sC.wait(c2.pe)
        rhs1 = c2.sbuf("fft_r1", [64, 2 * D], BF16)
        rhs2 = c2.sbuf("fft_r2", [P, 2 * D], BF16)
        Xsb = c2.sbuf("fft_X", [64, 2 * D], BF16)
        Fk = c2.sbuf("fft_F", [64, 2 * D], BF16)
        Ysb = c2.sbuf("fft_Y", [P, D], BF16)
        tmpa = c2.sbuf("fft_ta", [64, D], BF16)
        tmpb = c2.sbuf("fft_tb", [64, D], BF16)
        gate = c2.sbuf("fft_gt", [64, D], BF16)
        vout = c2.sbuf("fft_vo", [64, 2 * D], BF16)
        evb = c2.sbuf("fft_ev", [P, 2 * D], BF16)
        fstg = c2.sbuf("fft_fs", [P, D], BF16)
        ps1 = c2.psum("fft_ps1", [P, TCH])
        ps2 = c2.psum("fft_ps2", [P, TCH])
        s_r = [ctx.sem("fft_ra"), ctx.sem("fft_rb")]
        s_f = ctx.sem("fft_fk")
        s_g = ctx.sem("fft_gts")
        s_st = [ctx.sem("fft_sa"), ctx.sem("fft_sb")]
        s_vo = [ctx.sem("fft_va"), ctx.sem("fft_vb")]
        s_fs = ctx.sem("fft_fsm")
        evg = {0: 0, 1: 0}
        bg = {0: 0, 1: 0}
        rg = {0: 0, 1: 0}

        s_fc = ctx.sem("fft_fc")
        for sg_ in range(2):
            for i in range(16):
                s_fc.wait(c2.pool, s_fc.n)
                s_fc.inc(c2.pool.dma_start(
                    out=fstg[:, :],
                    in_=aps["filt"][sg_, 0, i * P:(i + 1) * P, :]), 16)
                s_fc.wait(c2.pool)
                s_fc.inc(c2.pool.dma_start(
                    out=aps["filtb"][sg_, i * P:(i + 1) * P, :],
                    in_=fstg[:, :]), 16)
        s_fc.wait(c2.sp)

        def sp_sync():
            for ss in (s_st[0], s_st[1], s_vo[0], s_vo[1], s_r[0], s_r[1], s_f, s_fs,
                       s_g):
                ss.wait(c2.sp)

        def run_fft_fwd(v_srcs, Fh, filt=False, dstF=None):
            sp_sync()
            k_rows = 32 if filt else 64
            for t2 in range(64):
                par = t2 % 2
                c2.s_pe.wait(c2.sp, rg[par])
                s_r[par].inc(c2.sp.dma_start(
                    out=rhs1[0:k_rows, par * D: (par + 1) * D], in_=v_srcs[t2]), 16)
                s_r[par].wait(c2.pe)
                for cn in range(2):
                    kk = cn
                    if bg[kk]:
                        c2.s_a.wait(c2.pe, bg[kk])
                    bank = [ps1, ps2][kk]
                    c2.s_pe.inc(c2.pe.matmul(
                        bank[:, :], cs1[0:k_rows, :],
                        rhs1[0:k_rows, par * D + cn * TCH: par * D + (cn + 1) * TCH],
                        start=True, stop=True))
                    c2.s_pe.wait(c2.act)
                    if evg[cn]:
                        s_st[cn].wait(c2.act, evg[cn])
                    c2.s_a.inc(c2.act.activation(
                        out=evb[:, cn * D + cn * 0: cn * D + TCH][:, 0:TCH]
                        if False else evb[:, cn * D: cn * D + TCH],
                        in_=bank[:, :], func=AF.Copy))
                    bg[kk] = c2.s_a.n
                    c2.s_a.wait(c2.sp)
                    s_st[cn].inc(c2.sp.dma_start(
                        out=aps["A_h"][t2, :, cn * TCH:(cn + 1) * TCH],
                        in_=evb[:, cn * D: cn * D + TCH]), 16)
                    evg[cn] = s_st[cn].n
                rg[par] = c2.s_pe.n
            sp_sync()
            for k1 in range(64):
                par = k1 % 2
                c2.s_pe.wait(c2.sp, rg[par])
                s_r[par].inc(c2.sp.dma_start(out=rhs2[0:64, par * D:(par + 1) * D],
                                             in_=aps["A_h"][:, k1, :]), 16)
                s_r[par].inc(c2.sp.dma_start(out=rhs2[64:128, par * D:(par + 1) * D],
                                             in_=aps["A_h"][:, 64 + k1, :]), 16)
                s_r[par].wait(c2.pe)
                for cn in range(2):
                    kk = cn
                    if bg[kk]:
                        c2.s_a.wait(c2.pe, bg[kk])
                    bank = [ps1, ps2][kk]
                    c2.s_pe.inc(c2.pe.matmul(
                        bank[:, :], t2m[:, k1 * 128:(k1 + 1) * 128],
                        rhs2[:, par * D + cn * TCH: par * D + (cn + 1) * TCH],
                        start=True, stop=True))
                    c2.s_pe.wait(c2.act)
                    c2.s_v.wait(c2.act)
                    if filt:
                        s_fs.wait(c2.act, s_fs.n)
                    c2.s_a.inc(c2.act.activation(out=Xsb[:, cn * TCH:(cn + 1) * TCH],
                                                 in_=bank[0:64, :], func=AF.Copy))
                    c2.s_a.inc(c2.act.activation(out=Xsb[:, D + cn * TCH: D + (cn + 1) * TCH],
                                                 in_=bank[64:128, :], func=AF.Copy))
                    bg[kk] = c2.s_a.n
                rg[par] = c2.s_pe.n
                if filt:
                    c2.s_a.wait(c2.sp)
                    s_fs.inc(c2.sp.dma_start(out=dstF[k1][0:64, :],
                                             in_=Xsb[:, 0:D]), 16)
                    s_fs.inc(c2.sp.dma_start(out=dstF[k1][64:128, :],
                                             in_=Xsb[:, D:2 * D]), 16)
                    continue
                s_f.wait(c2.sp, s_f.n)
                c2.s_v.wait(c2.sp)
                s_f.inc(c2.sp.dma_start(out=Fk[:, 0:D], in_=Fh[k1][0:64, :]), 16)
                s_f.inc(c2.sp.dma_start(out=Fk[:, D:2 * D], in_=Fh[k1][64:128, :]), 16)
                s_f.wait(c2.dve)
                c2.s_a.wait(c2.dve)
                c2.s_v.wait(c2.dve)
                c2.s_v.inc(c2.dve.tensor_tensor(out=Ysb[0:64, :], in0=Xsb[:, 0:D],
                                                in1=Fk[:, 0:D], op=ALU.mult))
                c2.s_v.inc(c2.dve.tensor_tensor(out=tmpa[:, :], in0=Xsb[:, D:2 * D],
                                                in1=Fk[:, D:2 * D], op=ALU.mult))
                c2.s_v.wait(c2.dve)
                c2.s_v.inc(c2.dve.tensor_tensor(out=Ysb[0:64, :], in0=Ysb[0:64, :],
                                                in1=tmpa[:, :], op=ALU.subtract))
                c2.s_v.wait(c2.dve)
                c2.s_v.inc(c2.dve.tensor_tensor(out=tmpa[:, :], in0=Xsb[:, 0:D],
                                                in1=Fk[:, D:2 * D], op=ALU.mult))
                c2.s_v.inc(c2.dve.tensor_tensor(out=tmpb[:, :], in0=Xsb[:, D:2 * D],
                                                in1=Fk[:, 0:D], op=ALU.mult))
                c2.s_v.wait(c2.dve)
                c2.s_v.inc(c2.dve.tensor_tensor(out=tmpa[:, :], in0=tmpa[:, :],
                                                in1=tmpb[:, :], op=ALU.add))
                c2.s_v.wait(c2.act)
                c2.s_a.wait(c2.act)
                c2.s_a.inc(c2.act.activation(out=Ysb[64:128, :], in_=tmpa[:, :],
                                             func=AF.Copy))
                ya = c2.s_a.n
                yv = c2.s_v.n
                c2.s_v.wait(c2.pe, yv)
                c2.s_a.wait(c2.pe, ya)
                for cn in range(2):
                    kk = cn
                    if bg[kk]:
                        c2.s_a.wait(c2.pe, bg[kk])
                    bank = [ps1, ps2][kk]
                    c2.s_pe.inc(c2.pe.matmul(
                        bank[:, :], iam[:, k1 * 128:(k1 + 1) * 128],
                        Ysb[:, cn * TCH:(cn + 1) * TCH], start=True, stop=True))
                    c2.s_pe.wait(c2.act)
                    if evg[cn]:
                        s_st[cn].wait(c2.act, evg[cn])
                    c2.s_a.inc(c2.act.activation(
                        out=evb[:, cn * D: cn * D + TCH], in_=bank[:, :], func=AF.Copy))
                    bg[kk] = c2.s_a.n
                    c2.s_a.wait(c2.sp)
                    s_st[cn].inc(c2.sp.dma_start(
                        out=aps["G_h"][k1, :, cn * TCH:(cn + 1) * TCH],
                        in_=evb[:, cn * D: cn * D + TCH]), 16)
                    evg[cn] = s_st[cn].n

        def run_passB(p_rb, gate_col0, v_dst):
            sp_sync()
            for bb in range(64):
                par = bb % 2
                c2.s_pe.wait(c2.sp, rg[par])
                s_r[par].inc(c2.sp.dma_start(out=rhs2[0:64, par * D:(par + 1) * D],
                                             in_=aps["G_h"][:, bb, :]), 16)
                s_r[par].inc(c2.sp.dma_start(out=rhs2[64:128, par * D:(par + 1) * D],
                                             in_=aps["G_h"][:, 64 + bb, :]), 16)
                s_g.wait(c2.sp, s_g.n)
                c2.s_v.wait(c2.sp)
                s_g.inc(c2.sp.dma_start(
                    out=gate[:, :], in_=p_rb[bb, :, gate_col0:gate_col0 + D]), 16)
                s_r[par].wait(c2.pe)
                for cn in range(2):
                    kk = cn
                    if bg[kk]:
                        c2.s_a.wait(c2.pe, bg[kk])
                    bank = [ps1, ps2][kk]
                    c2.s_pe.inc(c2.pe.matmul(
                        bank[0:64, :], pbm[:, 0:64],
                        rhs2[:, par * D + cn * TCH: par * D + (cn + 1) * TCH],
                        start=True, stop=True))
                    c2.s_pe.wait(c2.act)
                    s_vo[par].wait(c2.act, s_vo[par].n)
                    c2.s_v.wait(c2.act)
                    c2.s_a.inc(c2.act.activation(
                        out=vout[:, par * D + cn * TCH: par * D + (cn + 1) * TCH],
                        in_=bank[0:64, :], func=AF.Copy))
                    bg[kk] = c2.s_a.n
                rg[par] = c2.s_pe.n
                s_g.wait(c2.dve)
                c2.s_a.wait(c2.dve)
                c2.s_v.wait(c2.dve)
                s_vo[par].wait(c2.dve, s_vo[par].n)
                c2.s_v.inc(c2.dve.tensor_tensor(
                    out=vout[:, par * D:(par + 1) * D],
                    in0=vout[:, par * D:(par + 1) * D], in1=gate[:, :], op=ALU.mult))
                c2.s_v.wait(c2.sp)
                s_vo[par].inc(c2.sp.dma_start(out=v_dst[bb],
                                              in_=vout[:, par * D:(par + 1) * D]), 16)

        for sg_ in range(2):
            f_r = aps["filtb"][sg_].rearrange("(t1 t2) c -> t2 t1 c", t2=64)
            run_fft_fwd([f_r[t2] for t2 in range(64)], None, filt=True,
                        dstF=[aps["F_h"][sg_, k1] for k1 in range(64)])

        th2 = TransposeHelper(c2, consts)
        sxr = ctx.sem("ab0_xr")
        ssr = ctx.sem("ab0_sr")
        sv2 = ctx.sem("ab0_v2")
        Xr = c2.sbuf("ab0_Xr", [P, TCH], F32)
        V2F = c2.sbuf("ab0_V2", [P, TCH], BF16)
        v2t = c2.sbuf("ab0_v2t", [P, P], BF16)
        for b in range(NB):
            p_r = aps["p_h"][b].rearrange("(t1 t2) e -> t2 t1 e", t2=64)
            p_rb = aps["p_h"][b].rearrange("(a b2) e -> b2 a e", b2=64)
            v1_r = aps["v1_h"].rearrange("(t1 t2) c -> t2 t1 c", t2=64)
            v1_rb = aps["v1_h"].rearrange("(a b2) c -> b2 a c", b2=64)
            v2_rb = aps["v2_h"].rearrange("(a b2) c -> b2 a c", b2=64)
            run_fft_fwd([p_r[t2, :, 0:D] for t2 in range(64)],
                        [aps["F_h"][0, k1] for k1 in range(64)])
            run_passB(p_rb, D, [v1_rb[bb] for bb in range(64)])
            run_fft_fwd([v1_r[t2] for t2 in range(64)],
                        [aps["F_h"][1, k1] for k1 in range(64)])
            run_passB(p_rb, 2 * D, [v2_rb[bb] for bb in range(64)])
            for s in s_vo:
                s.wait(c2.sp)
                s.wait(c2.pool)
            for cc in range(8):
                for kc in range(KC):
                    sxr.wait(c2.sp, sxr.n)
                    ssr.wait(c2.sp)
                    sxr.inc(c2.sp.dma_start(
                        out=Xr[:, :],
                        in_=aps["rin"][b][kc * P:(kc + 1) * P,
                                          cc * TCH:(cc + 1) * TCH]), 16)
                    c2.s_v.wait(c2.act)
                    for tt in range(4):
                        sv2.wait(c2.pool, sv2.n)
                        c2.s_pe.wait(c2.pool)
                        sv2.inc(c2.pool.dma_start(
                            out=v2t[:, :],
                            in_=aps["v2_h"][cc * TCH + tt * P: cc * TCH + (tt + 1) * P,
                                            kc * P:(kc + 1) * P]), 16)
                        sv2.wait(c2.pe)
                        th2.transpose(v2t[:, :], V2F[:, tt * P:(tt + 1) * P])
                    c2.s_a.wait(c2.dve)
                    sxr.wait(c2.dve)
                    c2.s_v.wait(c2.dve)
                    c2.s_v.inc(c2.dve.tensor_tensor(out=Xr[:, :], in0=V2F[:, :],
                                                    in1=Xr[:, :], op=ALU.add))
                    c2.s_v.wait(c2.sp)
                    ssr.inc(c2.sp.dma_start(
                        out=aps["rout"][b][kc * P:(kc + 1) * P, cc * TCH:(cc + 1) * TCH],
                        in_=Xr[:, :]), 16)
            ssr.wait(c2.sp)
        for eng in (c2.pe, c2.dve, c2.act, c2.sp, c2.pool):
            c2.s_pe.wait(eng); c2.s_v.wait(eng); c2.s_a.wait(eng)
            ssr.wait(eng)


def build_l0ab(n_chunks=8):
    nc = bass.Bass("TRN2", target_bir_lowering=False)
    seq = 8 * TCH
    aps = {"rin": nc.dram_tensor("rin", [NB, D, seq], F32, kind="ExternalInput").ap()}
    aps["proj_w"] = nc.dram_tensor("proj_w", [D, E3], F32, kind="ExternalInput").ap()
    aps["proj_b"] = nc.dram_tensor("proj_b", [E3], F32, kind="ExternalInput").ap()
    aps["filt"] = nc.dram_tensor("filt", [2, 1, 2048, D], F32, kind="ExternalInput").ap()
    aps["rout"] = nc.dram_tensor("rout", [NB, D, seq], F32, kind="ExternalOutput").ap()
    aps["p_h"] = nc.dram_tensor("p_h", [NB, N4, E3], BF16, kind="Internal").ap()
    aps["v1_h"] = nc.dram_tensor("v1_h", [N4, D], BF16, kind="Internal").ap()
    aps["v2_h"] = nc.dram_tensor("v2_h", [N4, D], BF16, kind="Internal").ap()
    aps["A_h"] = nc.dram_tensor("A_h", [64, 128, D], BF16, kind="Internal").ap()
    aps["G_h"] = nc.dram_tensor("G_h", [64, 128, D], BF16, kind="Internal").ap()
    aps["F_h"] = nc.dram_tensor("F_h", [2, 64, 128, D], BF16, kind="Internal").ap()
    aps["filtb"] = nc.dram_tensor("filtb", [2, 2048, D], BF16, kind="Internal").ap()
    caps = declare_consts(nc)
    caps.update(declare_fft_consts(nc))
    for k in ("cs1", "t2m", "iam", "pbm"):
        aps[k] = caps[k]
    es = contextlib.ExitStack()
    with es:
        ctx = Ctx(nc, es)
        consts = load_consts(ctx, caps)
        emit_l0ab(ctx, consts, aps, 8)
    return nc




_BUILD_CACHE = {}


def _get(name, fn):
    if name not in _BUILD_CACHE:
        _BUILD_CACHE[name] = fn()
    return _BUILD_CACHE[name]


def _run(nc, per_core_maps):
    res = run_bass_kernel_spmd(nc, per_core_maps, core_ids=list(range(8)))
    return res.results


def kernel(**ins):
    x = np.asarray(ins["L0_attn_a_x"])
    ncores = 8
    shards = [np.ascontiguousarray(x[2 * c:2 * c + 2].transpose(0, 2, 1))
              for c in range(ncores)]
    cst = {}
    cst.update(make_consts())
    cst.update(moe_consts())
    cst.update(l1aa_consts())
    cst.update(ffn_consts())
    cst.update(fft_consts())

    def launch(nc, extra, rins):
        maps = []
        for c in range(ncores):
            m = dict(cst)
            m.update(extra)
            m["rin"] = rins[c]
            maps.append(m)
        res = _run(nc, maps)
        return [r["rout"] for r in res]

    r = shards
    nc = _get("l0aa", build_l0aa)
    r = launch(nc, {"W1": ins["L0_attn_a_W1"], "b1": ins["L0_attn_a_b1"],
                    "W2": ins["L0_attn_a_W2"], "b2": ins["L0_attn_a_b2"],
                    "W3": ins["L0_attn_a_W3"], "b3": ins["L0_attn_a_b3"]}, r)
    ffn_extra = {"W_exp": ins["L0_ffn_W_exp"], "b_exp": ins["L0_ffn_b_exp"],
                 "up_w": ins["L0_ffn_up_weights"], "up_bias": ins["L0_ffn_up_bias"],
                 "down_w": ins["L0_ffn_down_weights"], "W_con": ins["L0_ffn_W_con"],
                 "b_con": ins["L0_ffn_b_con"], "down_bias": ins["L0_ffn_down_bias"]}
    nc = _get("l0ffn", build_l0ffn)
    r = launch(nc, ffn_extra, r)
    nc = _get("l0ab", build_l0ab)
    r = launch(nc, {"proj_w": ins["L0_attn_b_proj_w"], "proj_b": ins["L0_attn_b_proj_b"],
                    "filt": ins["L0_attn_b_filter_time"]}, r)
    nc = _get("l0ffn", build_l0ffn)
    r = launch(nc, ffn_extra, r)
    aa_extra = {"w": ins["L1_attn_a_w"], "u": ins["L1_attn_a_u"]}
    for nm in ("g", "r", "k", "v", "o"):
        aa_extra["W" + nm] = ins[f"L1_attn_a_W{nm}_w"]
        aa_extra["b" + nm] = ins[f"L1_attn_a_W{nm}_b"]
    for nm in ("g", "r", "k", "v"):
        aa_extra["m" + nm] = ins[f"L1_attn_a_m{nm}"]
    nc = _get("l1aa", build_l1aa)
    r = launch(nc, aa_extra, r)
    moe_extra = {"Wg": ins["L1_ffn_Wg"], "ln_scale": ins["L1_ffn_ln_scale"],
                 "ln_bias": ins["L1_ffn_ln_bias"], "W1": ins["L1_ffn_W1"],
                 "b1": ins["L1_ffn_b1"], "W2": ins["L1_ffn_W2"],
                 "b2": ins["L1_ffn_b2"]}
    nc = _get("l1ffn", build_l1ffn)
    r = launch(nc, moe_extra, r)
    nc = _get("l1ab", build_l1ab)
    r = launch(nc, {"Wg": ins["L1_attn_b_Wg"], "bg": ins["L1_attn_b_bg"],
                    "Wv": ins["L1_attn_b_Wv"], "bv": ins["L1_attn_b_bv"],
                    "Wd": ins["L1_attn_b_Wd"], "bd": ins["L1_attn_b_bd"],
                    "w_rms_mix": ins["L1_attn_b_w_rms_mix"],
                    "w_rms_ffn": ins["L1_attn_b_w_rms_ffn"],
                    "W_gate": ins["L1_attn_b_W_gate"], "W_up": ins["L1_attn_b_W_up"],
                    "W_out": ins["L1_attn_b_W_out"]}, r)
    nc = _get("l1ffn", build_l1ffn)
    r = launch(nc, moe_extra, r)
    out = np.empty((16, 4096, 1024), np.float32)
    for c in range(ncores):
        out[2 * c:2 * c + 2] = np.asarray(r[c]).transpose(0, 2, 1)
    return out


# version 06 (final)
# speedup vs baseline: 1.0936x; 1.0936x over previous
import os
import contextlib
import numpy as np
import ml_dtypes
import concourse.bass as bass
import concourse.mybir as mybir
from concourse.bass_utils import run_bass_kernel_spmd

NB = 2

F32 = mybir.dt.float32
BF16 = mybir.dt.bfloat16
AF = mybir.ActivationFunctionType
ALU = mybir.AluOpType
AX = mybir.AxisListType

P = 128
D = 1024
KC = D // P
SEQ = 4096
TCH = 512
NCHUNK = SEQ // TCH
LN_EPS = 1e-5


class SemPool:

    def __init__(self, nc):
        self.nc = nc
        self.handles = {"sp": [], "pool": [], "event": []}
        self.cur = {"sp": 0, "pool": 0, "event": 0}
        self.active = []

    def get(self, cls):
        i = self.cur[cls]
        if i >= len(self.handles[cls]):
            self.handles[cls].append(
                self.nc.alloc_semaphore(name=f"kp_{cls}{i}"))
        self.cur[cls] += 1
        return self.handles[cls][i]

    def phase_reset(self):
        self.cur = {k: 0 for k in self.cur}
        self.active = []


class Sem:

    def __init__(self, pool, name):
        self.pool = pool
        self.name = name
        self.h = None
        self.n = 0
        pool.active.append(self)

    def inc(self, instr, k=1):
        if self.h is None:
            eng = str(getattr(instr, "engine", ""))
            cls = "sp" if eng.endswith(".SP") else (
                "pool" if eng.endswith(".Pool") else "event")
            self.h = self.pool.get(cls)
        instr.then_inc(self.h, k)
        self.n += k
        return self.n

    def wait(self, eng, target=None):
        t = self.n if target is None else target
        if t > 0:
            eng.wait_ge(self.h, t)


class Ctx:

    def __init__(self, nc, es):
        self.nc = nc
        self.es = es
        self.pe = nc.tensor
        self.dve = nc.vector
        self.act = nc.scalar
        self.sp = nc.sync
        self.pool = nc.gpsimd
        if not hasattr(nc, "_kl_sempool"):
            nc._kl_sempool = SemPool(nc)
        self.all_sems = nc._kl_sempool.active
        self.s_pe = self.sem("s_pe")
        self.s_v = self.sem("s_v")
        self.s_a = self.sem("s_a")
        self.s_p = self.sem("s_p")
        self.s_wload = self.sem("s_wload")
        self.s_io = self.sem("s_io")
        self.s_store = self.sem("s_store")

    def _uniq(self, name):
        used = getattr(self.nc, "_kl_names", None)
        if used is None:
            used = set()
            self.nc._kl_names = used
        base, i = name, 0
        while name in used:
            i += 1
            name = f"{base}_p{i}"
        used.add(name)
        return name

    def sbuf(self, name, shape, dt):
        return self.es.enter_context(self.nc.sbuf_tensor(self._uniq(name), shape, dt))

    def psum(self, name, shape, dt=F32):
        return self.es.enter_context(self.nc.psum_tensor(self._uniq(name), shape, dt))

    def sem(self, name):
        return Sem(self.nc._kl_sempool, name)


def phase_epilogue(ctx):
    for s in ctx.all_sems:
        s.wait(ctx.sp)
    ctx.nc.all_engine_barrier()


def phase_teardown(nc):
    sp = nc._kl_sempool
    nc.gpsimd.dma_reset(None)
    for s in sp.active:
        if s.h is not None and s.n > 0:
            nc.gpsimd.sem_clear(s.h)
    nc.all_engine_barrier()
    sp.phase_reset()


CONST_VALS = [1e-5, 1e-6, 0.001, 0.998, -1.0, 0.5, 2.0, 1e-9]


def make_consts():
    pool = np.tile(np.array(CONST_VALS, np.float32)[None, :], (P, 1))
    return {
        "c_eye": np.eye(P, dtype=ml_dtypes.bfloat16),
        "c_eyef": np.eye(P, dtype=np.float32),
        "c_ones_row": np.ones((1, P), dtype=ml_dtypes.bfloat16),
        "c_ones16n": np.full((16, 1), -1.0, dtype=np.float32),
        "c_pool": pool,
    }


def declare_consts(nc):
    return {
        "eye": nc.dram_tensor("c_eye", [P, P], BF16, kind="ExternalInput").ap(),
        "eyef": nc.dram_tensor("c_eyef", [P, P], F32, kind="ExternalInput").ap(),
        "ones_row": nc.dram_tensor("c_ones_row", [1, P], BF16, kind="ExternalInput").ap(),
        "ones16n": nc.dram_tensor("c_ones16n", [16, 1], F32, kind="ExternalInput").ap(),
        "pool": nc.dram_tensor("c_pool", [P, len(CONST_VALS)], F32, kind="ExternalInput").ap(),
    }


def load_consts(ctx, aps):
    t = {}
    t["eye"] = ctx.sbuf("t_eye", [P, P], BF16)
    t["eyef"] = ctx.sbuf("t_eyef", [P, P], F32)
    t["ones_row"] = ctx.sbuf("t_ones_row", [1, P], BF16)
    t["ones16n"] = ctx.sbuf("t_ones16n", [16, 1], F32)
    t["onesK"] = ctx.sbuf("t_onesK", [P, 1], BF16)
    t["onesKf"] = ctx.sbuf("t_onesKf", [P, 1], F32)
    ctx.s_io.inc(ctx.sp.dma_start(out=t["eye"][:, :], in_=aps["eye"][:, :]), 16)
    ctx.s_io.inc(ctx.sp.dma_start(out=t["eyef"][:, :], in_=aps["eyef"][:, :]), 16)
    ctx.s_io.inc(ctx.sp.dma_start(out=t["ones_row"][:, :], in_=aps["ones_row"][:, :]), 16)
    ctx.s_io.inc(ctx.sp.dma_start(out=t["ones16n"][:, :], in_=aps["ones16n"][:, :]), 16)
    t["pool"] = ctx.sbuf("t_pool", [P, len(CONST_VALS)], F32)
    ctx.s_io.inc(ctx.sp.dma_start(out=t["pool"][:, :], in_=aps["pool"][:, :]), 16)
    for i, v in enumerate(CONST_VALS):
        ctx.nc.const_aps.register(F32, v, t["pool"][:, i:i + 1])
    ctx.s_io.wait(ctx.act)
    ctx.s_io.wait(ctx.dve)
    ctx.s_io.wait(ctx.pe)
    ctx.s_io.wait(ctx.pool)
    ctx.s_v.inc(ctx.dve.memset(t["onesK"][:, :], 1.0))
    ctx.s_v.inc(ctx.dve.memset(t["onesKf"][:, :], 1.0))
    ctx.s_v.wait(ctx.pe)
    return t


def load_cast_bf16(ctx, dst_slice, src_ap, sem=None):
    s = sem or ctx.s_wload
    s.inc(ctx.pool.dma_start(out=dst_slice, in_=src_ap), 16)
    return s


class TransposeHelper:

    def __init__(self, ctx, consts, pst=None, width=P):
        self.ctx = ctx
        self.eye = consts["eye"]
        self.pst = pst if pst is not None else ctx.psum("pstT", [P, P], BF16)
        self.guard = 0

    def transpose(self, src_slice, dst_slice, rows=P, cols=P):
        ctx = self.ctx
        if self.guard:
            ctx.s_a.wait(ctx.pe, self.guard)
        tp = ctx.pe.transpose(self.pst[0:cols, 0:rows], src_slice,
                              self.eye[0:rows, 0:rows])
        ctx.s_pe.inc(tp)
        ctx.s_pe.wait(ctx.act)
        ev = ctx.act.activation(out=dst_slice, in_=self.pst[0:cols, 0:rows], func=AF.Copy)
        self.guard = ctx.s_a.inc(ev)


def load_weight_T_big(ctx, consts, th, w_ap, name, n_in=KC, n_out=KC, stg=None):
    if stg is None:
        stg = ctx.sbuf(f"stg_{name}", [P, n_in * P], BF16)
    wT = ctx.sbuf(f"wT_{name}", [P, n_in * n_out * P], BF16)
    s = ctx.sem(f"swl_{name}")
    for oc in range(n_out):
        ctx.s_pe.wait(ctx.pool)
        load_cast_bf16(ctx, stg[:, :], w_ap[oc * P:(oc + 1) * P, :], sem=s)
        s.wait(ctx.pe)
        for kc in range(n_in):
            th.transpose(stg[:, kc * P:(kc + 1) * P],
                         wT[:, (kc * n_out + oc) * P:(kc * n_out + oc + 1) * P])
    return wT


def emit_ln_chunk(ctx, consts, resid_ap, c, bufs, eps=LN_EPS, rms=False,
                  load_guard=None, src_sbuf=None, src_ofs=0):
    par = c % 2
    X, XB, SQ, XN = bufs.get("X"), bufs.get("XB"), bufs["SQ"], bufs["XN"]
    nrm = bufs["nrm"]
    nslx = bufs.get("nslx", 2)
    parx = par % nslx
    xofs = (par % bufs.get("nslxn", 2)) * KC * TCH
    xofsx = parx * KC * TCH
    if src_sbuf is None:
        s_x = bufs["s_x"][parx]
        if load_guard is not None:
            for sem, cnt in (load_guard if isinstance(load_guard, list) else [load_guard]):
                sem.wait(ctx.sp, cnt)
        for j in range(KC):
            dma = ctx.sp.dma_start(out=X[:, xofsx + j * TCH: xofsx + (j + 1) * TCH],
                                   in_=resid_ap[j * P:(j + 1) * P, c * TCH:(c + 1) * TCH])
            s_x.inc(dma, 16)
        s_x.wait(ctx.act)
    if bufs["xb_guard"][parx]:
        ctx.s_v.wait(ctx.act, bufs["xb_guard"][parx][0])
        ctx.s_pe.wait(ctx.act, bufs["xb_guard"][parx][1])
    if bufs.get("pre_square_guard"):
        ctx.s_pe.wait(ctx.act, bufs["pre_square_guard"][0])
        if len(bufs["pre_square_guard"]) > 1:
            ctx.s_v.wait(ctx.act, bufs["pre_square_guard"][1])
        bufs["pre_square_guard"] = None
    if src_sbuf is None:
        for j in range(KC):
            sl = slice(xofsx + j * TCH, xofsx + (j + 1) * TCH)
            ctx.s_a.inc(ctx.act.activation(out=XB[:, sl], in_=X[:, sl], func=AF.Copy))
            ctx.s_a.inc(ctx.act.activation(out=SQ[:, sl], in_=X[:, sl], func=AF.Square))
        src, sofs = XB, xofsx
    else:
        for j in range(KC):
            sl = slice(xofsx + j * TCH, xofsx + (j + 1) * TCH)
            ssl = slice(src_ofs + j * TCH, src_ofs + (j + 1) * TCH)
            ctx.s_a.inc(ctx.act.activation(out=SQ[:, sl], in_=src_sbuf[:, ssl],
                                           func=AF.Square))
        src, sofs = src_sbuf, src_ofs
    ctx.s_a.wait(ctx.pe)
    psS = nrm["psS"]
    if nrm["stats_guard"][0]:
        ctx.s_v.wait(ctx.pe, nrm["stats_guard"][0])
    for j in range(KC):
        sl = slice(xofsx + j * TCH, xofsx + (j + 1) * TCH)
        ssl = slice(sofs + j * TCH, sofs + (j + 1) * TCH)
        if not rms:
            ctx.s_pe.inc(ctx.pe.matmul(psS[0:1, 0:TCH], consts["onesK"][:, :],
                                       src[:, ssl], start=(j == 0), stop=(j == KC - 1)))
        ctx.s_pe.inc(ctx.pe.matmul(psS[0:1, TCH:2 * TCH], consts["onesK"][:, :],
                                   SQ[:, sl], start=(j == 0), stop=(j == KC - 1)))
    pe_stats_done = ctx.s_pe.n
    st = nrm["st"]
    mean = st[0:1, 0:TCH]
    var = st[0:1, TCH:2 * TCH]
    rstd = st[0:1, 2 * TCH:3 * TCH]
    men2 = st[0:1, 3 * TCH:4 * TCH]
    mrs = men2
    rowb = nrm["rowb"]
    rstd_b = rowb[0:1, 0:TCH]
    mrs_b = rowb[0:1, TCH:2 * TCH]
    ctx.s_pe.wait(ctx.dve)
    if nrm["st_guard"][0]:
        ctx.s_a.wait(ctx.dve, nrm["st_guard"][0])
    if not rms:
        ctx.s_v.inc(ctx.dve.tensor_scalar(out=mean, in0=psS[0:1, 0:TCH], scalar1=1.0 / D,
                                          scalar2=None, op0=ALU.mult))
        ctx.s_v.wait(ctx.dve)
        ctx.s_v.inc(ctx.dve.tensor_tensor(out=men2, in0=mean, in1=mean, op=ALU.mult))
        ctx.s_v.wait(ctx.dve)
        ctx.s_v.inc(ctx.dve.scalar_tensor_tensor(out=var, in0=psS[0:1, TCH:2 * TCH],
                                                 scalar=1.0 / D, in1=men2,
                                                 op0=ALU.mult, op1=ALU.subtract))
    else:
        ctx.s_v.inc(ctx.dve.tensor_scalar(out=var, in0=psS[0:1, TCH:2 * TCH],
                                          scalar1=1.0 / D, scalar2=None, op0=ALU.mult))
    nrm["stats_guard"][0] = ctx.s_v.n
    ctx.s_v.wait(ctx.act)
    ctx.s_a.inc(ctx.act.activation(out=var, in_=var, func=AF.Ln, bias=eps))
    ctx.s_a.wait(ctx.act)
    ctx.s_a.inc(ctx.act.activation(out=rstd, in_=var, func=AF.Exp, scale=-0.5))
    if not rms:
        ctx.s_a.wait(ctx.dve)
        ctx.s_v.inc(ctx.dve.tensor_tensor(out=mrs, in0=mean, in1=rstd, op=ALU.mult))
        ctx.s_v.wait(ctx.act)
    ctx.s_a.wait(ctx.act)
    ctx.s_a.inc(ctx.act.activation(out=rstd_b, in_=rstd, func=AF.Copy))
    if not rms:
        ctx.s_a.inc(ctx.act.activation(out=mrs_b, in_=mrs, func=AF.Copy))
    nrm["st_guard"][0] = ctx.s_a.n
    psB = nrm["psB"]
    rbs = nrm["rbs"]
    if nrm["rbs_guard"][0]:
        ctx.s_v.wait(ctx.act, nrm["rbs_guard"][0])
    rstd_bb = rbs[:, 0:TCH]
    mrs_bb = rbs[:, TCH:2 * TCH]
    if nrm["bcast_guard"][0]:
        ctx.s_a.wait(ctx.pe, nrm["bcast_guard"][0])
    ctx.s_a.wait(ctx.pe)
    ctx.s_pe.inc(ctx.pe.matmul(psB[:, 0:TCH], consts["ones_row"][:, :], rstd_b,
                               start=True, stop=True))
    ctx.s_pe.wait(ctx.act)
    ctx.s_a.inc(ctx.act.activation(out=rstd_bb, in_=psB[:, 0:TCH], func=AF.Copy))
    if not rms:
        ctx.s_a.wait(ctx.pe)
        ctx.s_pe.inc(ctx.pe.matmul(psB[:, 0:TCH], consts["ones_row"][:, :], mrs_b,
                                   start=True, stop=True))
        ctx.s_pe.wait(ctx.act)
        ctx.s_a.inc(ctx.act.activation(out=mrs_bb, in_=psB[:, 0:TCH], func=AF.Copy))
    nrm["bcast_guard"][0] = ctx.s_a.n
    ctx.s_a.wait(ctx.dve)
    if bufs["xn_guard"][par]:
        ctx.s_pe.wait(ctx.dve, bufs["xn_guard"][par])
    gain = bufs.get("gain_cols")
    for j in range(KC):
        sl = slice(xofs + j * TCH, xofs + (j + 1) * TCH)
        ssl = slice(sofs + j * TCH, sofs + (j + 1) * TCH)
        if rms and gain is not None:
            ctx.s_v.inc(ctx.dve.scalar_tensor_tensor(out=XN[:, sl], in0=src[:, ssl],
                                                     scalar=gain[:, j:j + 1],
                                                     in1=rstd_bb, op0=ALU.mult,
                                                     op1=ALU.mult))
        else:
            ctx.s_v.inc(ctx.dve.tensor_tensor(out=XN[:, sl], in0=src[:, ssl],
                                              in1=rstd_bb, op=ALU.mult))
    if not rms:
        ctx.s_v.wait(ctx.dve)
        for j in range(KC):
            sl = slice(xofs + j * TCH, xofs + (j + 1) * TCH)
            ctx.s_v.inc(ctx.dve.tensor_tensor(out=XN[:, sl], in0=XN[:, sl], in1=mrs_bb,
                                              op=ALU.subtract))
    bufs["xb_guard"][parx] = (ctx.s_v.n, pe_stats_done)
    nrm["rbs_guard"][0] = ctx.s_v.n
    return


def alloc_nrm(ctx, tag):
    return {
        "psS": ctx.psum(f"lnS_{tag}", [1, 2 * TCH]),
        "psB": ctx.psum(f"lnB_{tag}", [P, TCH]),
        "st": ctx.sbuf(f"lnst_{tag}", [1, 4 * TCH], F32),
        "rowb": ctx.sbuf(f"lnrowb_{tag}", [1, 2 * TCH], BF16),
        "rbs": ctx.sbuf(f"lnrbs_{tag}", [P, 2 * TCH], BF16),
        "stats_guard": [0],
        "bcast_guard": [0],
        "st_guard": [0],
        "rbs_guard": [0],
    }


def alloc_ln_bufs(ctx, tag, nrm, tiles=True, nslx=2, nslxn=2):
    b = {
        "nrm": nrm,
        "nslx": nslx,
        "s_x": [ctx.sem(f"lnsx0_{tag}"), ctx.sem(f"lnsx1_{tag}")],
        "xb_guard": [None, None],
        "xn_guard": [0, 0],
    }
    b["nslxn"] = nslxn
    if tiles:
        b["X"] = ctx.sbuf(f"lnX_{tag}", [P, nslx * KC * TCH], F32)
        b["XB"] = ctx.sbuf(f"lnXB_{tag}", [P, nslx * KC * TCH], BF16)
        b["XN"] = ctx.sbuf(f"lnXN_{tag}", [P, nslxn * KC * TCH], BF16)
    b["SQ"] = ctx.sbuf(f"lnSQ_{tag}", [P, nslx * KC * TCH], BF16)
    return b
def emit_l0aa(ctx, consts, th, aps, n_chunks, pre):
    nc = ctx.nc
    SETUP = int(os.environ.get('L0AA_SETUP', 4))
    if SETUP >= 2:
        w1T = load_weight_T_big(ctx, consts, th, aps["W1"], "w1")
    stg23 = ctx.sbuf("stg23", [64, D], BF16)
    if SETUP < 3:
        return
    s23 = ctx.sem("s23")
    m = ctx.dve.memset(stg23[:, :], 0.0)
    ctx.s_v.inc(m)
    ctx.s_v.wait(ctx.pool)
    load_cast_bf16(ctx, stg23[0:16, :], aps["W2"][:, :], sem=s23)
    load_cast_bf16(ctx, stg23[32:48, :], aps["W3"][:, :], sem=s23)
    s23.wait(ctx.pe)
    w23T = ctx.sbuf("w23T", [P, KC * 64], BF16)
    for kc in range(KC):
        th.transpose(stg23[0:64, kc * P:(kc + 1) * P], w23T[:, kc * 64:(kc + 1) * 64], rows=64)
    b1c = ctx.sbuf("b1c", [P, KC], F32)
    nb1c = ctx.sbuf("nb1c", [P, KC], F32)
    if SETUP < 4:
        return
    sb = ctx.sem("sb_l0aa")
    for j in range(KC):
        sb.inc(ctx.pool.dma_start(
            out=b1c[:, j:j + 1],
            in_=aps["b1"][j * P:(j + 1) * P].rearrange("(p one) -> p one", one=1)), 16)
    b2c = ctx.sbuf("b2c", [16, 1], F32)
    b3c = ctx.sbuf("b3c", [16, 1], F32)
    sb.inc(ctx.pool.dma_start(
        out=b2c[:, :], in_=aps["b2"][:].rearrange("(p one) -> p one", one=1)), 16)
    sb.inc(ctx.pool.dma_start(
        out=b3c[:, :], in_=aps["b3"][:].rearrange("(p one) -> p one", one=1)), 16)
    sb.wait(ctx.dve)
    ctx.s_v.inc(ctx.dve.tensor_scalar(out=nb1c[:, :], in0=b1c[:, :], scalar1=-1.0,
                                      scalar2=None, op0=ALU.mult))
    setup_act = ctx.s_a.n
    setup_v = ctx.s_v.n
    setup_sb = sb.n

    ln = pre["ln"]
    psA, psB_, psC = pre["psA"], pre["psB"], pre["psC"]
    Sd = pre["Sd"]
    Dl = pre["Dl"]
    BCP = pre["BCP"]
    BCC = pre["BCC"]
    PRm = pre["PR"]
    bcrow = pre["bcrow"]
    bcb = pre["bcb"]
    Yt = pre["Yt"]
    sst = pre["sst"]
    bankg = pre["bankg"]
    cg = pre["cg"]
    store_v = pre["store_v"]
    first = pre.get("first", True)

    gemm_banks = [psA, psB_]
    for b in range(NB):
        resid_in = aps["rin"][b]
        resid_out = aps["rout"][b]
        STAGE = int(os.environ.get('L0AA_STAGE', 6))
        for c in range(n_chunks):
            par = c % 2
            xofs = par * KC * TCH
            if STAGE < 2:
                continue
            emit_ln_chunk(ctx, consts, resid_in, c, ln,
                          load_guard=(sst[par], sst[par].n))
            xn_ready = ctx.s_v.n
            if first and b == 0 and c == 0:
                ctx.s_a.wait(ctx.pe, setup_act)
                ctx.s_v.wait(ctx.pe, setup_v)
                sb.wait(ctx.act, setup_sb)
                ctx.s_v.wait(ctx.act, setup_v)

            def XNs(kc):
                return ln["XN"][:, xofs + kc * TCH: xofs + (kc + 1) * TCH]

            if STAGE < 3:
                continue
            ctx.s_v.wait(ctx.pe, xn_ready)
            if cg[0]:
                ctx.s_a.wait(ctx.pe, cg[0])
            for kc in range(KC):
                ctx.s_pe.inc(ctx.pe.matmul(psC[0:64, :], w23T[:, kc * 64:(kc + 1) * 64],
                                           XNs(kc), start=(kc == 0), stop=(kc == KC - 1)))
            ctx.s_pe.wait(ctx.act)
            ctx.s_a.inc(ctx.act.activation(out=BCP[:, par * TCH:(par + 1) * TCH],
                                           in_=psC[0:16, :], func=AF.Identity,
                                           bias=b2c[:, :], scale=1.0))
            ctx.s_a.inc(ctx.act.activation(out=BCC[:, par * TCH:(par + 1) * TCH],
                                           in_=psC[32:48, :], func=AF.Identity,
                                           bias=b3c[:, :], scale=1.0))
            bc_evac = ctx.s_a.n
            ctx.s_a.wait(ctx.dve)
            ctx.s_v.inc(ctx.dve.tensor_tensor(
                out=PRm[:, par * TCH:(par + 1) * TCH],
                in0=BCP[:, par * TCH:(par + 1) * TCH],
                in1=BCC[:, par * TCH:(par + 1) * TCH], op=ALU.mult))
            pr_done = ctx.s_v.n

            if STAGE < 4:
                continue
            def delta_gemm(oc):
                bank = gemm_banks[oc % 2]
                if bankg[oc % 2]:
                    ctx.s_a.wait(ctx.pe, bankg[oc % 2])
                for kc in range(KC):
                    ctx.s_pe.inc(ctx.pe.matmul(
                        bank[:, :], w1T[:, (kc * KC + oc) * P:(kc * KC + oc + 1) * P],
                        XNs(kc), start=(kc == 0), stop=(kc == KC - 1)))
                ctx.s_pe.wait(ctx.act)
                ctx.s_a.wait(ctx.act)
                ctx.s_a.inc(ctx.act.activation(out=Sd[:, :], in_=bank[:, :],
                                               func=AF.Sigmoid, bias=nb1c[:, oc:oc + 1],
                                               scale=-1.0))
                bankg[oc % 2] = ctx.s_a.n
                ctx.s_a.wait(ctx.act)
                ctx.s_a.inc(ctx.act.activation(
                    out=Dl[:, xofs + oc * TCH: xofs + (oc + 1) * TCH],
                    in_=Sd[:, :], func=AF.Ln))

            if STAGE >= 5:
                delta_gemm(0)
                delta_gemm(1)

            ctx.s_a.wait(ctx.pe, bc_evac)
            ctx.s_v.wait(ctx.pe, pr_done)
            ctx.s_pe.inc(ctx.pe.matmul(psC[0:1, :], consts["ones16n"][:, :],
                                       PRm[:, par * TCH:(par + 1) * TCH],
                                       start=True, stop=True))
            ctx.s_pe.wait(ctx.act)
            ctx.s_a.inc(ctx.act.activation(out=bcrow[0:1, par * TCH:(par + 1) * TCH],
                                           in_=psC[0:1, :], func=AF.Copy))
            bcrow_done = ctx.s_a.n

            if STAGE >= 5:
                delta_gemm(2)
                delta_gemm(3)

            ctx.s_a.wait(ctx.pe, bcrow_done)
            ctx.s_pe.inc(ctx.pe.matmul(psC[0:P, :], consts["ones_row"][:, :],
                                       bcrow[0:1, par * TCH:(par + 1) * TCH],
                                       start=True, stop=True))
            ctx.s_pe.wait(ctx.act)
            ctx.s_a.inc(ctx.act.activation(out=bcb[:, par * TCH:(par + 1) * TCH],
                                           in_=psC[0:P, :], func=AF.Copy))
            cg[0] = ctx.s_a.n
            bcb_done = ctx.s_a.n

            if STAGE >= 5:
                for oc in range(4, KC):
                    delta_gemm(oc)
            if STAGE < 6:
                continue

            ln["xn_guard"][par] = ctx.s_pe.n
            act_delta_done = ctx.s_a.n
            ctx.s_a.wait(ctx.dve, act_delta_done)
            for oc in range(KC):
                osl = slice(xofs + oc * TCH, xofs + (oc + 1) * TCH)
                ctx.s_v.wait(ctx.dve)
                ctx.s_v.inc(ctx.dve.tensor_tensor(
                    out=Yt[:, par * TCH:(par + 1) * TCH], in0=Dl[:, osl],
                    in1=bcb[:, par * TCH:(par + 1) * TCH], op=ALU.mult))
                ctx.s_v.wait(ctx.dve)
                ctx.s_v.inc(ctx.dve.tensor_tensor(
                    out=Yt[:, par * TCH:(par + 1) * TCH],
                    in0=Yt[:, par * TCH:(par + 1) * TCH],
                    in1=ln["XN"][:, osl], op=ALU.mult))
                ctx.s_v.wait(ctx.dve)
                ctx.s_v.inc(ctx.dve.tensor_tensor(
                    out=ln["X"][:, osl], in0=Yt[:, par * TCH:(par + 1) * TCH],
                    in1=ln["X"][:, osl], op=ALU.add))
                ctx.s_v.wait(ctx.sp)
                sst[par].inc(ctx.sp.dma_start(
                    out=resid_out[oc * P:(oc + 1) * P, c * TCH:(c + 1) * TCH],
                    in_=ln["X"][:, osl]), 16)
    pre["first"] = False


def phase_l0aa(ctx, consts, aps, n_chunks):
    th = TransposeHelper(ctx, consts)
    pre = {
        "ln": alloc_ln_bufs(ctx, "l0aa", alloc_nrm(ctx, "l0aa")),
        "psA": ctx.psum("psA", [P, TCH]),
        "psB": ctx.psum("psB", [P, TCH]),
        "psC": ctx.psum("psC", [P, TCH]),
        "Sd": ctx.sbuf("Sd", [P, TCH], F32),
        "Dl": ctx.sbuf("Dl", [P, 2 * KC * TCH], BF16),
        "BCP": ctx.sbuf("BCP", [16, 2 * TCH], F32),
        "BCC": ctx.sbuf("BCC", [16, 2 * TCH], F32),
        "PR": ctx.sbuf("PRm", [16, 2 * TCH], F32),
        "bcrow": ctx.sbuf("bcrow", [1, 2 * TCH], BF16),
        "bcb": ctx.sbuf("bcb", [P, 2 * TCH], BF16),
        "Yt": ctx.sbuf("Yt", [P, 2 * TCH], F32),
        "sst": [ctx.sem("sst0"), ctx.sem("sst1")],
        "bankg": {0: 0, 1: 0},
        "cg": [0],
        "store_v": [0, 0],
    }
    emit_l0aa(ctx, consts, th, aps, n_chunks, pre)
    for s in pre["sst"]:
        s.wait(ctx.sp)


def build_l0aa(n_chunks=int(os.environ.get('L0AA_NC', 8))):
    nc = bass.Bass("TRN2", target_bir_lowering=False)
    seq = n_chunks * TCH
    rin = nc.dram_tensor("rin", [NB, D, seq], F32, kind="ExternalInput").ap()
    W1 = nc.dram_tensor("W1", [D, D], F32, kind="ExternalInput").ap()
    b1 = nc.dram_tensor("b1", [D], F32, kind="ExternalInput").ap()
    W2 = nc.dram_tensor("W2", [16, D], F32, kind="ExternalInput").ap()
    b2 = nc.dram_tensor("b2", [16], F32, kind="ExternalInput").ap()
    W3 = nc.dram_tensor("W3", [16, D], F32, kind="ExternalInput").ap()
    b3 = nc.dram_tensor("b3", [16], F32, kind="ExternalInput").ap()
    rout = nc.dram_tensor("rout", [NB, D, seq], F32, kind="ExternalOutput").ap()
    caps = declare_consts(nc)

    es = contextlib.ExitStack()
    with es:
        ctx = Ctx(nc, es)
        consts = load_consts(ctx, caps)
        aps = {"rin": rin, "rout": rout, "W1": W1, "b1": b1, "W2": W2, "b2": b2,
               "W3": W3, "b3": b3}
        phase_l0aa(ctx, consts, aps, n_chunks)
    return nc


NE = 8
HID = 256
BIGNEG = 1.0e9
def moe_consts():
    c = np.zeros((9, NE * P), np.float32)
    for e in range(NE):
        c[e, e * P:(e + 1) * P] = BIGNEG
        c[8, e * P:(e + 1) * P] = -BIGNEG
    return {"c_moepen": c.astype(ml_dtypes.bfloat16)}


def declare_moe_consts(nc):
    return {"moepen": nc.dram_tensor("c_moepen", [9, NE * P], BF16,
                                     kind="ExternalInput").ap()}


def emit_l1ffn(ctx, consts, th, aps, n_chunks, pre):
    nc = ctx.nc
    s_ws = ctx.sem("sws_l1ffn")
    pen = ctx.sbuf("moepen", [9, NE * P], BF16)
    s_ws.inc(ctx.pool.dma_start(out=pen[:, :], in_=aps["moepen"][:, :]), 16)
    stg8 = ctx.sbuf("stg8", [8, D], F32)
    s_ws.inc(ctx.pool.dma_start(out=stg8[:, :], in_=aps["Wg"][:, :]), 16)
    s_ws.wait(ctx.pe)
    wgTf = ctx.sbuf("wgTf", [P, KC * 8], F32)
    pstF0 = pre["pstF"]
    tguard = 0
    for kc in range(KC):
        if tguard:
            ctx.s_a.wait(ctx.pe, tguard)
        ctx.s_pe.inc(ctx.pe.transpose(pstF0[0:P, 0:8],
                                      stg8[0:8, kc * P:(kc + 1) * P],
                                      consts["eyef"][0:8, 0:8]))
        ctx.s_pe.wait(ctx.act)
        ctx.s_a.inc(ctx.act.activation(out=wgTf[:, kc * 8:(kc + 1) * 8],
                                       in_=pstF0[0:P, 0:8], func=AF.Copy))
        tguard = ctx.s_a.n
    if tguard:
        ctx.s_a.wait(ctx.pe, tguard)
    for kc in range(KC):
        ctx.s_pe.inc(ctx.pe.matmul(pstF0[0:8, 0:1], wgTf[:, kc * 8:(kc + 1) * 8],
                                   consts["onesKf"][:, :], start=(kc == 0),
                                   stop=(kc == KC - 1)))
    ctx.s_pe.wait(ctx.act)
    ncol = ctx.sbuf("ncol", [8, 1], F32)
    ctx.s_a.inc(ctx.act.activation(out=ncol[:, :], in_=pstF0[0:8, 0:1], func=AF.Copy,
                                   scale=-1.0))
    pre["pstF_g"][0] = ctx.s_a.n
    stgS = ctx.sbuf("stgS", [8, D], BF16)
    stgB = ctx.sbuf("stgB", [8, D], BF16)
    sS = ctx.sem("sS_l1ffn")
    load_cast_bf16(ctx, stgS[:, :], aps["ln_scale"][:, :], sem=sS)
    load_cast_bf16(ctx, stgB[:, :], aps["ln_bias"][:, :], sem=sS)
    sS.wait(ctx.pe)
    scolb = ctx.sbuf("scolb", [P, KC * 8], BF16)
    bcolb = ctx.sbuf("bcolb", [P, KC * 8], BF16)
    for kc in range(KC):
        th.transpose(stgS[0:8, kc * P:(kc + 1) * P], scolb[:, kc * 8:(kc + 1) * 8], rows=8)
        th.transpose(stgB[0:8, kc * P:(kc + 1) * P], bcolb[:, kc * 8:(kc + 1) * 8], rows=8)
    scolf = ctx.sbuf("scolf", [P, KC * 8], F32)
    ctx.s_a.wait(ctx.act)
    ctx.s_a.inc(ctx.act.activation(out=scolf[:, :], in_=scolb[:, :], func=AF.Copy))
    w1b = ctx.sbuf("w1b", [P, NE * KC * HID], BF16)
    sW1 = ctx.sem("sW1")
    for e in range(NE):
        for kc in range(KC):
            sW1.inc(ctx.pool.dma_start(
                out=w1b[:, (e * KC + kc) * HID:(e * KC + kc + 1) * HID],
                in_=aps["W1"][e, kc * P:(kc + 1) * P, :]), 16)
    w2b = ctx.sbuf("w2b", [P, NE * 2 * D], BF16)
    sW2 = ctx.sem("sW2")
    for e in range(NE):
        for hc in range(2):
            sW2.inc(ctx.pool.dma_start(
                out=w2b[:, (e * 2 + hc) * D:(e * 2 + hc + 1) * D],
                in_=aps["W2"][e, hc * P:(hc + 1) * P, :]), 16)
    b2l = ctx.sbuf("b2l", [8, KC * P], BF16)
    for oc in range(KC):
        sW2.inc(ctx.pool.dma_start(out=b2l[:, oc * P:(oc + 1) * P],
                                   in_=aps["b2"][:, oc * P:(oc + 1) * P]), 16)
    b1c = ctx.sbuf("b1c_ffn", [P, NE * 2], F32)
    sb1 = ctx.sem("sb1")
    for e in range(NE):
        for h in range(2):
            sb1.inc(ctx.pool.dma_start(
                out=b1c[:, e * 2 + h:e * 2 + h + 1],
                in_=aps["b1"][e, h * P:(h + 1) * P].rearrange("(p one) -> p one", one=1)), 16)
    ps1 = pre["pstF"]
    b1p = ctx.sbuf("b1p", [P, NE * 2], F32)
    sW1.wait(ctx.pe)
    sS.wait(ctx.pe)
    ctx.s_a.wait(ctx.pe)
    prev_ev = pre["pstF_g"][0]
    for e in range(NE):
        for h in range(2):
            if prev_ev:
                ctx.s_a.wait(ctx.pe, prev_ev)
            for kc in range(KC):
                ctx.s_pe.inc(ctx.pe.matmul(
                    ps1[0:P, 0:1],
                    w1b[:, (e * KC + kc) * HID + h * P:(e * KC + kc) * HID + (h + 1) * P],
                    bcolb[:, kc * 8 + e:kc * 8 + e + 1],
                    start=(kc == 0), stop=(kc == KC - 1)))
            ctx.s_pe.wait(ctx.act)
            sb1.wait(ctx.act)
            ctx.s_a.inc(ctx.act.activation(out=b1p[:, e * 2 + h:e * 2 + h + 1],
                                           in_=ps1[0:P, 0:1], func=AF.Identity,
                                           bias=b1c[:, e * 2 + h:e * 2 + h + 1],
                                           scale=1.0))
            prev_ev = ctx.s_a.n
    ctx.s_pe.wait(ctx.dve)
    ctx.s_a.wait(ctx.dve)
    for e in range(NE):
        for kc in range(KC):
            sl = slice((e * KC + kc) * HID, (e * KC + kc + 1) * HID)
            ctx.s_v.inc(ctx.dve.tensor_scalar(
                out=w1b[:, sl], in0=w1b[:, sl],
                scalar1=scolf[:, kc * 8 + e:kc * 8 + e + 1], scalar2=None,
                op0=ALU.mult))
    setup_v = ctx.s_v.n
    setup_a = ctx.s_a.n

    ln = pre["ln"]
    ln2 = pre["ln2"]
    pstF = pre["pstF"]
    phA, phB, psO = pre["phA"], pre["phB"], pre["psO"]
    lgS = pre["lgS"]
    LG = pre["LG"]
    Mx = pre["Mx"]
    mask01 = pre["mask01"]
    maskT = pre["maskT"]
    H = pre["H"]
    sst = pre["sst"]
    pstF_g = pre["pstF_g"]
    hg = pre["hg"]
    og = pre["og"]
    mg = pre["mg"]
    hwg = pre["hwg"]

    for par in range(2):
        for t in range(4):
            ctx.s_v.inc(ctx.dve.memset(mask01[:, par * 36 + 9 * t + 8: par * 36 + 9 * t + 9], 1.0))

    first = True
    for b in range(NB):
        resid_in = aps["rin"][b]
        resid_out = aps["rout"][b]
        for c in range(n_chunks):
            par = c % 2
            xofs = par * KC * TCH
            tsl = slice(par * TCH, (par + 1) * TCH)
            emit_ln_chunk(ctx, consts, resid_in, c, ln,
                          load_guard=[(sst[0], sst[0].n), (sst[1], sst[1].n)])
            outer_stats = ctx.s_pe.n
            xn_ready = ctx.s_v.n
            if first:
                ctx.s_v.wait(ctx.pe, setup_v)
                ctx.s_a.wait(ctx.pe, setup_a)
                sW2.wait(ctx.pe)
                s_ws.wait(ctx.pe)
                first = False

            def XNs(kc):
                return ln["XN"][:, xofs + kc * TCH: xofs + (kc + 1) * TCH]

            def XHs(kc):
                return ln2["XN"][:, xofs + kc * TCH: xofs + (kc + 1) * TCH]

            ln["s_x"][0].wait(ctx.pe)
            if pstF_g[0]:
                ctx.s_a.wait(ctx.pe, pstF_g[0])
            ctx.s_v.wait(ctx.pe, xn_ready)
            for kc in range(KC):
                ctx.s_pe.inc(ctx.pe.matmul(pstF[0:8, :], wgTf[:, kc * 8:(kc + 1) * 8],
                                           ln["X"][:, kc * TCH:(kc + 1) * TCH],
                                           start=(kc == 0), stop=(kc == KC - 1)))
            ctx.s_pe.wait(ctx.act)
            rbf = pre["rbf"]
            ctx.s_a.inc(ctx.act.activation(out=rbf[:, 0:TCH],
                                           in_=ln["nrm"]["rbs"][0:8, 0:TCH],
                                           func=AF.Copy))
            ctx.s_a.inc(ctx.act.activation(out=rbf[:, TCH:2 * TCH],
                                           in_=ln["nrm"]["rbs"][0:8, TCH:2 * TCH],
                                           func=AF.Copy))
            ctx.s_a.wait(ctx.dve)
            ctx.s_v.inc(ctx.dve.tensor_tensor(out=lgS[:, tsl], in0=pstF[0:8, :],
                                              in1=rbf[:, 0:TCH], op=ALU.mult))
            ctx.s_v.wait(ctx.dve)
            ctx.s_v.inc(ctx.dve.scalar_tensor_tensor(out=lgS[:, tsl], in0=rbf[:, TCH:2 * TCH],
                                                     scalar=ncol[:, :], in1=lgS[:, tsl],
                                                     op0=ALU.mult, op1=ALU.add))
            lg_done = ctx.s_v.n
            ctx.s_v.wait(ctx.pe, lg_done)
            pstF_g[0] = 0
            ctx.s_a.wait(ctx.pe)
            for t in range(4):
                ctx.s_pe.inc(ctx.pe.transpose(pstF[0:P, t * 8:(t + 1) * 8],
                                              lgS[0:8, par * TCH + t * P: par * TCH + (t + 1) * P],
                                              consts["eyef"][0:8, 0:8]))
            ctx.s_pe.wait(ctx.act)
            ctx.s_a.inc(ctx.act.activation(out=LG[:, par * 32:(par + 1) * 32],
                                           in_=pstF[0:P, 0:32], func=AF.Copy))
            pstF_g[0] = ctx.s_a.n
            ctx.s_a.wait(ctx.dve)
            for t in range(4):
                ctx.s_v.inc(ctx.dve.tensor_reduce(
                    out=Mx[:, par * 4 + t: par * 4 + t + 1],
                    in_=LG[:, par * 32 + t * 8: par * 32 + (t + 1) * 8],
                    axis=AX.X, op=ALU.max))
            ctx.s_v.wait(ctx.dve)
            for t in range(4):
                ctx.s_v.inc(ctx.dve.tensor_scalar(
                    out=mask01[:, par * 36 + 9 * t: par * 36 + 9 * t + 8],
                    in0=LG[:, par * 32 + t * 8: par * 32 + (t + 1) * 8],
                    scalar1=Mx[:, par * 4 + t: par * 4 + t + 1], scalar2=None,
                    op0=ALU.is_ge))
            msk_done = ctx.s_v.n
            ctx.s_v.wait(ctx.pe, msk_done)
            if mg[par]:
                ctx.s_pe.wait(ctx.act, mg[par])
            for t in range(4):
                th.transpose(mask01[:, par * 36 + 9 * t: par * 36 + 9 * (t + 1)],
                             maskT[:, par * TCH + t * P: par * TCH + (t + 1) * P],
                             rows=P, cols=9)
            mT_done = ctx.s_a.n
            ln2["pre_square_guard"] = (outer_stats, xn_ready)
            emit_ln_chunk(ctx, consts, None, c, ln2, src_sbuf=ln["XN"], src_ofs=xofs)
            xhat_ready = ctx.s_v.n
            banks = [phA, phB]
            ctx.s_v.wait(ctx.pe, xhat_ready)
            ctx.s_a.wait(ctx.pe, mT_done)
            k = 0
            for e in range(NE):
                for hc in range(2):
                    bank = banks[k % 2]
                    if hg[k % 2]:
                        ctx.s_a.wait(ctx.pe, hg[k % 2])
                    for kc in range(KC):
                        ctx.s_pe.inc(ctx.pe.matmul(
                            bank[:, :],
                            w1b[:, (e * KC + kc) * HID + hc * P:(e * KC + kc) * HID + (hc + 1) * P],
                            XHs(kc), start=(kc == 0), stop=False))
                    ctx.s_pe.inc(ctx.pe.matmul(
                        bank[:, :], pen[:, e * P:(e + 1) * P],
                        maskT[:, tsl], start=False, stop=True))
                    ctx.s_pe.wait(ctx.act)
                    if hwg[0]:
                        ctx.s_pe.wait(ctx.act, hwg[0])
                    ctx.s_a.inc(ctx.act.activation(
                        out=H[:, (e * 2 + hc) * TCH:(e * 2 + hc + 1) * TCH],
                        in_=bank[:, :], func=AF.Gelu,
                        bias=b1p[:, e * 2 + hc:e * 2 + hc + 1], scale=1.0))
                    hg[k % 2] = ctx.s_a.n
                    k += 1
            h_done = ctx.s_a.n
            ln["xn_guard"][par] = ctx.s_pe.n
            ln2["xn_guard"][par] = ctx.s_pe.n
            ctx.s_a.wait(ctx.pe, h_done)
            for oc in range(KC):
                if og[0]:
                    ctx.s_v.wait(ctx.pe, og[0])
                for e in range(NE):
                    for hc in range(2):
                        ctx.s_pe.inc(ctx.pe.matmul(
                            psO[:, :],
                            w2b[:, (e * 2 + hc) * D + oc * P:(e * 2 + hc) * D + (oc + 1) * P],
                            H[:, (e * 2 + hc) * TCH:(e * 2 + hc + 1) * TCH],
                            start=(e == 0 and hc == 0), stop=False))
                ctx.s_pe.inc(ctx.pe.matmul(psO[:, :], b2l[:, oc * P:(oc + 1) * P],
                                           maskT[0:8, tsl], start=False, stop=True))
                ctx.s_pe.wait(ctx.dve)
                osl = slice(oc * TCH, (oc + 1) * TCH)
                ctx.s_v.inc(ctx.dve.tensor_tensor(out=ln["X"][:, osl], in0=psO[:, :],
                                                  in1=ln["X"][:, osl], op=ALU.add))
                og[0] = ctx.s_v.n
                ctx.s_v.wait(ctx.sp)
                sst[par].inc(ctx.sp.dma_start(
                    out=resid_out[oc * P:(oc + 1) * P, c * TCH:(c + 1) * TCH],
                    in_=ln["X"][:, osl]), 16)
            hwg[0] = ctx.s_pe.n
            mg[par] = ctx.s_a.n


def phase_l1ffn(ctx, consts, aps, n_chunks, tag="ffn"):
    th = TransposeHelper(ctx, consts)
    nrm = alloc_nrm(ctx, tag)
    ln = alloc_ln_bufs(ctx, tag, nrm, nslx=1)
    ln2 = alloc_ln_bufs(ctx, tag + "2", nrm, tiles=False, nslx=1)
    ln2["SQ"] = ln["SQ"]
    ln2["XN"] = ctx.sbuf(f"lnXN_{tag}2", [P, 2 * KC * TCH], BF16)
    ln2["X"] = None
    ln2["XB"] = None
    pre = {
        "ln": ln, "ln2": ln2,
        "pstF": ctx.psum(f"{tag}pstF", [P, TCH]),
        "phA": ctx.psum(f"{tag}phA", [P, TCH]),
        "phB": ctx.psum(f"{tag}phB", [P, TCH]),
        "psO": ctx.psum(f"{tag}psO", [P, TCH]),
        "lgS": ctx.sbuf(f"{tag}lgS", [8, 2 * TCH], F32),
        "rbf": ctx.sbuf(f"{tag}rbf", [8, 2 * TCH], F32),
        "LG": ctx.sbuf(f"{tag}LG", [P, 2 * 32], F32),
        "Mx": ctx.sbuf(f"{tag}Mx", [P, 2 * 4], F32),
        "mask01": ctx.sbuf(f"{tag}mask01", [P, 2 * 36], BF16),
        "maskT": ctx.sbuf(f"{tag}maskT", [9, 2 * TCH], BF16),
        "H": ctx.sbuf(f"{tag}H", [P, NE * 2 * TCH], BF16),
        "sst": [ctx.sem(f"{tag}st0"), ctx.sem(f"{tag}st1")],
        "pstF_g": [0], "hg": {0: 0, 1: 0}, "og": [0],
        "mg": [0, 0], "hwg": [0],
    }
    emit_l1ffn(ctx, consts, th, aps, n_chunks, pre)
    for s in pre["sst"]:
        s.wait(ctx.sp)


def build_l1ffn(n_chunks=8):
    nc = bass.Bass("TRN2", target_bir_lowering=False)
    seq = n_chunks * TCH
    rin = nc.dram_tensor("rin", [NB, D, seq], F32, kind="ExternalInput").ap()
    Wg = nc.dram_tensor("Wg", [NE, D], F32, kind="ExternalInput").ap()
    ln_scale = nc.dram_tensor("ln_scale", [NE, D], F32, kind="ExternalInput").ap()
    ln_bias = nc.dram_tensor("ln_bias", [NE, D], F32, kind="ExternalInput").ap()
    W1 = nc.dram_tensor("W1", [NE, D, HID], F32, kind="ExternalInput").ap()
    b1 = nc.dram_tensor("b1", [NE, HID], F32, kind="ExternalInput").ap()
    W2 = nc.dram_tensor("W2", [NE, HID, D], F32, kind="ExternalInput").ap()
    b2 = nc.dram_tensor("b2", [NE, D], F32, kind="ExternalInput").ap()
    rout = nc.dram_tensor("rout", [NB, D, seq], F32, kind="ExternalOutput").ap()
    caps = declare_consts(nc)
    caps.update(declare_moe_consts(nc))

    es = contextlib.ExitStack()
    with es:
        ctx = Ctx(nc, es)
        consts = load_consts(ctx, caps)
        aps = {"rin": rin, "rout": rout, "Wg": Wg, "ln_scale": ln_scale,
               "ln_bias": ln_bias, "W1": W1, "b1": b1, "W2": W2, "b2": b2,
               "moepen": caps["moepen"]}
        phase_l1ffn(ctx, consts, aps, n_chunks)
    return nc


NH = 16
HD = 64
def l1aa_consts():
    eh = np.zeros((P, KC * NH), np.float32)
    ehb = np.zeros((NH, KC * P), np.float32)
    for oc in range(KC):
        for p in range(P):
            h = 2 * oc + (p // HD)
            eh[p, oc * NH + h] = 1.0
            ehb[h, oc * P + p] = 1.0
    return {"c_eh": eh.astype(ml_dtypes.bfloat16),
            "c_ehb": ehb.astype(ml_dtypes.bfloat16)}


def declare_l1aa_consts(nc):
    return {"eh": nc.dram_tensor("c_eh", [P, KC * NH], BF16, kind="ExternalInput").ap(),
            "ehb": nc.dram_tensor("c_ehb", [NH, KC * P], BF16, kind="ExternalInput").ap()}


def load_col_f32(ctx, dst, col, src_1d, sem):
    sem.inc(ctx.pool.dma_start(
        out=dst[:, col:col + 1],
        in_=src_1d.rearrange("(p one) -> p one", one=1)), 16)


def emit_l1aa(ctx, consts, th, aps, n_chunks, pre):
    nc = ctx.nc
    wT = {}
    stg_sh = ctx.sbuf("stg_sh", [P, D], BF16)
    for nm in ("g", "r", "k", "v", "o"):
        wT[nm] = load_weight_T_big(ctx, consts, th, aps["W" + nm], "w" + nm, stg=stg_sh)
    sb = ctx.sem("sb_l1aa")
    mcol = ctx.sbuf("mcol", [P, 4 * KC], F32)
    bcol = ctx.sbuf("bcol", [P, 5 * KC], F32)
    wcol = ctx.sbuf("wcol", [P, KC], F32)
    ucol = ctx.sbuf("ucol", [P, KC], F32)
    deccol = ctx.sbuf("deccol", [P, KC], F32)
    nmcol = ctx.sbuf("nmcol", [P, 4 * KC], F32)
    for i, nm in enumerate(("mg", "mr", "mk", "mv")):
        for j in range(KC):
            load_col_f32(ctx, mcol, i * KC + j, aps[nm][j * P:(j + 1) * P], sb)
    for i, nm in enumerate(("bg", "br", "bk", "bv", "bo")):
        for j in range(KC):
            load_col_f32(ctx, bcol, i * KC + j, aps[nm][j * P:(j + 1) * P], sb)
    for j in range(KC):
        load_col_f32(ctx, wcol, j, aps["w"].rearrange("h d -> (h d)")[j * P:(j + 1) * P], sb)
        load_col_f32(ctx, ucol, j, aps["u"].rearrange("h d -> (h d)")[j * P:(j + 1) * P], sb)
    ehs = ctx.sbuf("ehs", [P, KC * NH], BF16)
    ehbs = ctx.sbuf("ehbs", [NH, KC * P], BF16)
    sb.inc(ctx.pool.dma_start(out=ehs[:, :], in_=aps["eh"][:, :]), 16)
    sb.inc(ctx.pool.dma_start(out=ehbs[:, :], in_=aps["ehb"][:, :]), 16)
    sb.wait(ctx.act)
    sb.wait(ctx.dve)
    ctx.s_a.inc(ctx.act.activation(out=deccol[:, :], in_=wcol[:, :], func=AF.Exp))
    ctx.s_a.wait(ctx.act)
    ctx.s_a.inc(ctx.act.activation(out=deccol[:, :], in_=deccol[:, :], func=AF.Exp,
                                   scale=-1.0))
    ctx.s_v.inc(ctx.dve.tensor_scalar(out=nmcol[:, :], in0=mcol[:, :], scalar1=-1.0,
                                      scalar2=1.0, op0=ALU.mult, op1=ALU.add))
    setup_a, setup_v = ctx.s_a.n, ctx.s_v.n

    ln = pre["ln"]
    psA, psB_ = pre["psA"], pre["psB"]
    psH, psH2 = pre["psH"], pre["psH2"]
    MIX = pre["MIX"]
    XSH0 = pre["XSH0"]
    SG = pre["SG"]
    RW = pre["RW"]
    RWN = pre["RWN"]
    Kt, Vt, Rt = pre["Kt"], pre["Vt"], pre["Rt"]
    KVt, WUt = pre["KVt"], pre["WUt"]
    DEC = pre["DEC"]
    SQr = pre["SQr"]
    hst = pre["hst"]
    hrb = pre["hrb"]
    hbb = pre["hbb"]
    carry = pre["carry"]
    prevc = pre["prevc"]
    mt = pre["mt"]
    sst = pre["sst"]
    bankg = pre["bankg"]
    nrm = ln["nrm"]

    first = True
    for b in range(NB):
        resid_in = aps["rin"][b]
        resid_out = aps["rout"][b]
        ctx.s_v.wait(ctx.dve)
        ctx.s_v.inc(ctx.dve.memset(carry[:, :], 0.0))
        xlast = pre["xlast"]
        sxl = pre["sxl"][b % 2]
        for j in range(KC):
            sxl.inc(ctx.sp.dma_start(
                out=xlast[:, j:j + 1],
                in_=resid_in[j * P:(j + 1) * P, n_chunks * TCH - 1: n_chunks * TCH],
                allow_slow_non_contiguous=True), 16)
        sxl.wait(ctx.act)
        ctx.s_a.wait(ctx.act)
        xlb = pre["xlb"]
        xlsq = pre["xlsq"]
        ctx.s_a.inc(ctx.act.activation(out=xlb[:, :], in_=xlast[:, :], func=AF.Copy))
        ctx.s_a.inc(ctx.act.activation(out=xlsq[:, :], in_=xlast[:, :], func=AF.Square))
        ctx.s_a.wait(ctx.pe)
        psS = nrm["psS"]
        if nrm["stats_guard"][0]:
            ctx.s_v.wait(ctx.pe, nrm["stats_guard"][0])
        for j in range(KC):
            ctx.s_pe.inc(ctx.pe.matmul(psS[0:1, 0:1], consts["onesK"][:, :],
                                       xlb[:, j:j + 1], start=(j == 0), stop=(j == KC - 1)))
            ctx.s_pe.inc(ctx.pe.matmul(psS[0:1, TCH:TCH + 1], consts["onesK"][:, :],
                                       xlsq[:, j:j + 1], start=(j == 0), stop=(j == KC - 1)))
        ctx.s_pe.wait(ctx.dve)
        lstat = pre["lstat"]
        ctx.s_v.inc(ctx.dve.tensor_scalar(out=lstat[0:1, 0:1], in0=psS[0:1, 0:1],
                                          scalar1=1.0 / D, scalar2=None, op0=ALU.mult))
        ctx.s_v.wait(ctx.dve)
        ctx.s_v.inc(ctx.dve.tensor_tensor(out=lstat[0:1, 1:2], in0=lstat[0:1, 0:1],
                                          in1=lstat[0:1, 0:1], op=ALU.mult))
        ctx.s_v.wait(ctx.dve)
        ctx.s_v.inc(ctx.dve.scalar_tensor_tensor(out=lstat[0:1, 2:3], in0=psS[0:1, TCH:TCH + 1],
                                                 scalar=1.0 / D, in1=lstat[0:1, 1:2],
                                                 op0=ALU.mult, op1=ALU.subtract))
        nrm["stats_guard"][0] = ctx.s_v.n
        ctx.s_v.wait(ctx.act)
        ctx.s_a.wait(ctx.act)
        ctx.s_a.inc(ctx.act.activation(out=lstat[0:1, 3:4], in_=lstat[0:1, 2:3],
                                       func=AF.Ln, bias=1e-5))
        ctx.s_a.wait(ctx.act)
        ctx.s_a.inc(ctx.act.activation(out=lstat[0:1, 3:4], in_=lstat[0:1, 3:4],
                                       func=AF.Exp, scale=-0.5))
        if nrm["bcast_guard"][0]:
            ctx.s_a.wait(ctx.pe, nrm["bcast_guard"][0])
        ctx.s_a.wait(ctx.act)
        ctx.s_a.inc(ctx.act.activation(out=pre["lrow"][0:1, 0:1], in_=lstat[0:1, 3:4],
                                       func=AF.Copy))
        ctx.s_a.inc(ctx.act.activation(out=pre["lrow"][0:1, 1:2], in_=lstat[0:1, 0:1],
                                       func=AF.Copy))
        ctx.s_a.wait(ctx.pe)
        psB0 = nrm["psB"]
        ctx.s_pe.inc(ctx.pe.matmul(psB0[:, 0:2], consts["ones_row"][:, :],
                                   pre["lrow"][0:1, 0:2], start=True, stop=True))
        ctx.s_pe.wait(ctx.act)
        lcols = pre["lcols"]
        ctx.s_a.wait(ctx.act)
        ctx.s_a.inc(ctx.act.activation(out=lcols[:, :], in_=psB0[:, 0:2], func=AF.Copy))
        nrm["bcast_guard"][0] = ctx.s_a.n
        ctx.s_a.wait(ctx.dve)
        ctx.s_v.wait(ctx.dve)
        ctx.s_v.inc(ctx.dve.tensor_scalar(out=prevc[:, :], in0=xlb[:, :],
                                          scalar1=lcols[:, 1:2], scalar2=None,
                                          op0=ALU.subtract))
        ctx.s_v.wait(ctx.dve)
        ctx.s_v.inc(ctx.dve.tensor_scalar(out=prevc[:, :], in0=prevc[:, :],
                                          scalar1=lcols[:, 0:1], scalar2=None,
                                          op0=ALU.mult))

        for c in range(n_chunks):
            par = c % 2
            xnofs = 0
            emit_ln_chunk(ctx, consts, resid_in, c, ln,
                          load_guard=[(sst[0], sst[0].n), (sst[1], sst[1].n)])
            xn_ready = ctx.s_v.n
            if first:
                ctx.s_a.wait(ctx.pe, setup_a)
                ctx.s_v.wait(ctx.pe, setup_v)
                sb.wait(ctx.pe)
                first = False

            def XNsl(kc):
                return ln["XN"][:, xnofs + kc * TCH: xnofs + (kc + 1) * TCH]

            xsh_ready = ctx.s_v.n

            gemm_banks = [psA, psB_]
            evac = {}

            def mix_and_gemm(mi, nm, k_):
                ctx.s_pe.wait(ctx.dve)
                ctx.s_v.wait(ctx.dve, xn_ready)
                for kc in range(KC):
                    mc = mcol[:, mi * KC + kc: mi * KC + kc + 1]
                    nmc = nmcol[:, mi * KC + kc: mi * KC + kc + 1]
                    ctx.s_a.wait(ctx.act)
                    ctx.s_v.wait(ctx.act)
                    ctx.s_a.inc(ctx.act.activation(
                        out=mt[:, 1:TCH],
                        in_=ln["XN"][:, xnofs + kc * TCH: xnofs + (kc + 1) * TCH - 1],
                        func=AF.Identity, bias=0.0, scale=nmc))
                    ctx.s_a.wait(ctx.dve)
                    ctx.s_v.inc(ctx.dve.scalar_tensor_tensor(
                        out=MIX[:, kc * TCH + 1:(kc + 1) * TCH],
                        in0=ln["XN"][:, xnofs + kc * TCH + 1: xnofs + (kc + 1) * TCH],
                        scalar=mc, in1=mt[:, 1:TCH], op0=ALU.mult, op1=ALU.add))
                    ctx.s_v.wait(ctx.dve)
                    ctx.s_v.inc(ctx.dve.tensor_scalar(
                        out=MIX[:, kc * TCH:kc * TCH + 1], in0=prevc[:, kc:kc + 1],
                        scalar1=nmc, scalar2=None, op0=ALU.mult))
                    ctx.s_v.wait(ctx.dve)
                    ctx.s_v.inc(ctx.dve.scalar_tensor_tensor(
                        out=MIX[:, kc * TCH:kc * TCH + 1],
                        in0=ln["XN"][:, xnofs + kc * TCH: xnofs + kc * TCH + 1],
                        scalar=mc, in1=MIX[:, kc * TCH:kc * TCH + 1],
                        op0=ALU.mult, op1=ALU.add))
                mix_done = ctx.s_v.n
                ctx.s_v.wait(ctx.pe, mix_done)
                for oc in range(KC):
                    bank = gemm_banks[oc % 2]
                    if bankg[oc % 2]:
                        ctx.s_a.wait(ctx.pe, bankg[oc % 2][0])
                        ctx.s_v.wait(ctx.pe, bankg[oc % 2][1])
                    for kc in range(KC):
                        ctx.s_pe.inc(ctx.pe.matmul(
                            bank[:, :], wT[nm][:, (kc * KC + oc) * P:(kc * KC + oc + 1) * P],
                            MIX[:, kc * TCH:(kc + 1) * TCH],
                            start=(kc == 0), stop=(kc == KC - 1)))
                    k_(oc, bank)
                return mix_done

            def ev_g(oc, bank):
                ctx.s_pe.wait(ctx.act)
                ctx.s_a.inc(ctx.act.activation(out=SG[:, oc * TCH:(oc + 1) * TCH],
                                               in_=bank[:, :], func=AF.Sigmoid,
                                               bias=bcol[:, 0 * KC + oc:0 * KC + oc + 1],
                                               scale=1.0))
                bankg[oc % 2] = (ctx.s_a.n, 0)

            mix_and_gemm(0, "g", ev_g)

            KB, VB = pre["KB"], pre["VB"]

            def ev_k(oc, bank):
                ctx.s_pe.wait(ctx.act)
                ctx.s_a.inc(ctx.act.activation(out=KB[:, oc * TCH:(oc + 1) * TCH],
                                               in_=bank[:, :], func=AF.Identity,
                                               bias=bcol[:, 2 * KC + oc:2 * KC + oc + 1],
                                               scale=1.0))
                bankg[oc % 2] = (ctx.s_a.n, 0)

            def ev_v(oc, bank):
                ctx.s_pe.wait(ctx.act)
                ctx.s_a.inc(ctx.act.activation(out=VB[:, oc * TCH:(oc + 1) * TCH],
                                               in_=bank[:, :], func=AF.Identity,
                                               bias=bcol[:, 3 * KC + oc:3 * KC + oc + 1],
                                               scale=1.0))
                bankg[oc % 2] = (ctx.s_a.n, 0)

            def ev_r(oc, bank):
                ctx.s_pe.wait(ctx.act)
                ctx.s_a.inc(ctx.act.activation(out=Rt[:, oc * TCH:(oc + 1) * TCH],
                                               in_=bank[:, :], func=AF.Identity,
                                               bias=bcol[:, 1 * KC + oc:1 * KC + oc + 1],
                                               scale=1.0))
                bankg[oc % 2] = (ctx.s_a.n, 0)

            mix_and_gemm(2, "k", ev_k)
            mix_and_gemm(3, "v", ev_v)
            k_done = ctx.s_a.n
            mix_and_gemm(1, "r", ev_r)
            r_done = ctx.s_a.n

            ctx.s_a.wait(ctx.dve, k_done)
            for oc in range(KC):
                osl = slice(oc * TCH, (oc + 1) * TCH)
                ctx.s_v.wait(ctx.dve)
                ctx.s_v.inc(ctx.dve.tensor_tensor(out=KVt[:, :], in0=KB[:, osl],
                                                  in1=VB[:, osl], op=ALU.mult))
                ctx.s_a.wait(ctx.act)
                ctx.s_v.wait(ctx.act)
                ctx.s_a.inc(ctx.act.activation(out=DEC[:, :], in_=pre["onesT"][:, :],
                                               func=AF.Identity, bias=0.0,
                                               scale=deccol[:, oc:oc + 1]))
                ctx.s_a.wait(ctx.dve)
                ctx.s_v.wait(ctx.dve)
                ctx.s_v.inc(ctx.dve.tensor_tensor_scan(
                    out=WUt[:, :], data0=DEC[:, :], data1=KVt[:, :],
                    initial=carry[:, oc:oc + 1], op0=ALU.mult, op1=ALU.add))
                ctx.s_v.wait(ctx.dve)
                ctx.s_v.inc(ctx.dve.tensor_copy(carry[:, oc:oc + 1], WUt[:, TCH - 1:TCH]))
                ctx.s_v.wait(ctx.dve)
                ctx.s_v.inc(ctx.dve.scalar_tensor_tensor(out=pre["WUb"][:, :], in0=KVt[:, :],
                                                         scalar=ucol[:, oc:oc + 1],
                                                         in1=WUt[:, :], op0=ALU.mult,
                                                         op1=ALU.add))
                ctx.s_v.wait(ctx.dve)
                ctx.s_v.inc(ctx.dve.tensor_tensor(out=RW[:, osl], in0=Rt[:, osl],
                                                  in1=pre["WUb"][:, :], op=ALU.mult))
            rw_done = ctx.s_v.n
            ctx.s_v.wait(ctx.act, rw_done)
            if pre["hstat_g"][0]:
                ctx.s_v.wait(ctx.pe, pre["hstat_g"][0])
            for oc in range(KC):
                osl = slice(oc * TCH, (oc + 1) * TCH)
                ctx.s_a.wait(ctx.act)
                ctx.s_pe.wait(ctx.act)
                ctx.s_a.inc(ctx.act.activation(out=SQr[:, :], in_=RW[:, osl], func=AF.Square))
                ctx.s_a.wait(ctx.pe)
                ctx.s_pe.inc(ctx.pe.matmul(psH[0:NH, :], ehs[:, oc * NH:(oc + 1) * NH],
                                           RW[:, osl], start=(oc == 0), stop=(oc == KC - 1)))
                ctx.s_pe.inc(ctx.pe.matmul(psH2[0:NH, :], ehs[:, oc * NH:(oc + 1) * NH],
                                           SQr[:, :], start=(oc == 0), stop=(oc == KC - 1)))
            ctx.s_pe.wait(ctx.dve)
            hm = hst[:, 0:TCH]
            hm2 = hst[:, TCH:2 * TCH]
            hv = hst[:, 2 * TCH:3 * TCH]
            hr = hst[:, 3 * TCH:4 * TCH]
            hmr = hm2
            ctx.s_v.wait(ctx.dve)
            ctx.s_v.inc(ctx.dve.tensor_scalar(out=hm, in0=psH[0:NH, :], scalar1=1.0 / HD,
                                              scalar2=None, op0=ALU.mult))
            ctx.s_v.wait(ctx.dve)
            ctx.s_v.inc(ctx.dve.tensor_tensor(out=hm2, in0=hm, in1=hm, op=ALU.mult))
            ctx.s_v.wait(ctx.dve)
            ctx.s_v.inc(ctx.dve.scalar_tensor_tensor(out=hv, in0=psH2[0:NH, :],
                                                     scalar=1.0 / HD, in1=hm2,
                                                     op0=ALU.mult, op1=ALU.subtract))
            pre["hstat_g"][0] = ctx.s_v.n
            ctx.s_v.wait(ctx.act)
            ctx.s_a.wait(ctx.act)
            ctx.s_a.inc(ctx.act.activation(out=hv, in_=hv, func=AF.Ln, bias=1e-5))
            ctx.s_a.wait(ctx.act)
            ctx.s_a.inc(ctx.act.activation(out=hr, in_=hv, func=AF.Exp, scale=-0.5))
            ctx.s_a.wait(ctx.dve)
            ctx.s_v.inc(ctx.dve.tensor_tensor(out=hmr, in0=hm, in1=hr, op=ALU.mult))
            ctx.s_v.wait(ctx.act)
            ctx.s_a.wait(ctx.act)
            ctx.s_a.inc(ctx.act.activation(out=hrb[:, 0:TCH], in_=hr, func=AF.Copy))
            ctx.s_a.inc(ctx.act.activation(out=hrb[:, TCH:2 * TCH], in_=hmr, func=AF.Copy))
            hrb_done = ctx.s_a.n
            ctx.s_pe.wait(ctx.dve)
            for oc in range(KC):
                osl = slice(oc * TCH, (oc + 1) * TCH)
                if nrm["bcast_guard"][0]:
                    ctx.s_a.wait(ctx.pe, nrm["bcast_guard"][0])
                ctx.s_a.wait(ctx.pe, hrb_done)
                psB0 = nrm["psB"]
                ctx.s_pe.inc(ctx.pe.matmul(psB0[:, 0:TCH], ehbs[:, oc * P:(oc + 1) * P],
                                           hrb[:, 0:TCH], start=True, stop=True))
                ctx.s_pe.wait(ctx.act)
                ctx.s_v.wait(ctx.act)
                ctx.s_a.inc(ctx.act.activation(out=hbb[:, 0:TCH], in_=psB0[:, 0:TCH],
                                               func=AF.Copy))
                hb1 = ctx.s_a.n
                ctx.s_a.wait(ctx.pe, hb1)
                ctx.s_pe.inc(ctx.pe.matmul(psB0[:, 0:TCH], ehbs[:, oc * P:(oc + 1) * P],
                                           hrb[:, TCH:2 * TCH], start=True, stop=True))
                ctx.s_pe.wait(ctx.act)
                ctx.s_a.inc(ctx.act.activation(out=hbb[:, TCH:2 * TCH], in_=psB0[:, 0:TCH],
                                               func=AF.Copy))
                nrm["bcast_guard"][0] = ctx.s_a.n
                ctx.s_a.wait(ctx.dve)
                ctx.s_v.wait(ctx.dve)
                ctx.s_v.inc(ctx.dve.tensor_tensor(out=RWN[:, osl], in0=RW[:, osl],
                                                  in1=hbb[:, 0:TCH], op=ALU.mult))
                ctx.s_v.wait(ctx.dve)
                ctx.s_v.inc(ctx.dve.tensor_tensor(out=RWN[:, osl], in0=RWN[:, osl],
                                                  in1=hbb[:, TCH:2 * TCH], op=ALU.subtract))
                ctx.s_v.wait(ctx.dve)
                ctx.s_v.inc(ctx.dve.tensor_tensor(out=RWN[:, osl], in0=RWN[:, osl],
                                                  in1=SG[:, osl], op=ALU.mult))
            rwn_done = ctx.s_v.n

            ctx.s_v.wait(ctx.dve)
            for kc in range(KC):
                ctx.s_v.inc(ctx.dve.tensor_copy(
                    prevc[:, kc:kc + 1],
                    ln["XN"][:, xnofs + (kc + 1) * TCH - 1: xnofs + (kc + 1) * TCH]))

            ln["xn_guard"][par] = ctx.s_pe.n
            ctx.s_v.wait(ctx.pe, rwn_done)
            for oc in range(KC):
                bank = gemm_banks[oc % 2]
                if bankg[oc % 2]:
                    ctx.s_a.wait(ctx.pe, bankg[oc % 2][0])
                    if bankg[oc % 2][1]:
                        ctx.s_v.wait(ctx.pe, bankg[oc % 2][1])
                for kc in range(KC):
                    ctx.s_pe.inc(ctx.pe.matmul(
                        bank[:, :], wT["o"][:, (kc * KC + oc) * P:(kc * KC + oc + 1) * P],
                        RWN[:, kc * TCH:(kc + 1) * TCH],
                        start=(kc == 0), stop=(kc == KC - 1)))
                ctx.s_pe.wait(ctx.dve)
                osl = slice(oc * TCH, (oc + 1) * TCH)
                ctx.s_v.inc(ctx.dve.scalar_tensor_tensor(
                    out=ln["X"][:, osl], in0=bank[:, :],
                    scalar=bcol[:, 4 * KC + oc:4 * KC + oc + 1],
                    in1=ln["X"][:, osl], op0=ALU.add, op1=ALU.add))
                bankg[oc % 2] = (0, ctx.s_v.n)
                ctx.s_v.wait(ctx.sp)
                sst[par].inc(ctx.sp.dma_start(
                    out=resid_out[oc * P:(oc + 1) * P, c * TCH:(c + 1) * TCH],
                    in_=ln["X"][:, osl]), 16)


def build_l1aa(n_chunks=8):
    nc = bass.Bass("TRN2", target_bir_lowering=False)
    seq = n_chunks * TCH
    rin = nc.dram_tensor("rin", [NB, D, seq], F32, kind="ExternalInput").ap()
    names = ["Wg", "Wr", "Wk", "Wv", "Wo"]
    aps = {"rin": rin}
    for nm in names:
        aps[nm] = nc.dram_tensor(nm, [D, D], F32, kind="ExternalInput").ap()
    for nm in ("bg", "br", "bk", "bv", "bo", "mg", "mr", "mk", "mv"):
        aps[nm] = nc.dram_tensor(nm, [D], F32, kind="ExternalInput").ap()
    aps["w"] = nc.dram_tensor("w", [NH, HD], F32, kind="ExternalInput").ap()
    aps["u"] = nc.dram_tensor("u", [NH, HD], F32, kind="ExternalInput").ap()
    aps["rout"] = nc.dram_tensor("rout", [NB, D, seq], F32, kind="ExternalOutput").ap()
    caps = declare_consts(nc)
    caps.update(declare_l1aa_consts(nc))
    aps["eh"], aps["ehb"] = caps["eh"], caps["ehb"]
    aps["Wgm"] = aps["Wg"]

    es = contextlib.ExitStack()
    with es:
        ctx = Ctx(nc, es)
        consts = load_consts(ctx, caps)
        phase_l1aa(ctx, consts, aps, n_chunks)
    return nc


def phase_l1aa(ctx, consts, aps, n_chunks):
    th = TransposeHelper(ctx, consts)
    nrm = alloc_nrm(ctx, "aa")
    ln = alloc_ln_bufs(ctx, "aa", nrm, nslx=1, nslxn=1)
    pre = {
        "ln": ln,
        "psA": ctx.psum("psA", [P, TCH]),
        "psB": ctx.psum("psB2", [P, TCH]),
        "psH": ctx.psum("psH", [NH, TCH]),
        "psH2": ctx.psum("psH2", [NH, TCH]),
        "MIX": ctx.sbuf("MIX", [P, KC * TCH], BF16),
        "XSH0": ctx.sbuf("XSH0", [P, TCH], BF16),
        "SG": ctx.sbuf("SG", [P, KC * TCH], BF16),
        "RW": None,
        "RWN": None,
        "KB": ctx.sbuf("KB", [P, KC * TCH], BF16),
        "VB": ctx.sbuf("VB", [P, KC * TCH], BF16),
        "Kt": None, "Vt": None,
        "Rt": ctx.sbuf("Rt", [P, KC * TCH], BF16),
        "KVt": ctx.sbuf("KVt", [P, TCH], F32),
        "WUt": ctx.sbuf("WUt", [P, TCH], F32),
        "WUb": ctx.sbuf("WUb", [P, TCH], BF16),
        "DEC": ctx.sbuf("DEC", [P, TCH], F32),
        "SQr": ctx.sbuf("SQr", [P, TCH], BF16),
        "hst": ctx.sbuf("hst", [NH, 4 * TCH], F32),
        "hrb": ctx.sbuf("hrb", [NH, 2 * TCH], BF16),
        "hbb": ctx.sbuf("hbb", [P, 2 * TCH], BF16),
        "carry": ctx.sbuf("carry", [P, KC], F32),
        "prevc": ctx.sbuf("prevc", [P, KC], BF16),
        "mt": ctx.sbuf("mt", [P, TCH], BF16),
        "onesT": ctx.sbuf("onesT", [P, TCH], F32),
        "xlast": ctx.sbuf("xlast", [P, KC], F32),
        "xlb": ctx.sbuf("xlb", [P, KC], BF16),
        "xlsq": ctx.sbuf("xlsq", [P, KC], BF16),
        "lstat": ctx.sbuf("lstat", [1, 8], F32),
        "lrow": ctx.sbuf("lrow", [1, 2], BF16),
        "lcols": ctx.sbuf("lcols", [P, 2], F32),
        "sst": [ctx.sem("ast0"), ctx.sem("ast1")],
        "sxl": [ctx.sem("sxl0"), ctx.sem("sxl1")],
        "bankg": {0: None, 1: None},
        "hstat_g": [0],
    }
    pre["RWN"] = pre["MIX"]
    pre["RW"] = ln["SQ"]
    ctx.s_v.inc(ctx.dve.memset(pre["onesT"][:, :], 1.0))
    ctx.s_v.wait(ctx.act)
    aps2 = dict(aps)
    aps2["Wgm"] = aps2["Wg"]
    emit_l1aa(ctx, consts, th, aps2, n_chunks, pre)
    for s in pre["sst"]:
        s.wait(ctx.sp)


AB_FF = 3072
AB_FH = AB_FF // 2
AB_HCH = AB_FH // P
EPSB = 1e-6


def phase_barrier(ctx, extra_sems=()):
    for eng in (ctx.pe, ctx.dve, ctx.act, ctx.sp, ctx.pool):
        ctx.s_pe.wait(eng)
        ctx.s_v.wait(eng)
        ctx.s_a.wait(eng)
        for s in extra_sems:
            s.wait(eng)


def load_w_direct(ctx, dst, w_ap, n_in, n_out, sem, col0=0):
    for kc in range(n_in):
        sem.inc(ctx.pool.dma_start(
            out=dst[:, col0 + kc * n_out * P: col0 + (kc + 1) * n_out * P],
            in_=w_ap[kc * P:(kc + 1) * P, :]), 16)


def emit_l1ab(ctx, consts, aps, n_chunks, outer_es):
    nc = ctx.nc
    with contextlib.ExitStack() as es1:
        c1 = Ctx.__new__(Ctx)
        c1.__dict__ = dict(ctx.__dict__)
        c1.es = es1
        nrm = alloc_nrm(c1, "ab1")
        ln = alloc_ln_bufs(c1, "ab1", nrm, nslx=1, nslxn=1)
        ln2 = alloc_ln_bufs(c1, "ab1i", nrm, tiles=False, nslx=1, nslxn=1)
        ln2["SQ"] = ln["SQ"]
        ln2["XN"] = c1.sbuf("lnXN_ab1i", [P, KC * TCH], BF16)
        ln2["X"] = None
        ln2["XB"] = None
        sW = ctx.sem("abW1")
        wg = c1.sbuf("ab_wg", [P, KC * D], BF16)
        wv = c1.sbuf("ab_wv", [P, KC * D], BF16)
        wd = c1.sbuf("ab_wd", [P, KC * D], BF16)
        load_w_direct(c1, wg, aps["Wg"], KC, KC, sW)
        load_w_direct(c1, wv, aps["Wv"], KC, KC, sW)
        load_w_direct(c1, wd, aps["Wd"], KC, KC, sW)
        sB = ctx.sem("abB1")
        gcol = c1.sbuf("ab_gcol", [P, KC], F32)
        bcol = c1.sbuf("ab_bcol", [P, 3 * KC], F32)
        for j in range(KC):
            sB.inc(c1.pool.dma_start(out=gcol[:, j:j + 1],
                                     in_=aps["w_rms_mix"][j * P:(j + 1) * P]
                                     .rearrange("(p one) -> p one", one=1)), 16)
        for i, nm in enumerate(("bg", "bv", "bd")):
            for j in range(KC):
                sB.inc(c1.pool.dma_start(out=bcol[:, i * KC + j:i * KC + j + 1],
                                         in_=aps[nm][j * P:(j + 1) * P]
                                         .rearrange("(p one) -> p one", one=1)), 16)
        ln2["gain_cols"] = gcol
        psA = c1.psum("ab_psA", [P, TCH])
        psB_ = c1.psum("ab_psB", [P, TCH])
        SGt = c1.sbuf("ab_SG", [P, TCH], BF16)
        TVt = c1.sbuf("ab_TV", [P, TCH], BF16)
        XS = c1.sbuf("ab_XS", [P, KC * TCH], F32)
        At = c1.sbuf("ab_At", [P, KC * TCH], F32)
        SCt = c1.sbuf("ab_SC", [P, TCH], F32)
        s_sc = ctx.sem("ab_ssc")
        carry = c1.sbuf("ab_carry", [P, KC], F32)
        sst = [ctx.sem("abst0"), ctx.sem("abst1")]
        bankg = {0: None, 1: None}
        gb = [psA, psB_]
        sW.wait(c1.pe)
        sB.wait(c1.act)
        sB.wait(c1.dve)
        XNF = c1.sbuf("ab_XNF", [P, TCH], F32)
        for b in range(NB):
            rin = aps["rin"][b]
            r1 = aps["r1"][b]
            racc = aps["racc"][b]
            c1.s_v.wait(c1.dve)
            c1.s_v.inc(c1.dve.memset(carry[:, :], 0.0))
            for cc in range(n_chunks):
                par = cc % 2
                emit_ln_chunk(c1, consts, rin, cc, ln,
                              load_guard=[(sst[0], sst[0].n), (sst[1], sst[1].n),
                                          (s_sc, s_sc.n)])
                outer_stats = c1.s_pe.n
                t_ready = c1.s_v.n
                ln2["pre_square_guard"] = (outer_stats, t_ready)
                emit_ln_chunk(c1, consts, None, cc, ln2, eps=EPSB, rms=True,
                              src_sbuf=ln["XN"], src_ofs=0)
                xn_ready = c1.s_v.n

                def XNs(kc):
                    return ln2["XN"][:, kc * TCH:(kc + 1) * TCH]

                c1.s_v.wait(c1.pe, xn_ready)
                for oc in range(KC):
                    if bankg[0]:
                        c1.s_a.wait(c1.pe, bankg[0])
                    for kc in range(KC):
                        c1.s_pe.inc(c1.pe.matmul(psA[:, :],
                                                 wg[:, (kc * KC + oc) * P:(kc * KC + oc + 1) * P],
                                                 XNs(kc), start=(kc == 0), stop=(kc == KC - 1)))
                    if bankg[1]:
                        c1.s_a.wait(c1.pe, bankg[1])
                    for kc in range(KC):
                        c1.s_pe.inc(c1.pe.matmul(psB_[:, :],
                                                 wv[:, (kc * KC + oc) * P:(kc * KC + oc + 1) * P],
                                                 XNs(kc), start=(kc == 0), stop=(kc == KC - 1)))
                    pe_gv = c1.s_pe.n
                    c1.s_pe.wait(c1.act, pe_gv - KC)
                    c1.s_a.wait(c1.act)
                    c1.s_v.wait(c1.act)
                    c1.s_a.inc(c1.act.activation(out=SGt[:, :], in_=psA[:, :],
                                                 func=AF.Sigmoid,
                                                 bias=bcol[:, oc:oc + 1], scale=1.0))
                    bankg[0] = c1.s_a.n
                    c1.s_pe.wait(c1.act, pe_gv)
                    c1.s_a.inc(c1.act.activation(out=TVt[:, :], in_=psB_[:, :],
                                                 func=AF.Tanh,
                                                 bias=bcol[:, KC + oc:KC + oc + 1], scale=1.0))
                    bankg[1] = c1.s_a.n
                    c1.s_a.wait(c1.dve)
                    c1.s_v.inc(c1.dve.tensor_tensor(out=XS[:, oc * TCH:(oc + 1) * TCH],
                                                    in0=SGt[:, :], in1=TVt[:, :],
                                                    op=ALU.mult))
                    c1.s_a.wait(c1.pe, bankg[0])
                    for kc in range(KC):
                        c1.s_pe.inc(c1.pe.matmul(psA[:, :],
                                                 wd[:, (kc * KC + oc) * P:(kc * KC + oc + 1) * P],
                                                 XNs(kc), start=(kc == 0), stop=(kc == KC - 1)))
                    c1.s_pe.wait(c1.act)
                    c1.s_a.wait(c1.act)
                    c1.s_v.wait(c1.act)
                    c1.s_a.inc(c1.act.activation(out=SGt[:, :], in_=psA[:, :],
                                                 func=AF.Sigmoid,
                                                 bias=bcol[:, 2 * KC + oc:2 * KC + oc + 1],
                                                 scale=1.0))
                    bankg[0] = c1.s_a.n
                    c1.s_a.wait(c1.act)
                    c1.s_a.inc(c1.act.activation(out=At[:, oc * TCH:(oc + 1) * TCH],
                                                 in_=SGt[:, :], func=AF.Identity,
                                                 bias=0.001, scale=0.998))
                a_done = c1.s_a.n
                ln["xn_guard"][0] = c1.s_pe.n
                ln2["xn_guard"][0] = c1.s_pe.n
                c1.s_a.wait(c1.dve, a_done)
                for oc in range(KC):
                    osl = slice(oc * TCH, (oc + 1) * TCH)
                    c1.s_v.wait(c1.dve)
                    s_sc.wait(c1.dve, s_sc.n)
                    c1.s_v.inc(c1.dve.tensor_tensor_scan(
                        out=SCt[:, :], data0=At[:, osl], data1=XS[:, osl],
                        initial=carry[:, oc:oc + 1], op0=ALU.mult, op1=ALU.add))
                    c1.s_v.wait(c1.dve)
                    c1.s_v.inc(c1.dve.tensor_copy(carry[:, oc:oc + 1],
                                                  SCt[:, TCH - 1:TCH]))
                    c1.s_a.wait(c1.act)
                    c1.s_v.wait(c1.act)
                    c1.s_a.inc(c1.act.activation(out=XNF[:, :], in_=ln["XN"][:, osl],
                                                 func=AF.Copy))
                    c1.s_a.wait(c1.dve)
                    c1.s_v.wait(c1.dve)
                    c1.s_v.inc(c1.dve.tensor_tensor(out=SCt[:, :], in0=SCt[:, :],
                                                    in1=XNF[:, :], op=ALU.add))
                    c1.s_v.wait(c1.sp)
                    s_sc.inc(c1.sp.dma_start(
                        out=r1[oc * P:(oc + 1) * P, cc * TCH:(cc + 1) * TCH],
                        in_=SCt[:, :]), 16)
                    c1.s_v.wait(c1.dve)
                    c1.s_v.inc(c1.dve.tensor_tensor(out=ln["X"][:, osl], in0=SCt[:, :],
                                                    in1=ln["X"][:, osl], op=ALU.add))
                    c1.s_v.wait(c1.sp)
                    sst[par].inc(c1.sp.dma_start(
                        out=racc[oc * P:(oc + 1) * P, cc * TCH:(cc + 1) * TCH],
                        in_=ln["X"][:, osl]), 16)
        for s in sst:
            s.wait(c1.sp)
        s_sc.wait(c1.sp)
        phase_barrier(c1, extra_sems=sst + [s_sc])

    for half in range(2):
        with contextlib.ExitStack() as es2:
            c2 = Ctx.__new__(Ctx)
            c2.__dict__ = dict(ctx.__dict__)
            c2.es = es2
            nrm = alloc_nrm(c2, f"ab2{half}")
            ln = alloc_ln_bufs(c2, f"ab2{half}", nrm, nslx=1, nslxn=1)
            sW = ctx.sem(f"abW2{half}")
            wgt = c2.sbuf(f"ab_wgt{half}", [P, KC * AB_HCH * P], BF16)
            wup = c2.sbuf(f"ab_wup{half}", [P, KC * AB_HCH * P], BF16)
            wot = c2.sbuf(f"ab_wot{half}", [P, AB_HCH * KC * P], BF16)
            for kc in range(KC):
                sW.inc(c2.pool.dma_start(
                    out=wgt[:, kc * AB_HCH * P:(kc + 1) * AB_HCH * P],
                    in_=aps["W_gate"][kc * P:(kc + 1) * P, half * AB_FH:(half + 1) * AB_FH]), 16)
                sW.inc(c2.pool.dma_start(
                    out=wup[:, kc * AB_HCH * P:(kc + 1) * AB_HCH * P],
                    in_=aps["W_up"][kc * P:(kc + 1) * P, half * AB_FH:(half + 1) * AB_FH]), 16)
            for hc in range(AB_HCH):
                sW.inc(c2.pool.dma_start(
                    out=wot[:, hc * KC * P:(hc + 1) * KC * P],
                    in_=aps["W_out"][half * AB_FH + hc * P: half * AB_FH + (hc + 1) * P, :]), 16)
            sB = ctx.sem(f"abB2{half}")
            gcol = c2.sbuf(f"ab_g2{half}", [P, KC], F32)
            for j in range(KC):
                sB.inc(c2.pool.dma_start(out=gcol[:, j:j + 1],
                                         in_=aps["w_rms_ffn"][j * P:(j + 1) * P]
                                         .rearrange("(p one) -> p one", one=1)), 16)
            ln["gain_cols"] = gcol
            psA = c2.psum(f"ab2_psA{half}", [P, TCH])
            psB_ = c2.psum(f"ab2_psB{half}", [P, TCH])
            psO = c2.psum(f"ab2_psO{half}", [P, TCH])
            SILt = c2.sbuf(f"ab_SIL{half}", [P, TCH], BF16)
            UPt = c2.sbuf(f"ab_UP{half}", [P, TCH], BF16)
            HH = c2.sbuf(f"ab_HH{half}", [P, AB_HCH * TCH], BF16)
            XM = c2.sbuf(f"ab_XM{half}", [P, KC * TCH], F32)
            sxm = [ctx.sem(f"abxm0{half}"), ctx.sem(f"abxm1{half}")]
            sst = [ctx.sem(f"abs2a{half}"), ctx.sem(f"abs2b{half}")]
            bankg = {0: None, 1: None}
            og = [0]
            ogv = [0]
            sW.wait(c2.pe)
            sB.wait(c2.dve)
            sB.wait(c2.act)
            src_r = aps["r1"]
            acc_r = aps["racc"] if half == 0 else aps["rmid"]
            dst_r = aps["rmid"] if half == 0 else aps["rout"]
            for b in range(NB):
                for cc in range(n_chunks):
                    par = cc % 2
                    emit_ln_chunk(c2, consts, src_r[b], cc, ln, eps=EPSB, rms=True,
                                  load_guard=[(sst[0], sst[0].n), (sst[1], sst[1].n)])
                    xn_ready = c2.s_v.n
                    XMsrc = XM
                    sxm[0].wait(c2.sp, sxm[0].n)
                    sst[0].wait(c2.sp)
                    sst[1].wait(c2.sp)
                    for j in range(KC):
                        sxm[0].inc(c2.sp.dma_start(
                            out=XM[:, j * TCH:(j + 1) * TCH],
                            in_=acc_r[b][j * P:(j + 1) * P, cc * TCH:(cc + 1) * TCH]), 16)

                    def YNs(kc):
                        return ln["XN"][:, kc * TCH:(kc + 1) * TCH]

                    c2.s_v.wait(c2.pe, xn_ready)
                    for hc in range(AB_HCH):
                        if bankg[0]:
                            c2.s_a.wait(c2.pe, bankg[0])
                        for kc in range(KC):
                            c2.s_pe.inc(c2.pe.matmul(
                                psA[:, :], wgt[:, (kc * AB_HCH + hc) * P:(kc * AB_HCH + hc + 1) * P],
                                YNs(kc), start=(kc == 0), stop=(kc == KC - 1)))
                        if bankg[1]:
                            c2.s_a.wait(c2.pe, bankg[1])
                        for kc in range(KC):
                            c2.s_pe.inc(c2.pe.matmul(
                                psB_[:, :], wup[:, (kc * AB_HCH + hc) * P:(kc * AB_HCH + hc + 1) * P],
                                YNs(kc), start=(kc == 0), stop=(kc == KC - 1)))
                        pe_gu = c2.s_pe.n
                        c2.s_pe.wait(c2.act, pe_gu - KC)
                        c2.s_a.wait(c2.act)
                        c2.s_v.wait(c2.act)
                        c2.s_a.inc(c2.act.activation(out=SILt[:, :], in_=psA[:, :],
                                                     func=AF.Silu))
                        bankg[0] = c2.s_a.n
                        c2.s_pe.wait(c2.act, pe_gu)
                        c2.s_a.inc(c2.act.activation(out=UPt[:, :], in_=psB_[:, :],
                                                     func=AF.Copy))
                        bankg[1] = c2.s_a.n
                        c2.s_a.wait(c2.dve)
                        if og[0]:
                            c2.s_pe.wait(c2.dve, og[0])
                        c2.s_v.inc(c2.dve.tensor_tensor(out=HH[:, hc * TCH:(hc + 1) * TCH],
                                                        in0=SILt[:, :], in1=UPt[:, :],
                                                        op=ALU.mult))
                    h_done = c2.s_v.n
                    ln["xn_guard"][0] = c2.s_pe.n
                    c2.s_v.wait(c2.pe, h_done)
                    sxm[0].wait(c2.dve)
                    for oc in range(KC):
                        if ogv[0]:
                            c2.s_v.wait(c2.pe, ogv[0])
                        for hc in range(AB_HCH):
                            c2.s_pe.inc(c2.pe.matmul(
                                psO[:, :], wot[:, (hc * KC + oc) * P:(hc * KC + oc + 1) * P],
                                HH[:, hc * TCH:(hc + 1) * TCH],
                                start=(hc == 0), stop=(hc == AB_HCH - 1)))
                        c2.s_pe.wait(c2.dve)
                        osl = slice(oc * TCH, (oc + 1) * TCH)
                        c2.s_v.inc(c2.dve.tensor_tensor(out=XMsrc[:, osl], in0=psO[:, :],
                                                        in1=XMsrc[:, osl], op=ALU.add))
                        ogv[0] = c2.s_v.n
                        c2.s_v.wait(c2.sp)
                        sst[par].inc(c2.sp.dma_start(
                            out=dst_r[b][oc * P:(oc + 1) * P, cc * TCH:(cc + 1) * TCH],
                            in_=XMsrc[:, osl]), 16)
                    og[0] = c2.s_pe.n
            for s in sst:
                s.wait(c2.sp)
            phase_barrier(c2, extra_sems=sst + sxm)


def build_l1ab(n_chunks=8):
    nc = bass.Bass("TRN2", target_bir_lowering=False)
    seq = n_chunks * TCH
    aps = {"rin": nc.dram_tensor("rin", [NB, D, seq], F32, kind="ExternalInput").ap()}
    for nm, shp in (("Wg", [D, D]), ("Wv", [D, D]), ("Wd", [D, D]),
                    ("W_gate", [D, AB_FF]), ("W_up", [D, AB_FF]), ("W_out", [AB_FF, D])):
        aps[nm] = nc.dram_tensor(nm, shp, F32, kind="ExternalInput").ap()
    for nm in ("bg", "bv", "bd", "w_rms_mix", "w_rms_ffn"):
        aps[nm] = nc.dram_tensor(nm, [D], F32, kind="ExternalInput").ap()
    aps["rout"] = nc.dram_tensor("rout", [NB, D, seq], F32, kind="ExternalOutput").ap()
    aps["r1"] = nc.dram_tensor("r1", [NB, D, seq], F32, kind="Internal").ap()
    aps["racc"] = nc.dram_tensor("racc", [NB, D, seq], F32, kind="Internal").ap()
    aps["rmid"] = nc.dram_tensor("rmid", [NB, D, seq], F32, kind="Internal").ap()
    caps = declare_consts(nc)
    es = contextlib.ExitStack()
    with es:
        ctx = Ctx(nc, es)
        consts = load_consts(ctx, caps)
        emit_l1ab(ctx, consts, aps, n_chunks, es)
    return nc


FN_DH = 4096
FN_HCHT = FN_DH // P
FN_HH = FN_DH // 2
FN_HCH = FN_HH // P


def ffn_consts():
    u0 = np.zeros((64, FN_DH), np.float32)
    for i in range(FN_DH):
        u0[i % 64, i] = 1.0
    v0 = np.zeros((64, FN_DH), np.float32)
    for r in range(64):
        for j in range(64):
            v0[j, 64 * r + j] = 1.0
    return {"c_u0": u0.astype(ml_dtypes.bfloat16), "c_v0": v0.astype(ml_dtypes.bfloat16)}


def declare_ffn_consts(nc):
    return {"u0": nc.dram_tensor("c_u0", [64, FN_DH], BF16, kind="ExternalInput").ap(),
            "v0": nc.dram_tensor("c_v0", [64, FN_DH], BF16, kind="ExternalInput").ap()}



def wait_evac_done(ctx, sc, eng):
    for which in ("evac_state", "evacB_state"):
        if which in sc and sc[which]["sems"]:
            for s in sc[which]["sems"]:
                s.wait(eng)


def evac_store(ctx, sc, smid, ps_slice, dst_ap, rows, cols, which="evac"):
    slots = sc.setdefault(which + "_state", {"k": 0, "sems": None})
    if slots["sems"] is None:
        slots["sems"] = [ctx.sem(f"sev_{which}_{i}") for i in range(4)]
    k = slots["k"] % 4
    slots["k"] += 1
    sem = slots["sems"][k]
    buf = sc[which]
    if which == "evac":
        view = buf[0:rows, k * P: k * P + cols]
    else:
        view = buf[0:rows, k * 64: k * 64 + cols]
    sem.wait(ctx.act)
    ctx.s_pe.wait(ctx.act)
    ctx.s_a.inc(ctx.act.activation(out=view, in_=ps_slice, func=AF.Copy))
    sc["pg"][0] = ctx.s_a.n
    ctx.s_a.wait(ctx.sp)
    sem.inc(ctx.sp.dma_start(out=dst_ap, in_=view), 16)
    smid.inc_count_only = None

def emit_fold(ctx, consts, aps, sc):
    nc = ctx.nc
    th = sc["th"]
    sW = ctx.sem("ffold")

    def build_UV(w_ap, transposed, Udst, Vdst):
        sc["s_uv"].wait(ctx.dve)
        ctx.s_pe.wait(ctx.dve)
        ctx.s_v.wait(ctx.dve)
        ctx.s_v.inc(ctx.dve.tensor_copy(Udst[:, :], sc["u0"][:, :]))
        ctx.s_v.inc(ctx.dve.tensor_copy(Vdst[:, :], sc["v0"][:, :]))
        stages_u = list(range(6))
        stages_v = list(range(6, 12))
        if transposed:
            stages_u, stages_v = stages_u[::-1], stages_v[::-1]
        for tens, stages, within in ((Udst, stages_u, True), (Vdst, stages_v, False)):
            for s in stages:
                rowb, Wb, psW = sc["row"], sc["wb"], sc["psW"]
                sW.wait(ctx.pool, sW.n)
                ctx.s_pe.wait(ctx.pool)
                sW.inc(ctx.pool.dma_start(
                    out=rowb[:, :],
                    in_=w_ap[s].rearrange("a b c -> (a b c)")
                    .rearrange("(one f) -> one f", one=1)), 16)
                sW.wait(ctx.pe)
                if sc["wbg"][0]:
                    ctx.s_v.wait(ctx.pe, sc["wbg"][0])
                for i in range(16):
                    if sc["psWg"][0]:
                        ctx.s_a.wait(ctx.pe, sc["psWg"][0])
                    ctx.s_pe.inc(ctx.pe.matmul(psW[0:64, :], consts["ones_row"][:, 0:64],
                                               rowb[0:1, i * 512:(i + 1) * 512],
                                               start=True, stop=True))
                    ctx.s_pe.wait(ctx.act)
                    ctx.s_a.inc(ctx.act.activation(out=Wb[:, i * 512:(i + 1) * 512],
                                                   in_=psW[0:64, :], func=AF.Copy))
                    sc["psWg"][0] = ctx.s_a.n
                ctx.s_a.wait(ctx.dve)
                stx = 2 ** s
                if within:
                    nb_, t_ = 32 // stx, stx
                    fdims = [[64, 64], [2 * stx, nb_], [1, t_]]
                    b_off = stx
                    wdims = [[4 * 32, 64], [4 * stx, nb_], [4, t_]]
                else:
                    stb = stx // 64
                    nb_, t_ = 32 // stb, stb
                    fdims = [[64, 64], [2 * stb, nb_], [1, t_]]
                    b_off = stb
                    wdims = [[4, 64], [4 * 64 * stb, nb_], [4 * 64, t_]]
                w00o, w01o, w10o, w11o = 0, 1, 2, 3
                if transposed:
                    w01o, w10o = 2, 1
                PA = [[FN_DH, 64]]
                PW = [[4 * 2048, 64]]
                Ua = bass.AP(tens, 0, PA + fdims)
                Ub = bass.AP(tens, b_off, PA + fdims)
                Ta = bass.AP(sc["tmp1"], 0, PA + fdims)
                Tb = bass.AP(sc["tmp1"], b_off, PA + fdims)
                T2a = bass.AP(sc["tmp2"], 0, PA + fdims)
                T2b = bass.AP(sc["tmp2"], b_off, PA + fdims)
                W00 = bass.AP(Wb, w00o, PW + wdims)
                W01 = bass.AP(Wb, w01o, PW + wdims)
                W10 = bass.AP(Wb, w10o, PW + wdims)
                W11 = bass.AP(Wb, w11o, PW + wdims)
                for (o, i0, i1) in ((Ta, W00, Ua), (Tb, W01, Ub),
                                    (T2a, W11, Ub), (T2b, W10, Ua)):
                    ctx.s_v.wait(ctx.dve)
                    ctx.s_v.inc(ctx.dve.tensor_tensor(out=o, in0=i0, in1=i1, op=ALU.mult))
                ctx.s_v.wait(ctx.dve)
                ctx.s_v.inc(ctx.dve.tensor_tensor(out=Ua, in0=Ta, in1=Tb, op=ALU.add))
                ctx.s_v.wait(ctx.dve)
                ctx.s_v.inc(ctx.dve.tensor_tensor(out=Ub, in0=T2b, in1=T2a, op=ALU.add))
                sc["wbg"][0] = ctx.s_v.n

    build_UV(aps["up_w"], False, sc["U"], sc["V"])
    stg = sc["stg"]
    sL = ctx.sem("ffld2")
    wexpT_mc = sc["wexpT_mc"]
    ctx.s_v.wait(ctx.pe)
    smid = ctx.sem("fmid")
    for mc in range(KC):
        ctx.s_pe.wait(ctx.pool)
        sL.inc(ctx.pool.dma_start(out=stg[:, :], in_=aps["W_exp"][mc * P:(mc + 1) * P, :]), 16)
        sL.wait(ctx.pe)
        for j in range(64):
            th.transpose(stg[:, j * 64:(j + 1) * 64],
                         wexpT_mc[:, j * P:(j + 1) * P], rows=P, cols=64)
        ctx.s_a.wait(ctx.pe)
        for j in range(64):
            if sc["pg"][0]:
                ctx.s_a.wait(ctx.pe, sc["pg"][0])
            ctx.s_pe.inc(ctx.pe.matmul(
                sc["psF"][0:64, 0:P],
                sc["U"][:, j * 64:(j + 1) * 64],
                wexpT_mc[:, j * P:(j + 1) * P],
                start=True, stop=True))
            evac_store(ctx, sc, smid, sc["psF"][0:64, 0:P],
                       aps["wmid"][j * 64:(j + 1) * 64, mc * P:(mc + 1) * P], 64, P)
    for j in range(64):
        if sc["pg"][0]:
            ctx.s_a.wait(ctx.pe, sc["pg"][0])
        ctx.s_pe.inc(ctx.pe.matmul(
            sc["psF"][0:64, 0:1],
            sc["U"][:, j * 64:(j + 1) * 64],
            sc["bcolT"][0:64, j:j + 1],
            start=True, stop=True))
        evac_store(ctx, sc, smid, sc["psF"][0:64, 0:1],
                   aps["bmid"][j * 64:(j + 1) * 64, 0:1], 64, 1)
    sB = ctx.sem("fB")
    for r in range(64):
        wait_evac_done(ctx, sc, ctx.pool)
        ctx.s_a.wait(ctx.pool)
        ctx.s_pe.wait(ctx.pool)
        sB.inc(ctx.pool.dma_start(
            out=sc["lhsW"][:, 0:D],
            in_=aps["wmid"].rearrange("(q g) m -> q g m", g=64)[:, r, :]), 16)
        sB.inc(ctx.pool.dma_start(
            out=sc["lhsB"][0:64, 0:1],
            in_=aps["bmid"].rearrange("(q g) one -> q g one", g=64)[:, r, :],
            allow_slow_non_contiguous=True), 16)
        sB.wait(ctx.pe)
        for mc in range(KC):
            if sc["pg"][0]:
                ctx.s_a.wait(ctx.pe, sc["pg"][0])
            ctx.s_pe.inc(ctx.pe.matmul(
                sc["psF"][0:P, 0:64],
                sc["lhsW"][0:64, mc * P:(mc + 1) * P],
                sc["V"][:, r * 64:(r + 1) * 64],
                start=True, stop=True))
            evac_store(ctx, sc, smid, sc["psF"][0:P, 0:64],
                       aps["w1eff"][mc * P:(mc + 1) * P, r * 64:(r + 1) * 64],
                       P, 64, which="evacB")
        if sc["pg"][0]:
            ctx.s_a.wait(ctx.pe, sc["pg"][0])
        ctx.s_pe.inc(ctx.pe.matmul(
            sc["psF"][0:1, 0:64], sc["lhsB"][0:64, 0:1],
            sc["V"][:, r * 64:(r + 1) * 64], start=True, stop=True))
        evac_store(ctx, sc, smid, sc["psF"][0:1, 0:64],
                   aps["b1eff"][r:r + 1, 0:64], 1, 64, which="evacB")

    build_UV(aps["down_w"], True, sc["U"], sc["V"])
    sD = ctx.sem("fD")
    ctx.s_v.wait(ctx.pe)
    for r in range(64):
        ctx.s_pe.wait(ctx.pool)
        wait_evac_done(ctx, sc, ctx.pool)
        sD.inc(ctx.pool.dma_start(
            out=sc["lhsW"][:, 0:D],
            in_=aps["W_con"].rearrange("(q g) m -> q g m", g=64)[:, r, :]), 16)
        sD.wait(ctx.pe)
        for mc in range(KC):
            if sc["pg"][0]:
                ctx.s_a.wait(ctx.pe, sc["pg"][0])
            ctx.s_pe.inc(ctx.pe.matmul(
                sc["psF"][0:64, 0:P],
                sc["V"][:, r * 64:(r + 1) * 64],
                sc["lhsW"][0:64, mc * P:(mc + 1) * P],
                start=True, stop=True))
            evac_store(ctx, sc, smid, sc["psF"][0:64, 0:P],
                       aps["wmid"][r * 64:(r + 1) * 64, mc * P:(mc + 1) * P], 64, P)
    for j in range(64):
        ctx.s_pe.wait(ctx.pool)
        wait_evac_done(ctx, sc, ctx.pool)
        sD.inc(ctx.pool.dma_start(
            out=sc["lhsW"][:, 0:D],
            in_=aps["wmid"].rearrange("(q g) m -> q g m", g=64)[:, j, :]), 16)
        sD.wait(ctx.pe)
        for mc in range(KC):
            if sc["pg"][0]:
                ctx.s_a.wait(ctx.pe, sc["pg"][0])
            ctx.s_pe.inc(ctx.pe.matmul(
                sc["psF"][0:64, 0:P],
                sc["U"][:, j * 64:(j + 1) * 64],
                sc["lhsW"][0:64, mc * P:(mc + 1) * P],
                start=True, stop=True))
            evac_store(ctx, sc, smid, sc["psF"][0:64, 0:P],
                       aps["w2eff"][j * 64:(j + 1) * 64, mc * P:(mc + 1) * P], 64, P)
    for which in ("evac_state", "evacB_state"):
        if which in sc:
            for s in sc[which]["sems"]:
                s.wait(ctx.sp)
    sB.wait(ctx.sp)
    sD.wait(ctx.sp)
    return


def emit_ffn_apply(ctx, consts, aps, n_chunks, rin_key, rout_key, tagp):
    nc = ctx.nc
    for half in range(2):
        with contextlib.ExitStack() as es2:
            c2 = Ctx.__new__(Ctx)
            c2.__dict__ = dict(ctx.__dict__)
            c2.es = es2
            tag = f"{tagp}{half}"
            nrm = alloc_nrm(c2, tag)
            ln = alloc_ln_bufs(c2, tag, nrm, nslx=1, nslxn=1)
            sW = ctx.sem(f"fW_{tag}")
            w1 = c2.sbuf(f"f_w1_{tag}", [P, KC * FN_HCH * P], BF16)
            w2 = c2.sbuf(f"f_w2_{tag}", [P, FN_HCH * KC * P], BF16)
            for kc in range(KC):
                sW.inc(c2.pool.dma_start(
                    out=w1[:, kc * FN_HCH * P:(kc + 1) * FN_HCH * P],
                    in_=aps["w1eff"][kc * P:(kc + 1) * P,
                                     half * FN_HH:(half + 1) * FN_HH]), 16)
            for hc in range(FN_HCH):
                r0 = (half * FN_HH + hc * P) // 64
                for g in range(2):
                    sW.inc(c2.pool.dma_start(
                        out=w2[:, hc * KC * P:(hc + 1) * KC * P][g * 64:(g + 1) * 64, :],
                        in_=aps["w2eff"].rearrange("(j g) m -> j g m", g=64)
                        [:, r0 + g, :]), 16)
            sBl = ctx.sem(f"fb_{tag}")
            b1c = c2.sbuf(f"f_b1c_{tag}", [P, FN_HCH], F32)
            ub = c2.sbuf(f"f_ub_{tag}", [P, FN_HCH], F32)
            b1r = c2.sbuf(f"f_b1r_{tag}", [P, FN_HCH], F32)
            for hc in range(FN_HCH):
                pr0 = half * FN_HH + hc * P
                sBl.inc(c2.pool.dma_start(
                    out=b1c[:, hc:hc + 1],
                    in_=aps["b1eff"].rearrange("a b -> (a b)")[pr0:pr0 + P]
                    .rearrange("(p one) -> p one", one=1)), 16)
                for g in range(2):
                    r_ = pr0 // 64 + g
                    sBl.inc(c2.pool.dma_start(
                        out=ub[g * 64:(g + 1) * 64, hc:hc + 1],
                        in_=aps["up_bias"].rearrange("(j r) -> j r", r=64)[:, r_]
                        .rearrange("(p one) -> p one", one=1),
                        allow_slow_non_contiguous=True), 16)
            sBl.wait(c2.dve)
            c2.s_v.wait(c2.dve)
            c2.s_v.inc(c2.dve.tensor_tensor(out=b1r[:, :], in0=b1c[:, :], in1=ub[:, :],
                                            op=ALU.add))
            b1_ready = c2.s_v.n
            b2c = c2.sbuf(f"f_b2c_{tag}", [P, KC], F32)
            if half == 1:
                sB2 = ctx.sem(f"fb2_{tag}")
                dcol = c2.sbuf(f"f_dc_{tag}", [P, KC], F32)
                for j in range(KC):
                    sB2.inc(c2.pool.dma_start(
                        out=b2c[:, j:j + 1],
                        in_=aps["b_con"][j * P:(j + 1) * P]
                        .rearrange("(p one) -> p one", one=1)), 16)
                    sB2.inc(c2.pool.dma_start(
                        out=dcol[:, j:j + 1],
                        in_=aps["down_bias"][j * P:(j + 1) * P]
                        .rearrange("(p one) -> p one", one=1)), 16)
                sB2.wait(c2.dve)
                c2.s_v.wait(c2.dve)
                c2.s_v.inc(c2.dve.tensor_tensor(out=b2c[:, :], in0=b2c[:, :],
                                                in1=dcol[:, :], op=ALU.add))
            b2_ready = c2.s_v.n
            psA = c2.psum(f"f_psA_{tag}", [P, TCH])
            psO = c2.psum(f"f_psO_{tag}", [P, TCH])
            HHb = c2.sbuf(f"f_HH_{tag}", [P, FN_HCH * TCH], BF16)
            XM = c2.sbuf(f"f_XM_{tag}", [P, KC * TCH], F32)
            sxm = [ctx.sem(f"fxm_{tag}")]
            sst = [ctx.sem(f"fsa_{tag}"), ctx.sem(f"fsb_{tag}")]
            bankg = [0]
            og = [0]
            ogv = [0]
            sW.wait(c2.pe)
            c2.s_v.wait(c2.pe, b2_ready)
            src_r = aps[rin_key]
            acc_r = aps[rin_key] if half == 0 else aps["rmid2" if tagp == "B" else "rmid"]
            dst_r = aps["rmid2" if tagp == "B" else "rmid"] if half == 0 else aps[rout_key]
            for b in range(NB):
                for cc in range(n_chunks):
                    par = cc % 2
                    emit_ln_chunk(c2, consts, src_r[b], cc, ln,
                                  load_guard=[(sst[0], sst[0].n), (sst[1], sst[1].n)])
                    xn_ready = c2.s_v.n
                    if half == 0:
                        XMsrc = ln["X"]
                    else:
                        XMsrc = XM
                        sxm[0].wait(c2.sp, sxm[0].n)
                        sst[0].wait(c2.sp)
                        sst[1].wait(c2.sp)
                        for j in range(KC):
                            sxm[0].inc(c2.sp.dma_start(
                                out=XM[:, j * TCH:(j + 1) * TCH],
                                in_=acc_r[b][j * P:(j + 1) * P, cc * TCH:(cc + 1) * TCH]), 16)

                    def YNs(kc):
                        return ln["XN"][:, kc * TCH:(kc + 1) * TCH]

                    c2.s_v.wait(c2.pe, xn_ready)
                    for hc in range(FN_HCH):
                        if bankg[0]:
                            c2.s_a.wait(c2.pe, bankg[0])
                        for kc in range(KC):
                            c2.s_pe.inc(c2.pe.matmul(
                                psA[:, :], w1[:, (kc * FN_HCH + hc) * P:(kc * FN_HCH + hc + 1) * P],
                                YNs(kc), start=(kc == 0), stop=(kc == KC - 1)))
                        c2.s_pe.wait(c2.act)
                        c2.s_a.wait(c2.act)
                        if og[0]:
                            c2.s_pe.wait(c2.act, og[0])
                        c2.s_v.wait(c2.act, b1_ready)
                        c2.s_a.inc(c2.act.activation(
                            out=HHb[:, hc * TCH:(hc + 1) * TCH], in_=psA[:, :],
                            func=AF.Gelu, bias=b1r[:, hc:hc + 1], scale=1.0))
                        bankg[0] = c2.s_a.n
                    h_done = c2.s_a.n
                    ln["xn_guard"][0] = c2.s_pe.n
                    c2.s_a.wait(c2.pe, h_done)
                    if half == 1:
                        sxm[0].wait(c2.dve)
                    for oc in range(KC):
                        if ogv[0]:
                            c2.s_v.wait(c2.pe, ogv[0])
                        for hc in range(FN_HCH):
                            c2.s_pe.inc(c2.pe.matmul(
                                psO[:, :], w2[:, (hc * KC + oc) * P:(hc * KC + oc + 1) * P],
                                HHb[:, hc * TCH:(hc + 1) * TCH],
                                start=(hc == 0), stop=(hc == FN_HCH - 1)))
                        c2.s_pe.wait(c2.dve)
                        osl = slice(oc * TCH, (oc + 1) * TCH)
                        if half == 1:
                            c2.s_v.inc(c2.dve.scalar_tensor_tensor(
                                out=XMsrc[:, osl], in0=psO[:, :],
                                scalar=b2c[:, oc:oc + 1], in1=XMsrc[:, osl],
                                op0=ALU.add, op1=ALU.add))
                        else:
                            c2.s_v.inc(c2.dve.tensor_tensor(
                                out=XMsrc[:, osl], in0=psO[:, :],
                                in1=XMsrc[:, osl], op=ALU.add))
                        ogv[0] = c2.s_v.n
                        c2.s_v.wait(c2.sp)
                        sst[par].inc(c2.sp.dma_start(
                            out=dst_r[b][oc * P:(oc + 1) * P, cc * TCH:(cc + 1) * TCH],
                            in_=XMsrc[:, osl]), 16)
                    og[0] = c2.s_pe.n
            for s in sst:
                s.wait(c2.sp)
            for eng in (c2.pe, c2.dve, c2.act, c2.sp, c2.pool):
                c2.s_pe.wait(eng)
                c2.s_v.wait(eng)
                c2.s_a.wait(eng)
                for s in sst + sxm:
                    s.wait(eng)


def emit_l0ffn_fold_phase(ctx, consts, aps):
    with contextlib.ExitStack() as esf:
        cf = Ctx.__new__(Ctx)
        cf.__dict__ = dict(ctx.__dict__)
        cf.es = esf
        sc = {
            "th": TransposeHelper(cf, consts),
            "u0": cf.sbuf("f_u0", [64, FN_DH], BF16),
            "v0": cf.sbuf("f_v0", [64, FN_DH], BF16),
            "U": cf.sbuf("f_U", [64, FN_DH], BF16),
            "V": cf.sbuf("f_V", [64, FN_DH], BF16),
            "tmp1": cf.sbuf("f_t1", [64, FN_DH], BF16),
            "tmp2": cf.sbuf("f_t2", [64, FN_DH], BF16),
            "row": cf.sbuf("f_row", [1, 8192], BF16),
            "wb": cf.sbuf("f_wb", [64, 8192], BF16),
            "psW": cf.psum("f_psW", [64, 512]),
            "psF": cf.psum("f_psF", [P, 512]),
            "stg": cf.sbuf("f_stg", [P, FN_DH], BF16),
            "wexpT_mc": cf.sbuf("f_wexpTmc", [64, 64 * P], BF16),
            "bcolT": cf.sbuf("f_bcolT", [64, 64], BF16),
            "evac": cf.sbuf("f_evac", [64, 4 * P], BF16),
            "evacB": cf.sbuf("f_evacB", [P, 4 * 64], BF16),
            "lhsW": cf.sbuf("f_lhsW", [64, D], BF16),
            "lhsB": cf.sbuf("f_lhsB", [64, 1], BF16),
            "wbg": [0],
            "pg": [0],
            "psWg": [0],
        }
        s_uv = ctx.sem("fuv")
        s_uv.inc(cf.sp.dma_start(out=sc["u0"][:, :], in_=aps["u0"][:, :]), 16)
        s_uv.inc(cf.sp.dma_start(out=sc["v0"][:, :], in_=aps["v0"][:, :]), 16)
        sc["s_uv"] = s_uv
        sbc = ctx.sem("fbc")
        for j in range(64):
            sbc.inc(cf.pool.dma_start(
                out=sc["bcolT"][:, j:j + 1],
                in_=aps["b_exp"][j * 64:(j + 1) * 64].rearrange("(p one) -> p one", one=1)), 16)
        sbc.wait(cf.pe)
        emit_fold(cf, consts, aps, sc)
        for eng in (cf.pe, cf.dve, cf.act, cf.sp, cf.pool):
            cf.s_pe.wait(eng)
            cf.s_v.wait(eng)
            cf.s_a.wait(eng)
            wait_evac_done(cf, sc, eng)


def build_l0ffn(n_chunks=8):
    nc = bass.Bass("TRN2", target_bir_lowering=False)
    seq = n_chunks * TCH
    aps = {"rin": nc.dram_tensor("rin", [NB, D, seq], F32, kind="ExternalInput").ap()}
    aps["W_exp"] = nc.dram_tensor("W_exp", [D, FN_DH], F32, kind="ExternalInput").ap()
    aps["b_exp"] = nc.dram_tensor("b_exp", [FN_DH], F32, kind="ExternalInput").ap()
    aps["up_w"] = nc.dram_tensor("up_w", [12, 2048, 2, 2], F32, kind="ExternalInput").ap()
    aps["up_bias"] = nc.dram_tensor("up_bias", [FN_DH], F32, kind="ExternalInput").ap()
    aps["down_w"] = nc.dram_tensor("down_w", [12, 2048, 2, 2], F32, kind="ExternalInput").ap()
    aps["W_con"] = nc.dram_tensor("W_con", [FN_DH, D], F32, kind="ExternalInput").ap()
    aps["b_con"] = nc.dram_tensor("b_con", [D], F32, kind="ExternalInput").ap()
    aps["down_bias"] = nc.dram_tensor("down_bias", [D], F32, kind="ExternalInput").ap()
    aps["rout"] = nc.dram_tensor("rout", [NB, D, seq], F32, kind="ExternalOutput").ap()
    aps["rmid"] = nc.dram_tensor("rmid", [NB, D, seq], F32, kind="Internal").ap()
    aps["wmid"] = nc.dram_tensor("wmid", [FN_DH, D], BF16, kind="Internal").ap()
    aps["bmid"] = nc.dram_tensor("bmid", [FN_DH, 1], BF16, kind="Internal").ap()
    aps["w1eff"] = nc.dram_tensor("w1eff", [D, FN_DH], BF16, kind="Internal").ap()
    aps["b1eff"] = nc.dram_tensor("b1eff", [64, 64], BF16, kind="Internal").ap()
    aps["w2eff"] = nc.dram_tensor("w2eff", [FN_DH, D], BF16, kind="Internal").ap()
    caps = declare_consts(nc)
    caps.update(declare_ffn_consts(nc))
    aps["u0"], aps["v0"] = caps["u0"], caps["v0"]
    es = contextlib.ExitStack()
    with es:
        ctx = Ctx(nc, es)
        consts = load_consts(ctx, caps)
        emit_l0ffn_fold_phase(ctx, consts, aps)
        emit_ffn_apply(ctx, consts, aps, n_chunks, "rin", "rout", "A")
    return nc


N4 = 4096
E3 = 3072


def fft_consts():
    t1 = np.arange(64)
    k1 = np.arange(64)
    ang = 2 * np.pi * np.outer(t1, k1) / 64.0
    cs1 = np.concatenate([np.cos(ang), -np.sin(ang)], axis=1)
    T2 = np.zeros((64, 128, 128), np.float64)
    IA = np.zeros((64, 128, 128), np.float64)
    t2 = np.arange(64)
    k2 = np.arange(64)
    for a1 in range(64):
        th = 2 * np.pi * (np.outer(t2, np.full(64, a1)) / 4096.0 +
                          np.outer(t2, k2) / 64.0)
        wre, wim = np.cos(th), -np.sin(th)
        T2[a1, 0:64, 0:64] = wre
        T2[a1, 0:64, 64:128] = wim
        T2[a1, 64:128, 0:64] = -wim
        T2[a1, 64:128, 64:128] = wre
        b = np.arange(64)
        th2 = 2 * np.pi * (np.outer(k2, np.full(64, a1)) * 0 +
                           np.outer(k2, b) / 64.0)
        th2 = th2 + 2 * np.pi * a1 * b[None, :] / 4096.0
        wre2, wim2 = np.cos(th2) / 4096.0, np.sin(th2) / 4096.0
        IA[a1, 0:64, 0:64] = wre2
        IA[a1, 0:64, 64:128] = wim2
        IA[a1, 64:128, 0:64] = -wim2
        IA[a1, 64:128, 64:128] = wre2
    a = np.arange(64)
    angb = 2 * np.pi * np.outer(k1, a) / 64.0
    pb = np.concatenate([np.cos(angb), -np.sin(angb)], axis=0)
    return {
        "c_cs1": cs1.astype(ml_dtypes.bfloat16),
        "c_t2m": T2.reshape(64 * 128, 128).astype(ml_dtypes.bfloat16),
        "c_iam": IA.reshape(64 * 128, 128).astype(ml_dtypes.bfloat16),
        "c_pbm": pb.astype(ml_dtypes.bfloat16),
    }


def declare_fft_consts(nc):
    return {
        "cs1": nc.dram_tensor("c_cs1", [64, 128], BF16, kind="ExternalInput").ap(),
        "t2m": nc.dram_tensor("c_t2m", [64 * 128, 128], BF16, kind="ExternalInput").ap(),
        "iam": nc.dram_tensor("c_iam", [64 * 128, 128], BF16, kind="ExternalInput").ap(),
        "pbm": nc.dram_tensor("c_pbm", [128, 64], BF16, kind="ExternalInput").ap(),
    }


def emit_l0ab(ctx, consts, aps, n_chunks):
    nc = ctx.nc
    assert n_chunks == 8
    with contextlib.ExitStack() as e1:
        c1 = Ctx.__new__(Ctx)
        c1.__dict__ = dict(ctx.__dict__)
        c1.es = e1
        nrm = alloc_nrm(c1, "ab0")
        ln = alloc_ln_bufs(c1, "ab0", nrm, nslx=1, nslxn=1)
        sW = ctx.sem("pj_w")
        pw = c1.sbuf("pj_pw", [P, KC * E3], BF16)
        for kc in range(KC):
            sW.inc(c1.pool.dma_start(out=pw[:, kc * E3:(kc + 1) * E3],
                                     in_=aps["proj_w"][kc * P:(kc + 1) * P, :]), 16)
        pbrow = c1.sbuf("pj_pb", [1, E3], BF16)
        sW.inc(c1.pool.dma_start(out=pbrow[:, :],
                                 in_=aps["proj_b"].rearrange("(one f) -> one f", one=1)), 16)
        psA = c1.psum("pj_psA", [P, TCH])
        psB_ = c1.psum("pj_psB", [P, TCH])
        EV = c1.sbuf("pj_ev", [P, 2 * TCH], BF16)
        sst = [ctx.sem("pj_s0"), ctx.sem("pj_s1")]
        sev = [ctx.sem("pj_e0"), ctx.sem("pj_e1")]
        bankg = {0: 0, 1: 0}
        sW.wait(c1.pe)
        for b in range(NB):
            for cc in range(8):
                emit_ln_chunk(c1, consts, aps["rin"][b], cc, ln,
                              load_guard=[(sev[0], sev[0].n), (sev[1], sev[1].n)])
                xn_ready = c1.s_v.n
                c1.s_v.wait(c1.pe, xn_ready)
                for tt in range(4):
                    for ec in range(6):
                        k = (tt * 6 + ec) % 2
                        bank = [psA, psB_][k]
                        if bankg[k]:
                            c1.s_a.wait(c1.pe, bankg[k])
                        for kc in range(KC):
                            c1.s_pe.inc(c1.pe.matmul(
                                bank[:, :],
                                ln["XN"][:, kc * TCH + tt * P: kc * TCH + (tt + 1) * P],
                                pw[:, kc * E3 + ec * TCH: kc * E3 + (ec + 1) * TCH],
                                start=(kc == 0), stop=False))
                        c1.s_pe.inc(c1.pe.matmul(
                            bank[:, :], consts["ones_row"][:, :],
                            pbrow[0:1, ec * TCH:(ec + 1) * TCH],
                            start=False, stop=True))
                        c1.s_pe.wait(c1.act)
                        sev[k].wait(c1.act, sev[k].n)
                        c1.s_a.inc(c1.act.activation(
                            out=EV[:, k * TCH:(k + 1) * TCH], in_=bank[:, :],
                            func=AF.Copy))
                        bankg[k] = c1.s_a.n
                        c1.s_a.wait(c1.sp)
                        sev[k].inc(c1.sp.dma_start(
                            out=aps["p_h"][b][cc * TCH + tt * P: cc * TCH + (tt + 1) * P,
                                              ec * TCH:(ec + 1) * TCH],
                            in_=EV[:, k * TCH:(k + 1) * TCH]), 16)
                ln["xn_guard"][0] = c1.s_pe.n
        for eng in (c1.pe, c1.dve, c1.act, c1.sp, c1.pool):
            c1.s_pe.wait(eng); c1.s_v.wait(eng); c1.s_a.wait(eng)
            for s in sev:
                s.wait(eng)

    with contextlib.ExitStack() as e2:
        c2 = Ctx.__new__(Ctx)
        c2.__dict__ = dict(ctx.__dict__)
        c2.es = e2
        sC = ctx.sem("fft_c")
        cs1 = c2.sbuf("fft_cs1", [64, 128], BF16)
        t2m = c2.sbuf("fft_t2m", [P, 64 * 128], BF16)
        iam = c2.sbuf("fft_iam", [P, 64 * 128], BF16)
        pbm = c2.sbuf("fft_pbm", [P, 64], BF16)
        sC.inc(c2.sp.dma_start(out=cs1[:, :], in_=aps["cs1"][:, :]), 16)
        for a_ in range(64):
            sC.inc(c2.sp.dma_start(out=t2m[:, a_ * 128:(a_ + 1) * 128],
                                   in_=aps["t2m"][a_ * 128:(a_ + 1) * 128, :]), 16)
            sC.inc(c2.sp.dma_start(out=iam[:, a_ * 128:(a_ + 1) * 128],
                                   in_=aps["iam"][a_ * 128:(a_ + 1) * 128, :]), 16)
        sC.inc(c2.sp.dma_start(out=pbm[:, :], in_=aps["pbm"][:, :]), 16)
        sC.wait(c2.pe)
        rhs1 = c2.sbuf("fft_r1", [64, 2 * D], BF16)
        rhs2 = c2.sbuf("fft_r2", [P, 2 * D], BF16)
        Xsb = c2.sbuf("fft_X", [64, 2 * D], BF16)
        Fk = c2.sbuf("fft_F", [64, 2 * D], BF16)
        Ysb = c2.sbuf("fft_Y", [P, D], BF16)
        tmpa = c2.sbuf("fft_ta", [64, D], BF16)
        tmpb = c2.sbuf("fft_tb", [64, D], BF16)
        gate = c2.sbuf("fft_gt", [64, D], BF16)
        vout = c2.sbuf("fft_vo", [64, 2 * D], BF16)
        evb = c2.sbuf("fft_ev", [P, 2 * D], BF16)
        fstg = c2.sbuf("fft_fs", [P, D], BF16)
        ps1 = c2.psum("fft_ps1", [P, TCH])
        ps2 = c2.psum("fft_ps2", [P, TCH])
        s_r = [ctx.sem("fft_ra"), ctx.sem("fft_rb")]
        s_f = ctx.sem("fft_fk")
        s_g = ctx.sem("fft_gts")
        s_st = [ctx.sem("fft_sa"), ctx.sem("fft_sb")]
        s_vo = [ctx.sem("fft_va"), ctx.sem("fft_vb")]
        s_fs = ctx.sem("fft_fsm")
        evg = {0: 0, 1: 0}
        bg = {0: 0, 1: 0}
        rg = {0: 0, 1: 0}

        s_fc = ctx.sem("fft_fc")
        for sg_ in range(2):
            for i in range(16):
                s_fc.wait(c2.pool, s_fc.n)
                s_fc.inc(c2.pool.dma_start(
                    out=fstg[:, :],
                    in_=aps["filt"][sg_, 0, i * P:(i + 1) * P, :]), 16)
                s_fc.wait(c2.pool)
                s_fc.inc(c2.pool.dma_start(
                    out=aps["filtb"][sg_, i * P:(i + 1) * P, :],
                    in_=fstg[:, :]), 16)
        s_fc.wait(c2.sp)

        def sp_sync():
            for ss in (s_st[0], s_st[1], s_vo[0], s_vo[1], s_r[0], s_r[1], s_f, s_fs,
                       s_g):
                ss.wait(c2.sp)

        def run_fft_fwd(v_srcs, Fh, filt=False, dstF=None):
            sp_sync()
            k_rows = 32 if filt else 64
            for t2 in range(64):
                par = t2 % 2
                c2.s_pe.wait(c2.sp, rg[par])
                s_r[par].inc(c2.sp.dma_start(
                    out=rhs1[0:k_rows, par * D: (par + 1) * D], in_=v_srcs[t2]), 16)
                s_r[par].wait(c2.pe)
                for cn in range(2):
                    kk = cn
                    if bg[kk]:
                        c2.s_a.wait(c2.pe, bg[kk])
                    bank = [ps1, ps2][kk]
                    c2.s_pe.inc(c2.pe.matmul(
                        bank[:, :], cs1[0:k_rows, :],
                        rhs1[0:k_rows, par * D + cn * TCH: par * D + (cn + 1) * TCH],
                        start=True, stop=True))
                    c2.s_pe.wait(c2.act)
                    if evg[cn]:
                        s_st[cn].wait(c2.act, evg[cn])
                    c2.s_a.inc(c2.act.activation(
                        out=evb[:, cn * D + cn * 0: cn * D + TCH][:, 0:TCH]
                        if False else evb[:, cn * D: cn * D + TCH],
                        in_=bank[:, :], func=AF.Copy))
                    bg[kk] = c2.s_a.n
                    c2.s_a.wait(c2.sp)
                    s_st[cn].inc(c2.sp.dma_start(
                        out=aps["A_h"][t2, :, cn * TCH:(cn + 1) * TCH],
                        in_=evb[:, cn * D: cn * D + TCH]), 16)
                    evg[cn] = s_st[cn].n
                rg[par] = c2.s_pe.n
            sp_sync()
            for k1 in range(64):
                par = k1 % 2
                c2.s_pe.wait(c2.sp, rg[par])
                s_r[par].inc(c2.sp.dma_start(out=rhs2[0:64, par * D:(par + 1) * D],
                                             in_=aps["A_h"][:, k1, :]), 16)
                s_r[par].inc(c2.sp.dma_start(out=rhs2[64:128, par * D:(par + 1) * D],
                                             in_=aps["A_h"][:, 64 + k1, :]), 16)
                s_r[par].wait(c2.pe)
                for cn in range(2):
                    kk = cn
                    if bg[kk]:
                        c2.s_a.wait(c2.pe, bg[kk])
                    bank = [ps1, ps2][kk]
                    c2.s_pe.inc(c2.pe.matmul(
                        bank[:, :], t2m[:, k1 * 128:(k1 + 1) * 128],
                        rhs2[:, par * D + cn * TCH: par * D + (cn + 1) * TCH],
                        start=True, stop=True))
                    c2.s_pe.wait(c2.act)
                    c2.s_v.wait(c2.act)
                    if filt:
                        s_fs.wait(c2.act, s_fs.n)
                    c2.s_a.inc(c2.act.activation(out=Xsb[:, cn * TCH:(cn + 1) * TCH],
                                                 in_=bank[0:64, :], func=AF.Copy))
                    c2.s_a.inc(c2.act.activation(out=Xsb[:, D + cn * TCH: D + (cn + 1) * TCH],
                                                 in_=bank[64:128, :], func=AF.Copy))
                    bg[kk] = c2.s_a.n
                rg[par] = c2.s_pe.n
                if filt:
                    c2.s_a.wait(c2.sp)
                    s_fs.inc(c2.sp.dma_start(out=dstF[k1][0:64, :],
                                             in_=Xsb[:, 0:D]), 16)
                    s_fs.inc(c2.sp.dma_start(out=dstF[k1][64:128, :],
                                             in_=Xsb[:, D:2 * D]), 16)
                    continue
                s_f.wait(c2.sp, s_f.n)
                c2.s_v.wait(c2.sp)
                s_f.inc(c2.sp.dma_start(out=Fk[:, 0:D], in_=Fh[k1][0:64, :]), 16)
                s_f.inc(c2.sp.dma_start(out=Fk[:, D:2 * D], in_=Fh[k1][64:128, :]), 16)
                s_f.wait(c2.dve)
                c2.s_a.wait(c2.dve)
                c2.s_v.wait(c2.dve)
                c2.s_v.inc(c2.dve.tensor_tensor(out=Ysb[0:64, :], in0=Xsb[:, 0:D],
                                                in1=Fk[:, 0:D], op=ALU.mult))
                c2.s_v.inc(c2.dve.tensor_tensor(out=tmpa[:, :], in0=Xsb[:, D:2 * D],
                                                in1=Fk[:, D:2 * D], op=ALU.mult))
                c2.s_v.wait(c2.dve)
                c2.s_v.inc(c2.dve.tensor_tensor(out=Ysb[0:64, :], in0=Ysb[0:64, :],
                                                in1=tmpa[:, :], op=ALU.subtract))
                c2.s_v.wait(c2.dve)
                c2.s_v.inc(c2.dve.tensor_tensor(out=tmpa[:, :], in0=Xsb[:, 0:D],
                                                in1=Fk[:, D:2 * D], op=ALU.mult))
                c2.s_v.inc(c2.dve.tensor_tensor(out=tmpb[:, :], in0=Xsb[:, D:2 * D],
                                                in1=Fk[:, 0:D], op=ALU.mult))
                c2.s_v.wait(c2.dve)
                c2.s_v.inc(c2.dve.tensor_tensor(out=tmpa[:, :], in0=tmpa[:, :],
                                                in1=tmpb[:, :], op=ALU.add))
                c2.s_v.wait(c2.act)
                c2.s_a.wait(c2.act)
                c2.s_a.inc(c2.act.activation(out=Ysb[64:128, :], in_=tmpa[:, :],
                                             func=AF.Copy))
                ya = c2.s_a.n
                yv = c2.s_v.n
                c2.s_v.wait(c2.pe, yv)
                c2.s_a.wait(c2.pe, ya)
                for cn in range(2):
                    kk = cn
                    if bg[kk]:
                        c2.s_a.wait(c2.pe, bg[kk])
                    bank = [ps1, ps2][kk]
                    c2.s_pe.inc(c2.pe.matmul(
                        bank[:, :], iam[:, k1 * 128:(k1 + 1) * 128],
                        Ysb[:, cn * TCH:(cn + 1) * TCH], start=True, stop=True))
                    c2.s_pe.wait(c2.act)
                    if evg[cn]:
                        s_st[cn].wait(c2.act, evg[cn])
                    c2.s_a.inc(c2.act.activation(
                        out=evb[:, cn * D: cn * D + TCH], in_=bank[:, :], func=AF.Copy))
                    bg[kk] = c2.s_a.n
                    c2.s_a.wait(c2.sp)
                    s_st[cn].inc(c2.sp.dma_start(
                        out=aps["G_h"][k1, :, cn * TCH:(cn + 1) * TCH],
                        in_=evb[:, cn * D: cn * D + TCH]), 16)
                    evg[cn] = s_st[cn].n

        def run_passB(p_rb, gate_col0, v_dst):
            sp_sync()
            for bb in range(64):
                par = bb % 2
                c2.s_pe.wait(c2.sp, rg[par])
                s_r[par].inc(c2.sp.dma_start(out=rhs2[0:64, par * D:(par + 1) * D],
                                             in_=aps["G_h"][:, bb, :]), 16)
                s_r[par].inc(c2.sp.dma_start(out=rhs2[64:128, par * D:(par + 1) * D],
                                             in_=aps["G_h"][:, 64 + bb, :]), 16)
                s_g.wait(c2.sp, s_g.n)
                c2.s_v.wait(c2.sp)
                s_g.inc(c2.sp.dma_start(
                    out=gate[:, :], in_=p_rb[bb, :, gate_col0:gate_col0 + D]), 16)
                s_r[par].wait(c2.pe)
                for cn in range(2):
                    kk = cn
                    if bg[kk]:
                        c2.s_a.wait(c2.pe, bg[kk])
                    bank = [ps1, ps2][kk]
                    c2.s_pe.inc(c2.pe.matmul(
                        bank[0:64, :], pbm[:, 0:64],
                        rhs2[:, par * D + cn * TCH: par * D + (cn + 1) * TCH],
                        start=True, stop=True))
                    c2.s_pe.wait(c2.act)
                    s_vo[par].wait(c2.act, s_vo[par].n)
                    c2.s_v.wait(c2.act)
                    c2.s_a.inc(c2.act.activation(
                        out=vout[:, par * D + cn * TCH: par * D + (cn + 1) * TCH],
                        in_=bank[0:64, :], func=AF.Copy))
                    bg[kk] = c2.s_a.n
                rg[par] = c2.s_pe.n
                s_g.wait(c2.dve)
                c2.s_a.wait(c2.dve)
                c2.s_v.wait(c2.dve)
                s_vo[par].wait(c2.dve, s_vo[par].n)
                c2.s_v.inc(c2.dve.tensor_tensor(
                    out=vout[:, par * D:(par + 1) * D],
                    in0=vout[:, par * D:(par + 1) * D], in1=gate[:, :], op=ALU.mult))
                c2.s_v.wait(c2.sp)
                s_vo[par].inc(c2.sp.dma_start(out=v_dst[bb],
                                              in_=vout[:, par * D:(par + 1) * D]), 16)

        for sg_ in range(2):
            f_r = aps["filtb"][sg_].rearrange("(t1 t2) c -> t2 t1 c", t2=64)
            run_fft_fwd([f_r[t2] for t2 in range(64)], None, filt=True,
                        dstF=[aps["F_h"][sg_, k1] for k1 in range(64)])

        th2 = TransposeHelper(c2, consts)
        sxr = ctx.sem("ab0_xr")
        ssr = ctx.sem("ab0_sr")
        sv2 = ctx.sem("ab0_v2")
        Xr = c2.sbuf("ab0_Xr", [P, TCH], F32)
        V2F = c2.sbuf("ab0_V2", [P, TCH], BF16)
        v2t = c2.sbuf("ab0_v2t", [P, P], BF16)
        for b in range(NB):
            p_r = aps["p_h"][b].rearrange("(t1 t2) e -> t2 t1 e", t2=64)
            p_rb = aps["p_h"][b].rearrange("(a b2) e -> b2 a e", b2=64)
            v1_r = aps["v1_h"].rearrange("(t1 t2) c -> t2 t1 c", t2=64)
            v1_rb = aps["v1_h"].rearrange("(a b2) c -> b2 a c", b2=64)
            v2_rb = aps["v2_h"].rearrange("(a b2) c -> b2 a c", b2=64)
            run_fft_fwd([p_r[t2, :, 0:D] for t2 in range(64)],
                        [aps["F_h"][0, k1] for k1 in range(64)])
            run_passB(p_rb, D, [v1_rb[bb] for bb in range(64)])
            run_fft_fwd([v1_r[t2] for t2 in range(64)],
                        [aps["F_h"][1, k1] for k1 in range(64)])
            run_passB(p_rb, 2 * D, [v2_rb[bb] for bb in range(64)])
            for s in s_vo:
                s.wait(c2.sp)
                s.wait(c2.pool)
            for cc in range(8):
                for kc in range(KC):
                    sxr.wait(c2.sp, sxr.n)
                    ssr.wait(c2.sp)
                    sxr.inc(c2.sp.dma_start(
                        out=Xr[:, :],
                        in_=aps["rin"][b][kc * P:(kc + 1) * P,
                                          cc * TCH:(cc + 1) * TCH]), 16)
                    c2.s_v.wait(c2.act)
                    for tt in range(4):
                        sv2.wait(c2.pool, sv2.n)
                        c2.s_pe.wait(c2.pool)
                        sv2.inc(c2.pool.dma_start(
                            out=v2t[:, :],
                            in_=aps["v2_h"][cc * TCH + tt * P: cc * TCH + (tt + 1) * P,
                                            kc * P:(kc + 1) * P]), 16)
                        sv2.wait(c2.pe)
                        th2.transpose(v2t[:, :], V2F[:, tt * P:(tt + 1) * P])
                    c2.s_a.wait(c2.dve)
                    sxr.wait(c2.dve)
                    c2.s_v.wait(c2.dve)
                    c2.s_v.inc(c2.dve.tensor_tensor(out=Xr[:, :], in0=V2F[:, :],
                                                    in1=Xr[:, :], op=ALU.add))
                    c2.s_v.wait(c2.sp)
                    ssr.inc(c2.sp.dma_start(
                        out=aps["rout"][b][kc * P:(kc + 1) * P, cc * TCH:(cc + 1) * TCH],
                        in_=Xr[:, :]), 16)
            ssr.wait(c2.sp)
        for eng in (c2.pe, c2.dve, c2.act, c2.sp, c2.pool):
            c2.s_pe.wait(eng); c2.s_v.wait(eng); c2.s_a.wait(eng)
            ssr.wait(eng)


def build_l0ab(n_chunks=8):
    nc = bass.Bass("TRN2", target_bir_lowering=False)
    seq = 8 * TCH
    aps = {"rin": nc.dram_tensor("rin", [NB, D, seq], F32, kind="ExternalInput").ap()}
    aps["proj_w"] = nc.dram_tensor("proj_w", [D, E3], F32, kind="ExternalInput").ap()
    aps["proj_b"] = nc.dram_tensor("proj_b", [E3], F32, kind="ExternalInput").ap()
    aps["filt"] = nc.dram_tensor("filt", [2, 1, 2048, D], F32, kind="ExternalInput").ap()
    aps["rout"] = nc.dram_tensor("rout", [NB, D, seq], F32, kind="ExternalOutput").ap()
    aps["p_h"] = nc.dram_tensor("p_h", [NB, N4, E3], BF16, kind="Internal").ap()
    aps["v1_h"] = nc.dram_tensor("v1_h", [N4, D], BF16, kind="Internal").ap()
    aps["v2_h"] = nc.dram_tensor("v2_h", [N4, D], BF16, kind="Internal").ap()
    aps["A_h"] = nc.dram_tensor("A_h", [64, 128, D], BF16, kind="Internal").ap()
    aps["G_h"] = nc.dram_tensor("G_h", [64, 128, D], BF16, kind="Internal").ap()
    aps["F_h"] = nc.dram_tensor("F_h", [2, 64, 128, D], BF16, kind="Internal").ap()
    aps["filtb"] = nc.dram_tensor("filtb", [2, 2048, D], BF16, kind="Internal").ap()
    caps = declare_consts(nc)
    caps.update(declare_fft_consts(nc))
    for k in ("cs1", "t2m", "iam", "pbm"):
        aps[k] = caps[k]
    es = contextlib.ExitStack()
    with es:
        ctx = Ctx(nc, es)
        consts = load_consts(ctx, caps)
        emit_l0ab(ctx, consts, aps, 8)
    return nc




_BUILD_CACHE = {}


def _get(name, fn):
    if name not in _BUILD_CACHE:
        _BUILD_CACHE[name] = fn()
    return _BUILD_CACHE[name]


def build_fused(n_chunks=8):
    nc = bass.Bass("TRN2", target_bir_lowering=False)
    seq = n_chunks * TCH
    nph = int(os.environ.get("FUSED_NPH", "9"))
    dbg = "ExternalOutput" if os.environ.get("FUSED_DBG", "") == "1" else "Internal"

    def dr(nm, shp, kind="ExternalInput", dt=F32):
        return nc.dram_tensor(nm, shp, dt, kind=kind).ap()

    A = {
        "rin": dr("rin", [NB, D, seq]),
        "rout": dr("rout", [NB, D, seq], "ExternalOutput"),
        "rA": dr("rA", [NB, D, seq], dbg),
        "rB": dr("rB", [NB, D, seq], dbg),
        "g_rmid": dr("g_rmid", [NB, D, seq], "Internal"),
    }
    for nm, shp in (("aa_W1", [D, D]), ("aa_b1", [D]), ("aa_W2", [16, D]),
                    ("aa_b2", [16]), ("aa_W3", [16, D]), ("aa_b3", [16])):
        A[nm] = dr(nm, shp)
    for nm, shp in (("fn_W_exp", [D, 4096]), ("fn_b_exp", [4096]),
                    ("fn_up_w", [12, 2048, 2, 2]), ("fn_up_bias", [4096]),
                    ("fn_down_w", [12, 2048, 2, 2]), ("fn_W_con", [4096, D]),
                    ("fn_b_con", [D]), ("fn_down_bias", [D])):
        A[nm] = dr(nm, shp)
    for nm, shp in (("fn_wmid", [4096, D]), ("fn_bmid", [4096, 1]),
                    ("fn_w1eff", [D, 4096]), ("fn_b1eff", [64, 64]),
                    ("fn_w2eff", [4096, D])):
        A[nm] = dr(nm, shp, "Internal", BF16)
    A["ab_proj_w"] = dr("ab_proj_w", [D, 3072])
    A["ab_proj_b"] = dr("ab_proj_b", [3072])
    A["ab_filt"] = dr("ab_filt", [2, 1, 2048, D])
    for nm, shp in (("ab_p_h", [NB, 4096, 3072]), ("ab_v1_h", [4096, D]),
                    ("ab_v2_h", [4096, D]), ("ab_A_h", [64, 128, D]),
                    ("ab_G_h", [64, 128, D]), ("ab_F_h", [2, 64, 128, D]),
                    ("ab_filtb", [2, 2048, D])):
        A[nm] = dr(nm, shp, "Internal", BF16)
    for nm in ("Wg", "Wr", "Wk", "Wv", "Wo"):
        A["ra_" + nm] = dr("ra_" + nm, [D, D])
    for nm in ("bg", "br", "bk", "bv", "bo", "mg", "mr", "mk", "mv"):
        A["ra_" + nm] = dr("ra_" + nm, [D])
    A["ra_w"] = dr("ra_w", [NH, HD])
    A["ra_u"] = dr("ra_u", [NH, HD])
    for nm, shp in (("mo_Wg", [NE, D]), ("mo_ln_scale", [NE, D]),
                    ("mo_ln_bias", [NE, D]), ("mo_W1", [NE, D, HID]),
                    ("mo_b1", [NE, HID]), ("mo_W2", [NE, HID, D]),
                    ("mo_b2", [NE, D])):
        A[nm] = dr(nm, shp)
    for nm, shp in (("lb_Wg", [D, D]), ("lb_bg", [D]), ("lb_Wv", [D, D]),
                    ("lb_bv", [D]), ("lb_Wd", [D, D]), ("lb_bd", [D]),
                    ("lb_wmix", [D]), ("lb_wffn", [D]),
                    ("lb_W_gate", [D, 3072]), ("lb_W_up", [D, 3072]),
                    ("lb_W_out", [3072, D])):
        A[nm] = dr(nm, shp)
    A["lb_r1"] = dr("lb_r1", [NB, D, seq], "Internal")
    A["lb_racc"] = dr("lb_racc", [NB, D, seq], "Internal")

    caps = declare_consts(nc)
    caps.update(declare_moe_consts(nc))
    caps.update(declare_l1aa_consts(nc))
    caps.update(declare_ffn_consts(nc))
    caps.update(declare_fft_consts(nc))

    def phase(fn):
        psum = nc.psum_base, nc.psum_top
        sbuf = nc.sbuf_base, nc.sbuf_top
        snap = nc.snapshot_sems()
        with contextlib.ExitStack() as es:
            ctx = Ctx(nc, es)
            consts = load_consts(ctx, caps)
            fn(ctx, consts)
            phase_epilogue(ctx)
        nc.psum_base, nc.psum_top = psum
        nc.sbuf_base, nc.sbuf_top = sbuf
        phase_teardown(nc)

    ffn_aps = {"W_exp": A["fn_W_exp"], "b_exp": A["fn_b_exp"],
               "up_w": A["fn_up_w"], "up_bias": A["fn_up_bias"],
               "down_w": A["fn_down_w"], "W_con": A["fn_W_con"],
               "b_con": A["fn_b_con"], "down_bias": A["fn_down_bias"],
               "wmid": A["fn_wmid"], "bmid": A["fn_bmid"],
               "w1eff": A["fn_w1eff"], "b1eff": A["fn_b1eff"],
               "w2eff": A["fn_w2eff"], "rmid": A["g_rmid"],
               "rmid2": A["g_rmid"], "u0": caps["u0"], "v0": caps["v0"]}
    plan = []
    plan.append(lambda c, k: emit_l0ffn_fold_phase(c, k, ffn_aps))
    aa_aps = {"rin": A["rin"], "rout": A["rA"], "W1": A["aa_W1"],
              "b1": A["aa_b1"], "W2": A["aa_W2"], "b2": A["aa_b2"],
              "W3": A["aa_W3"], "b3": A["aa_b3"]}
    plan.append(lambda c, k: phase_l0aa(c, k, aa_aps, n_chunks))
    ap1 = dict(ffn_aps)
    ap1["rin"], ap1["rout"] = A["rA"], A["rB"]
    plan.append(lambda c, k: emit_ffn_apply(c, k, ap1, n_chunks, "rin", "rout", "A"))
    ab_aps = {"rin": A["rB"], "rout": A["rA"], "proj_w": A["ab_proj_w"],
              "proj_b": A["ab_proj_b"], "filt": A["ab_filt"],
              "p_h": A["ab_p_h"], "v1_h": A["ab_v1_h"], "v2_h": A["ab_v2_h"],
              "A_h": A["ab_A_h"], "G_h": A["ab_G_h"], "F_h": A["ab_F_h"],
              "filtb": A["ab_filtb"], "cs1": caps["cs1"], "t2m": caps["t2m"],
              "iam": caps["iam"], "pbm": caps["pbm"]}
    plan.append(lambda c, k: emit_l0ab(c, k, ab_aps, 8))
    ap2 = dict(ffn_aps)
    ap2["rin"], ap2["rout"] = A["rA"], A["rB"]
    plan.append(lambda c, k: emit_ffn_apply(c, k, ap2, n_chunks, "rin", "rout", "B"))
    ra_aps = {"rin": A["rB"], "rout": A["rA"], "w": A["ra_w"], "u": A["ra_u"],
              "eh": caps["eh"], "ehb": caps["ehb"]}
    for nm in ("Wg", "Wr", "Wk", "Wv", "Wo", "bg", "br", "bk", "bv", "bo",
               "mg", "mr", "mk", "mv"):
        ra_aps[nm] = A["ra_" + nm]
    plan.append(lambda c, k: phase_l1aa(c, k, ra_aps, n_chunks))
    mo_aps = {"Wg": A["mo_Wg"], "ln_scale": A["mo_ln_scale"],
              "ln_bias": A["mo_ln_bias"], "W1": A["mo_W1"], "b1": A["mo_b1"],
              "W2": A["mo_W2"], "b2": A["mo_b2"], "moepen": caps["moepen"]}
    mo1 = dict(mo_aps)
    mo1["rin"], mo1["rout"] = A["rA"], A["rB"]
    plan.append(lambda c, k: phase_l1ffn(c, k, mo1, n_chunks, tag="moe1"))
    lb_aps = {"rin": A["rB"], "rout": A["rA"], "Wg": A["lb_Wg"],
              "bg": A["lb_bg"], "Wv": A["lb_Wv"], "bv": A["lb_bv"],
              "Wd": A["lb_Wd"], "bd": A["lb_bd"], "w_rms_mix": A["lb_wmix"],
              "w_rms_ffn": A["lb_wffn"], "W_gate": A["lb_W_gate"],
              "W_up": A["lb_W_up"], "W_out": A["lb_W_out"],
              "r1": A["lb_r1"], "racc": A["lb_racc"], "rmid": A["g_rmid"]}
    plan.append(lambda c, k: emit_l1ab(c, k, lb_aps, n_chunks, c.es))
    mo2 = dict(mo_aps)
    mo2["rin"], mo2["rout"] = A["rA"], A["rout"]
    plan.append(lambda c, k: phase_l1ffn(c, k, mo2, n_chunks, tag="moe2"))
    sel = os.environ.get("FUSED_PLAN", "")
    if sel:
        picked = [plan[int(i) - 1] for i in sel.split(",")]
    else:
        picked = plan[:nph]
    for fn in picked:
        phase(fn)
    return nc


def kernel(**ins):
    x = np.asarray(ins["L0_attn_a_x"])
    _ = ins.get("L0_attn_a_A")
    ncores = 8
    shards = [np.ascontiguousarray(x[2 * c:2 * c + 2].transpose(0, 2, 1))
              for c in range(ncores)]
    feed = {}
    feed.update(make_consts())
    feed.update(moe_consts())
    feed.update(l1aa_consts())
    feed.update(ffn_consts())
    feed.update(fft_consts())
    for nm in ("W1", "b1", "W2", "b2", "W3", "b3"):
        feed["aa_" + nm] = ins["L0_attn_a_" + nm]
    feed["fn_W_exp"] = ins["L0_ffn_W_exp"]
    feed["fn_b_exp"] = ins["L0_ffn_b_exp"]
    feed["fn_up_w"] = ins["L0_ffn_up_weights"]
    feed["fn_up_bias"] = ins["L0_ffn_up_bias"]
    feed["fn_down_w"] = ins["L0_ffn_down_weights"]
    feed["fn_W_con"] = ins["L0_ffn_W_con"]
    feed["fn_b_con"] = ins["L0_ffn_b_con"]
    feed["fn_down_bias"] = ins["L0_ffn_down_bias"]
    feed["ab_proj_w"] = ins["L0_attn_b_proj_w"]
    feed["ab_proj_b"] = ins["L0_attn_b_proj_b"]
    feed["ab_filt"] = ins["L0_attn_b_filter_time"]
    for nm in ("g", "r", "k", "v", "o"):
        feed["ra_W" + nm] = ins[f"L1_attn_a_W{nm}_w"]
        feed["ra_b" + nm] = ins[f"L1_attn_a_W{nm}_b"]
    for nm in ("g", "r", "k", "v"):
        feed["ra_m" + nm] = ins[f"L1_attn_a_m{nm}"]
    feed["ra_w"] = ins["L1_attn_a_w"]
    feed["ra_u"] = ins["L1_attn_a_u"]
    for nm in ("Wg", "ln_scale", "ln_bias", "W1", "b1", "W2", "b2"):
        feed["mo_" + nm] = ins["L1_ffn_" + nm]
    for nm in ("Wg", "bg", "Wv", "bv", "Wd", "bd", "W_gate", "W_up", "W_out"):
        feed["lb_" + nm] = ins["L1_attn_b_" + nm]
    feed["lb_wmix"] = ins["L1_attn_b_w_rms_mix"]
    feed["lb_wffn"] = ins["L1_attn_b_w_rms_ffn"]

    nc = _get("fused", build_fused)
    maps = [dict(feed, rin=shards[c]) for c in range(ncores)]
    res = run_bass_kernel_spmd(nc, maps, core_ids=list(range(ncores)))
    out = np.empty((16, 4096, 1024), np.float32)
    for c in range(ncores):
        out[2 * c:2 * c + 2] = np.asarray(res.results[c]["rout"]).transpose(0, 2, 1)
    return out
